# version 02
# baseline (speedup 1.0000x reference)
import numpy as np
import ml_dtypes
from contextlib import ExitStack

import concourse.bass as bass
import concourse.mybir as mybir
from concourse.bass_utils import run_bass_kernel_spmd

F32 = mybir.dt.float32
BF16 = mybir.dt.bfloat16
AF = mybir.ActivationFunctionType
ALU = mybir.AluOpType
AX = mybir.AxisListType

T = 2048
D = 1024
NT = 16
ND = 8
FF = 3072
NCF = 24
MAGIC = float(1.5 * 2.0**23)
QMAX = 2.0**15 - 1.0
EPS = 1e-5
ENGS = ("sync", "gpsimd", "tensor", "vector", "scalar")
DEBUG = False


class Ch:
    __slots__ = ("name", "n", "sem")

    def __init__(self, name):
        self.name = name
        self.n = 0
        self.sem = None


class Prog:

    def __init__(self):
        self.ops = {e: [] for e in ENGS}
        self.chans = []
        self.guar = {e: {} for e in ENGS}

    def chan(self, name):
        c = Ch(name)
        self.chans.append(c)
        return c

    def op(self, eng, fn, waits=(), inc=None, inck=1):
        wl = []
        g = self.guar[eng]
        for ch, val in waits:
            if val is None:
                val = ch.n
            if val > 0 and g.get(ch.name, 0) < val:
                wl.append((ch, val))
                g[ch.name] = val
        incs = []
        if inc is not None:
            inc.n += inck
            incs.append((inc, inck))
        self.ops[eng].append((wl, fn, incs))

    def emit(self, block):
        for ename in ENGS:
            ops = self.ops[ename]

            def body(e, ops=ops):
                for wl, fn, incs in ops:
                    for ch, val in wl:
                        e.wait_ge(ch.sem, val)
                    ins = fn(e)
                    for ch, k in incs:
                        ins.then_inc(ch.sem, k)
            getattr(block, ename)(body)


def build_nc():
    nc = bass.Bass("TRN2", target_bir_lowering=False, debug=False)

    x_h = nc.dram_tensor("x", [T, D], F32, kind="ExternalInput")
    wq_h = nc.dram_tensor("wq_t", [ND, D, 128], F32, kind="ExternalInput")
    wk_h = nc.dram_tensor("wk_t", [ND, D, 128], F32, kind="ExternalInput")
    wv_h = nc.dram_tensor("wv_t", [ND, D, 128], F32, kind="ExternalInput")
    wo_h = nc.dram_tensor("wo_t", [ND, D, 128], F32, kind="ExternalInput")
    lsrq_h = nc.dram_tensor("lsr_q", [16, 64, 32], F32, kind="ExternalInput")
    lsrk_h = nc.dram_tensor("lsr_k", [16, 64, 32], F32, kind="ExternalInput")
    b_h = nc.dram_tensor("b_qkvo", [4, D], F32, kind="ExternalInput")
    wup_h = nc.dram_tensor("w_up", [FF, D], F32, kind="ExternalInput")
    wgt_h = nc.dram_tensor("w_gate", [FF, D], F32, kind="ExternalInput")
    wdn_h = nc.dram_tensor("w_down", [D, FF], F32, kind="ExternalInput")
    idf_h = nc.dram_tensor("idn_f", [128, 128], F32, kind="ExternalInput")
    idb_h = nc.dram_tensor("idn_b", [128, 128], BF16, kind="ExternalInput")
    msk_h = nc.dram_tensor("maskbig", [128, 1024], BF16, kind="ExternalInput")
    onef_h = nc.dram_tensor("ones_f", [1, 128], F32, kind="ExternalInput")
    cst_h = nc.dram_tensor("consts", [128, 4], F32, kind="ExternalInput")
    out_h = nc.dram_tensor("out", [T, D], F32, kind="ExternalOutput")

    kind = dict(kind="ExternalOutput") if DEBUG else {}
    scr = {
        "up": nc.dram_tensor("scr_up", [NCF, 128, T], BF16, **kind),
        "gt": nc.dram_tensor("scr_gt", [NCF, 128, T], BF16, **kind),
        "g1": nc.dram_tensor("scr_g1", [NCF, 128, T], BF16, **kind),
        "o": nc.dram_tensor("scr_o", [NCF, 128, T], BF16, **kind),
    }
    if DEBUG:
        d_xnT = nc.dram_tensor("d_xnT", [128, NT * 1024], BF16, kind="ExternalOutput")
        d_xqT = nc.dram_tensor("d_xqT", [128, NT * 1024], BF16, kind="ExternalOutput")
        d_yall = nc.dram_tensor("d_yall", [128, NT * 1024], BF16, kind="ExternalOutput")
        d_stats = nc.dram_tensor("d_stats", [128, 96], F32, kind="ExternalOutput")
        d_scl = nc.dram_tensor("d_scl", [128, 16], F32, kind="ExternalOutput")
        d_var = nc.dram_tensor("d_var", [128, 48], F32, kind="ExternalOutput")
    dnT_h = nc.dram_tensor("dnT_cache", [ND, 128, FF], BF16)
    cc_in = [nc.dram_tensor(f"cc_in{i}", [1, 16], F32) for i in range(4)]
    cc_out = [nc.dram_tensor(f"cc_out{i}", [1, 16], F32) for i in range(4)]

    P = Prog()
    chL = P.chan("chL")
    chACC = P.chan("chACC")
    chOUT = P.chan("chOUT")
    chSW = P.chan("chSW")
    chSR = P.chan("chSR")
    chGW = P.chan("chGW")
    chCC = P.chan("chCC")
    chPE = P.chan("chPE")
    chACT = P.chan("chACT")
    chDVE = P.chan("chDVE")
    chGP = P.chan("chGP")
    chDBG = P.chan("chDBG")

    with ExitStack() as ctx:
        sbt = lambda name, shape, dt: ctx.enter_context(nc.sbuf_tensor(name, shape, dt))
        pst = lambda name, shape, dt: ctx.enter_context(nc.psum_tensor(name, shape, dt))

        xstage = sbt("xstage", [128, 4 * 1024], F32)
        wf32 = sbt("wf32", [128, 8 * 1024], F32)
        arA = sbt("arA", [128, NT * 1024], BF16)
        arB = sbt("arB", [128, NT * 1024], BF16)
        arC = sbt("arC", [128, NT * 1024], BF16)
        scb = sbt("scb", [128, 6 * 1024], BF16)
        wbf = sbt("wbf", [128, 6 * 1024], BF16)
        vhat = sbt("vhat", [128, NT * 130], BF16)
        qkscr = sbt("qkscr", [128, T], BF16)
        qlrb = sbt("qlrb", [64, T], BF16)
        klrb = sbt("klrb", [64, T], BF16)
        ptile = sbt("ptile", [128, 2 * 512], BF16)
        ybf = sbt("ybf", [64, 512], BF16)
        lsq = sbt("lsq", [128, 8 * 64], BF16)
        lsk = sbt("lsk", [128, 8 * 64], BF16)
        idf = sbt("idf", [128, 128], F32)
        idb = sbt("idb", [128, 128], BF16)
        mask = sbt("mask", [128, 1024], BF16)
        onef = sbt("onef", [1, 128], F32)
        cst = sbt("cst", [128, 4], F32)
        sS1 = sbt("sS1", [128, 16], F32)
        sS2 = sbt("sS2", [128, 16], F32)
        sM = sbt("sM", [128, 16], F32)
        sR = sbt("sR", [128, 16], F32)
        sNB = sbt("sNB", [128, 16], F32)
        sMX = sbt("sMX", [128, 16], F32)
        sA = sbt("sA", [128, 16], F32)
        sBt = sbt("sBt", [128, 16], F32)
        tmp16 = sbt("tmp16", [128, 16], F32)
        tmp16b = sbt("tmp16b", [128, 16], F32)
        bias4 = sbt("bias4", [128, 4 * 8], F32)
        scl = sbt("scl", [128, 16], F32)
        pmx = sbt("pmx", [128, 4 * 24], F32)
        dqs = sbt("dqs", [128, 4 * 32], F32)
        ccs = sbt("ccs", [1, 16], F32)
        cco = sbt("cco", [1, 16], F32)
        rcp = sbt("rcp", [1, 512], F32)

        psA = pst("psA", [128, 512], F32)
        psB = pst("psB", [128, 512], F32)
        psC = pst("psC", [128, 512], F32)
        psD = pst("psD", [128, 512], F32)
        psE = pst("psE", [128, 1024], BF16)
        psF = pst("psF", [128, 1024], BF16)
        psG = pst("psG", [128, 512], F32)
        psH = pst("psH", [128, 512], F32)

        C_EPS = cst[:, 0:1]
        C_RSQ32 = cst[:, 1:2]
        C_ZERO = cst[:, 2:3]
        SCL = lambda i: scl[:, i:i + 1]
        I_SX, I_RX, I_SUP, I_RUP, I_SGT, I_RGT, I_SG1, I_RG1, I_SO, I_RO = range(10)

        arAf = arA[:].bitcast(F32)
        arBf = arB[:].bitcast(F32)
        wfb = wf32[:].bitcast(BF16)
        xsb = xstage[:].bitcast(BF16)

        xslot = lambda i: xstage[:, (i % 4) * 1024:((i % 4) + 1) * 1024]
        wfslot = lambda i: wf32[:, i * 1024:(i + 1) * 1024]
        wbslot = lambda i: wbf[:, i * 1024:(i + 1) * 1024]
        scbh = lambda i: scb[:, i * 1024:(i + 1) * 1024]
        scbs = lambda i: scb[:, i * 2048:(i + 1) * 2048]
        pslot = lambda i: ptile[:, (i % 2) * 512:((i % 2) + 1) * 512]

        for dst, src in ((idf, idf_h), (idb, idb_h), (mask, msk_h), (onef, onef_h), (cst, cst_h)):
            P.op("sync", lambda e, dst=dst, src=src: e.dma_start(out=dst[:], in_=src[:, :]),
                 inc=chL, inck=16)
        P.op("sync", lambda e: e.dma_start(
            out=xstage[0:1, :].rearrange("o (i d) -> o i d", i=4),
            in_=b_h[:, :].unsqueeze(0)), inc=chL, inck=16)
        ld_consts = chL.n

        P.op("gpsimd", lambda e: e.memset(lsq[:], 0.0), inc=chGP)
        P.op("gpsimd", lambda e: e.memset(lsk[:], 0.0), inc=chGP)
        P.op("gpsimd", lambda e: e.memset(
            vhat[:].rearrange("p (b w) -> p b w", w=130)[:, :, 64:65], 1.0), inc=chGP)
        P.op("gpsimd", lambda e: e.memset(
            vhat[:].rearrange("p (b w) -> p b w", w=130)[:, :, 129:130], 1.0), inc=chGP)
        lsr_memset = chGP.n
        for a in range(8):
            for dst, src in ((lsq, lsrq_h), (lsk, lsrk_h)):
                P.op("gpsimd", lambda e, dst=dst, src=src, a=a: e.dma_start(
                    out=dst[0:64, a * 64:a * 64 + 32], in_=src[2 * a, :, :]),
                    waits=[(chGP, lsr_memset)], inc=chGW, inck=16)
                P.op("gpsimd", lambda e, dst=dst, src=src, a=a: e.dma_start(
                    out=dst[64:128, a * 64 + 32:a * 64 + 64], in_=src[2 * a + 1, :, :]),
                    inc=chGW, inck=16)
        lsr_ready = chGW.n

        for i in range(4):
            for c in range(ND):
                P.op("tensor", lambda e, i=i, c=c: e.matmul(
                    psD[:, (i * 8 + c):(i * 8 + c) + 1],
                    xstage[0:1, i * 1024 + c * 128:i * 1024 + (c + 1) * 128],
                    onef[0:1, 0:1], start=True, stop=True),
                    waits=[(chL, ld_consts)], inc=chPE)
        bias_mm = chPE.n
        P.op("scalar", lambda e: e.activation(bias4[:], psD[:, 0:32], AF.Identity,
                                              bias=C_ZERO, scale=1.0),
             waits=[(chPE, bias_mm), (chL, ld_consts)], inc=chACT)
        bias_ready = chACT.n

        stats_done = {}
        for t in range(NT):
            P.op("sync", lambda e, t=t: e.dma_start(out=xslot(t), in_=x_h[t * 128:(t + 1) * 128, :]),
                 waits=([(chDVE, stats_done[t - 4])] if t >= 4 else [(chACT, bias_ready)]),
                 inc=chL, inck=16)
            ldn = chL.n
            P.op("vector", lambda e, t=t: e.tensor_reduce(
                sS1[:, t:t + 1], xslot(t), AX.XY, ALU.add),
                waits=[(chL, ldn), (chDVE, chDVE.n)], inc=chDVE)
            n_s1 = chDVE.n
            P.op("scalar", lambda e, t=t: e.activation(
                xslot(t), xslot(t), AF.Square, bias=C_ZERO, scale=1.0),
                waits=[(chDVE, n_s1)], inc=chACT)
            n_sq = chACT.n
            P.op("vector", lambda e, t=t: e.tensor_reduce(
                sS2[:, t:t + 1], xslot(t), AX.XY, ALU.add),
                waits=[(chACT, n_sq), (chDVE, chDVE.n)], inc=chDVE)
            stats_done[t] = chDVE.n
        stats_all = chDVE.n

        P.op("vector", lambda e: e.tensor_scalar(sM[:], sS1[:], 1.0 / D, None, ALU.mult),
             waits=[(chDVE, stats_all), (chDVE, chDVE.n)], inc=chDVE)
        P.op("vector", lambda e: e.tensor_scalar(tmp16[:], sS2[:], 1.0 / D, None, ALU.mult), inc=chDVE)
        P.op("vector", lambda e: e.tensor_tensor(tmp16b[:], sM[:], sM[:], ALU.mult),
             waits=[(chDVE, chDVE.n)], inc=chDVE)
        P.op("vector", lambda e: e.tensor_tensor(tmp16[:], tmp16[:], tmp16b[:], ALU.subtract),
             waits=[(chDVE, chDVE.n)], inc=chDVE)
        var_done = chDVE.n
        if DEBUG:
            P.op("sync", lambda e: e.dma_start(out=d_var[:, 0:16], in_=tmp16[:]),
                 waits=[(chDVE, var_done)], inc=chDBG, inck=16)
            P.op("sync", lambda e: e.dma_start(out=d_var[:, 16:32], in_=tmp16b[:]),
                 inc=chDBG, inck=16)
        P.op("scalar", lambda e: e.activation(sR[:], tmp16[:], AF.Sqrt, bias=C_EPS, scale=1.0),
             waits=[(chDVE, var_done), (chDBG, chDBG.n)], inc=chACT)
        sd_done = chACT.n
        if DEBUG:
            P.op("sync", lambda e: e.dma_start(out=d_var[:, 32:48], in_=sR[:]),
                 waits=[(chACT, sd_done)], inc=chDBG, inck=16)
        P.op("vector", lambda e: e.reciprocal(sR[:], sR[:]),
             waits=[(chACT, sd_done), (chDBG, chDBG.n)], inc=chDVE)
        P.op("vector", lambda e: e.tensor_tensor(sNB[:], sM[:], sR[:], ALU.mult),
             waits=[(chDVE, chDVE.n)], inc=chDVE)
        P.op("vector", lambda e: e.tensor_scalar(sNB[:], sNB[:], -1.0, None, ALU.mult),
             waits=[(chDVE, chDVE.n)], inc=chDVE)
        ln_ready = chDVE.n

        trS = {"i": 0, "ev": {}}

        def transpose_tile_to_arena(t, src_ap, dst_ar, dep_waits):
            for g in range(2):
                par = trS["i"] % 2
                pdst = psE if par == 0 else psF
                w = list(dep_waits) + [(chDVE, trS["ev"].get(par, 0)), (chL, ld_consts)]
                for i in range(4):
                    c = g * 4 + i
                    P.op("tensor", lambda e, i=i, c=c, pdst=pdst, src_ap=src_ap: e.transpose(
                        pdst[:, i * 128:(i + 1) * 128], src_ap[:, c * 128:(c + 1) * 128], idb[:]),
                        waits=(w if i == 0 else []), inc=chPE)
                trn = chPE.n
                dv = dst_ar[:].rearrange("p (c w) -> p c w", w=T)[
                    :, g * 4:(g + 1) * 4, t * 128:(t + 1) * 128]
                P.op("vector", lambda e, pdst=pdst, dv=dv: e.tensor_copy(
                    dv, pdst[:, 0:512].rearrange("p (a w) -> p a w", w=128)),
                    waits=[(chPE, trn)], inc=chDVE)
                trS["ev"][par] = chDVE.n
                trS["i"] += 1

        xn_done = {}
        for t in range(NT):
            P.op("sync", lambda e, t=t: e.dma_start(out=xslot(t), in_=x_h[t * 128:(t + 1) * 128, :]),
                 waits=([(chDVE, xn_done[t - 4])] if t >= 4 else [(chDVE, stats_all)]),
                 inc=chL, inck=16)
            ldn = chL.n
            dst = scbh(t % 2)
            P.op("scalar", lambda e, t=t, dst=dst: e.activation(
                dst, xslot(t), AF.Identity, bias=sNB[:, t:t + 1], scale=sR[:, t:t + 1]),
                waits=[(chL, ldn), (chDVE, ln_ready), (chDVE, xn_done.get(t - 2, 0))],
                inc=chACT)
            act_n = chACT.n
            transpose_tile_to_arena(t, dst, arA, [(chACT, act_n)])
            P.op("vector", lambda e, t=t, dst=dst: e.tensor_reduce(
                sMX[:, t:t + 1], dst, AX.XY, ALU.max, apply_absolute_value=True), inc=chDVE)
            xn_done[t] = chDVE.n
        xnT_ready = chDVE.n
        if DEBUG:
            P.op("sync", lambda e: e.dma_start(out=d_xnT[:, :], in_=arA[:]),
                 waits=[(chDVE, xnT_ready)], inc=chDBG, inck=16)
            for i, st_ in enumerate((sM, sR, sNB, sMX, sS1, sS2)):
                P.op("sync", lambda e, i=i, st_=st_: e.dma_start(
                    out=d_stats[:, i * 16:(i + 1) * 16], in_=st_[:]), inc=chDBG, inck=16)

        def part_max_to_ccs(src_ap, col, extra_waits=()):
            P.op("vector", lambda e, src_ap=src_ap: e.tensor_reduce(
                tmp16[:, 0:1], src_ap, AX.X, ALU.max),
                waits=list(extra_waits) + [(chGW, chGW.n), (chDVE, chDVE.n)], inc=chDVE)
            n1 = chDVE.n
            P.op("tensor", lambda e: e.transpose(
                psG[0:1, 0:128], tmp16[:, 0:1], idf[:]),
                waits=[(chDVE, n1)], inc=chPE)
            n2 = chPE.n
            P.op("vector", lambda e, col=col: e.tensor_reduce(
                ccs[0:1, col:col + 1], psG[0:1, 0:128], AX.X, ALU.max),
                waits=[(chPE, n2)], inc=chDVE)
            return chDVE.n

        def collective(idx, dep_ch, dep_val):
            P.op("gpsimd", lambda e, idx=idx: e.dma_start(out=cc_in[idx][:, :], in_=ccs[:, :]),
                 waits=[(dep_ch, dep_val)], inc=chGW, inck=16)
            n = chGW.n
            P.op("gpsimd", lambda e, idx=idx: e.collective_compute(
                "AllReduce", ALU.max, replica_groups=[list(range(8))],
                ins=[cc_in[idx].ap()], outs=[cc_out[idx].ap()]),
                waits=[(chGW, n)], inc=chCC)
            nn = chCC.n
            P.op("gpsimd", lambda e, idx=idx: e.dma_start(out=cco[:, :], in_=cc_out[idx][:, :]),
                 waits=[(chCC, nn)], inc=chGW, inck=16)
            return chGW.n

        def scale_math(col, s_idx, r_idx, dep):
            P.op("vector", lambda e, col=col: e.tensor_scalar(
                ccs[0:1, 8:9], cco[0:1, col:col + 1], 1.0 / QMAX, 1e-12, ALU.mult, ALU.max),
                waits=[(chGW, dep)], inc=chDVE)
            P.op("vector", lambda e: e.reciprocal(ccs[0:1, 9:10], ccs[0:1, 8:9]),
                 waits=[(chDVE, chDVE.n)], inc=chDVE)
            n1 = chDVE.n
            P.op("tensor", lambda e: e.matmul(psG[:, 0:1], onef[0:1, :], ccs[0:1, 8:9],
                                              start=True, stop=True),
                 waits=[(chDVE, n1)], inc=chPE)
            P.op("tensor", lambda e: e.matmul(psG[:, 1:2], onef[0:1, :], ccs[0:1, 9:10],
                                              start=True, stop=True), inc=chPE)
            n2 = chPE.n
            P.op("vector", lambda e, s_idx=s_idx: e.tensor_copy(SCL(s_idx), psG[:, 0:1]),
                 waits=[(chPE, n2)], inc=chDVE)
            P.op("vector", lambda e, r_idx=r_idx: e.tensor_copy(SCL(r_idx), psG[:, 1:2]), inc=chDVE)
            return chDVE.n

        mx_done = part_max_to_ccs(sMX[:], 0, [(chDVE, xnT_ready)])
        c1 = collective(0, chDVE, mx_done)
        sx_ready = scale_math(0, I_SX, I_RX, c1)

        P.op("vector", lambda e: e.tensor_scalar(sA[:], sR[:], SCL(I_RX), None, ALU.mult),
             waits=[(chDVE, sx_ready)], inc=chDVE)
        P.op("vector", lambda e: e.tensor_scalar(sBt[:], sNB[:], SCL(I_RX), -1.0, ALU.mult, ALU.mult),
             inc=chDVE)
        xq_scales = chDVE.n

        xq_done = {}
        for t in range(NT):
            P.op("sync", lambda e, t=t: e.dma_start(out=xslot(t), in_=x_h[t * 128:(t + 1) * 128, :]),
                 waits=([(chDVE, xq_done[t - 4])] if t >= 4 else [(chDVE, xnT_ready)]),
                 inc=chL, inck=16)
            ldn = chL.n
            dst = scbh(t % 2)
            ftmp = wfslot(t % 2)
            P.op("vector", lambda e, t=t, ftmp=ftmp: e.tensor_scalar(
                ftmp, xslot(t), sA[:, t:t + 1], sBt[:, t:t + 1], ALU.mult, ALU.subtract),
                waits=[(chL, ldn), (chDVE, xq_scales), (chDVE, xq_done.get(t - 2, 0))],
                inc=chDVE)
            P.op("vector", lambda e, ftmp=ftmp: e.tensor_scalar(
                ftmp, ftmp, MAGIC, MAGIC, ALU.add, ALU.subtract),
                waits=[(chDVE, chDVE.n)], inc=chDVE)
            P.op("vector", lambda e, dst=dst, ftmp=ftmp: e.tensor_scalar(
                dst, ftmp, SCL(I_SX), None, ALU.mult),
                waits=[(chDVE, chDVE.n)], inc=chDVE)
            dve_n = chDVE.n
            transpose_tile_to_arena(t, dst, arB, [(chDVE, dve_n)])
            xq_done[t] = chDVE.n
        xqT_ready = chDVE.n
        if DEBUG:
            P.op("sync", lambda e: e.dma_start(out=d_xqT[:, :], in_=arB[:]),
                 waits=[(chDVE, xqT_ready)], inc=chDBG, inck=16)

        WCFG = {"up": dict(wh=wup_h, wsl=(0, 1), wtsl=(2, 3), nat=0, dq=0),
                "gt": dict(wh=wgt_h, wsl=(2, 3), wtsl=(4, 5), nat=1, dq=1)}
        dq_free = {}
        nat_free = {}
        wt_free = {}
        ev_free = {}
        dqs_free = {}
        bank_ev = {0: 0, 1: 0}

        for c in range(NCF):
            for wi, wn in enumerate(("up", "gt")):
                cfg = WCFG[wn]
                par = c % 2
                fw = wfslot(cfg["wsl"][par])
                nat = wbslot(cfg["nat"])
                wt = wbslot(cfg["wtsl"][par])
                sq = dqs[:, (2 * cfg["dq"]) * 32:(2 * cfg["dq"] + 1) * 32]
                rq = dqs[:, (2 * cfg["dq"] + 1) * 32:(2 * cfg["dq"] + 2) * 32]

                P.op("sync", lambda e, c=c, wh=cfg["wh"], fw=fw: e.dma_start(
                    out=fw, in_=wh[c * 128:(c + 1) * 128, :]),
                    waits=[(chGP, dq_free.get((wn, c - 2), 0)),
                           (chDVE, xqT_ready)],
                    inc=chL, inck=16)
                ldn = chL.n

                P.op("vector", lambda e, fw=fw, sq=sq: e.tensor_reduce(
                    sq, fw.rearrange("p (b w) -> p b w", w=32), AX.X, ALU.max,
                    apply_absolute_value=True),
                    waits=[(chL, ldn), (chGP, dqs_free.get(wn, 0))], inc=chDVE)
                P.op("vector", lambda e, sq=sq: e.tensor_scalar(
                    sq, sq, 1.0 / 7.0, 1e-12, ALU.mult, ALU.max),
                    waits=[(chDVE, chDVE.n)], inc=chDVE)
                P.op("vector", lambda e, sq=sq, rq=rq: e.reciprocal(rq, sq),
                    waits=[(chDVE, chDVE.n)], inc=chDVE)
                n_sr = chDVE.n
                P.op("gpsimd", lambda e, fw=fw, rq=rq: e.tensor_tensor(
                    fw.rearrange("p (b w) -> p b w", w=32),
                    fw.rearrange("p (b w) -> p b w", w=32),
                    rq.unsqueeze(2).broadcast_to([128, 32, 32]), ALU.mult),
                    waits=[(chDVE, n_sr)], inc=chGP)
                n_gp1 = chGP.n
                P.op("vector", lambda e, fw=fw: e.tensor_scalar(
                    fw, fw, MAGIC, MAGIC, ALU.add, ALU.subtract),
                    waits=[(chGP, n_gp1)], inc=chDVE)
                n_rnd = chDVE.n
                P.op("gpsimd", lambda e, fw=fw, nat=nat, sq=sq: e.tensor_tensor(
                    nat.rearrange("p (b w) -> p b w", w=32),
                    fw.rearrange("p (b w) -> p b w", w=32),
                    sq.unsqueeze(2).broadcast_to([128, 32, 32]), ALU.mult),
                    waits=[(chDVE, n_rnd), (chPE, nat_free.get(wn, 0))], inc=chGP)
                dq_free[(wn, c)] = chGP.n
                dqs_free[wn] = chGP.n
                n_nat = chGP.n

                for g in range(2):
                    tpar = trS["i"] % 2
                    pdst = psE if tpar == 0 else psF
                    w = [(chGP, n_nat), (chDVE, trS["ev"].get(tpar, 0)),
                         (chPE, wt_free.get((wn, par), 0))]
                    for i in range(4):
                        dc = g * 4 + i
                        P.op("tensor", lambda e, i=i, dc=dc, pdst=pdst, nat=nat: e.transpose(
                            pdst[:, i * 128:(i + 1) * 128], nat[:, dc * 128:(dc + 1) * 128],
                            idb[:]), waits=(w if i == 0 else []), inc=chPE)
                    trn = chPE.n
                    if g == 1:
                        nat_free[wn] = trn
                    P.op("vector", lambda e, pdst=pdst, wt=wt, g=g: e.tensor_copy(
                        wt[:, g * 512:(g + 1) * 512], pdst[:, 0:512]),
                        waits=[(chPE, trn)], inc=chDVE)
                    trS["ev"][tpar] = chDVE.n
                    trS["i"] += 1
                n_wt = chDVE.n

                ev = scbs(cfg["nat"])
                for p in range(4):
                    bkey = p % 2
                    bank = psA if bkey == 0 else psB
                    for dc in range(ND):
                        P.op("tensor", lambda e, p=p, dc=dc, bank=bank, wt=wt: e.matmul(
                            bank[:], wt[:, dc * 128:(dc + 1) * 128],
                            arB[:, dc * T + p * 512:dc * T + (p + 1) * 512],
                            start=(dc == 0), stop=(dc == ND - 1)),
                            waits=([(chDVE, n_wt), (chACT, bank_ev[bkey])]
                                   if dc == 0 else []), inc=chPE)
                    mmn = chPE.n
                    P.op("scalar", lambda e, p=p, bank=bank, ev=ev: e.activation(
                        ev[:, p * 512:(p + 1) * 512], bank[:], AF.Identity,
                        bias=C_ZERO, scale=1.0),
                        waits=[(chPE, mmn), (chSW, ev_free.get(wn, 0))], inc=chACT)
                    bank_ev[bkey] = chACT.n
                wt_free[(wn, par)] = chPE.n
                evn = chACT.n

                P.op("vector", lambda e, ev=ev, c=c, wi=wi: e.tensor_reduce(
                    pmx[:, wi * 24 + c:wi * 24 + c + 1], ev, AX.XY, ALU.max,
                    apply_absolute_value=True),
                    waits=[(chACT, evn)], inc=chDVE)
                P.op("scalar", lambda e, ev=ev, c=c, wn=wn: e.dma_start(
                    out=scr[wn][c, :, :], in_=ev),
                    waits=[(chACT, evn)], inc=chSW, inck=16)
                ev_free[wn] = chSW.n

        p1_gp_done = chGP.n
        p1_sw_done = chSW.n
        m_up = part_max_to_ccs(pmx[:, 0:24], 0)
        m_gt = part_max_to_ccs(pmx[:, 24:48], 1)
        c2 = collective(1, chDVE, m_gt)
        s_up = scale_math(0, I_SUP, I_RUP, c2)
        s_gt = scale_math(1, I_SGT, I_RGT, s_up)

        attw_free = {}
        qlr_free = {}
        vhat_free = {}
        lsr_ev = {}
        sc_used = {}
        yevac_done = [0]
        ynorm_done = [0]
        wload = {}

        def prefetch_attn(a):
            par = a % 2
            extra = []
            if a == 0:
                extra = [(chPE, nat_free["up"]), (chPE, wt_free[("up", 0)]),
                         (chPE, wt_free[("gt", 0)])]
            elif a == 1:
                extra = [(chPE, nat_free["gt"]), (chPE, wt_free[("up", 1)]),
                         (chPE, wt_free[("gt", 1)])]
            for wi, wh in enumerate((wq_h, wk_h, wv_h)):
                P.op("gpsimd", lambda e, wh=wh, wi=wi, par=par, a=a: e.dma_start(
                    out=wbslot(2 * wi + par).rearrange("p (c o) -> p c o", o=128),
                    in_=wh[a].rearrange("(c p) o -> p c o", p=128)),
                    waits=extra + [(chPE, attw_free.get((wi, par), 0)),
                                   (chGP, p1_gp_done)],
                    inc=chGW, inck=16)
            wload[a] = chGW.n

        prefetch_attn(0)
        prefetch_attn(1)

        p2_infree = {}
        p2_outfree = {0: p1_sw_done, 1: p1_sw_done}
        p2_sig_done = {}
        p3_infree = {}
        p3_outfree = {}
        s_g1 = [0]
        s_o = [0]

        def emit_p2_chunk(c):
            par = c % 2
            inb = wfb[:, par * 2048:(par + 1) * 2048]
            gate0 = wf32[:, 2 * 1024:4 * 1024]
            sig = wf32[:, 4 * 1024:6 * 1024]
            outb = scbs(par)
            P.op("scalar", lambda e, c=c, inb=inb: e.dma_start(out=inb, in_=scr["gt"][c, :, :]),
                 waits=[(chSW, p1_sw_done), (chDVE, p2_infree.get(par, 0)),
                        (chGP, p1_gp_done)],
                 inc=chSR, inck=16)
            ldn = chSR.n
            P.op("vector", lambda e, inb=inb, gate0=gate0: e.tensor_scalar(
                gate0, inb, SCL(I_RGT), MAGIC, ALU.mult, ALU.add),
                waits=[(chSR, ldn), (chDVE, s_gt), (chACT, p2_sig_done.get(c - 1, 0))],
                inc=chDVE)
            P.op("vector", lambda e, gate0=gate0: e.tensor_scalar(
                gate0, gate0, MAGIC, SCL(I_SGT), ALU.subtract, ALU.mult),
                waits=[(chDVE, chDVE.n)], inc=chDVE)
            p2_infree[par] = chDVE.n
            n_g0 = chDVE.n
            P.op("scalar", lambda e, gate0=gate0, sig=sig: e.activation(
                sig, gate0, AF.Sigmoid, bias=C_ZERO, scale=1.0),
                waits=[(chDVE, n_g0)], inc=chACT)
            p2_sig_done[c] = chACT.n
            P.op("vector", lambda e, sig=sig: e.tensor_scalar(
                sig, sig, 65536.0, MAGIC, ALU.mult, ALU.add),
                waits=[(chACT, p2_sig_done[c])], inc=chDVE)
            P.op("vector", lambda e, sig=sig: e.tensor_scalar(
                sig, sig, MAGIC, 65535.0, ALU.subtract, ALU.min),
                waits=[(chDVE, chDVE.n)], inc=chDVE)
            P.op("vector", lambda e, sig=sig: e.tensor_scalar(
                sig, sig, 1.0 / 65536.0, None, ALU.mult),
                waits=[(chDVE, chDVE.n)], inc=chDVE)
            P.op("vector", lambda e, gate0=gate0, sig=sig, outb=outb: e.tensor_tensor(
                outb, gate0, sig, ALU.mult),
                waits=[(chSW, p2_outfree.get(par, 0)), (chDVE, chDVE.n)], inc=chDVE)
            P.op("vector", lambda e, outb=outb, c=c: e.tensor_reduce(
                pmx[:, 48 + c:48 + c + 1], outb, AX.XY, ALU.max,
                apply_absolute_value=True),
                waits=[(chDVE, chDVE.n)], inc=chDVE)
            n_out = chDVE.n
            P.op("scalar", lambda e, outb=outb, c=c: e.dma_start(
                out=scr["g1"][c, :, :], in_=outb),
                waits=[(chDVE, n_out)], inc=chSW, inck=16)
            p2_outfree[par] = chSW.n

        def emit_p3_chunk(c, p2_dve_done):
            par = c % 2
            ing = wfb[:, par * 2048:(par + 1) * 2048]
            inu = wfb[:, 4096 + par * 2048:4096 + (par + 1) * 2048]
            g1 = wf32[:, 4 * 1024:6 * 1024]
            up = wf32[:, 6 * 1024:8 * 1024]
            outb = scbs(par)
            P.op("scalar", lambda e, c=c, ing=ing: e.dma_start(out=ing, in_=scr["g1"][c, :, :]),
                 waits=[(chSW, chSW.n), (chDVE, p3_infree.get(par, p2_dve_done))],
                 inc=chSR, inck=16)
            P.op("scalar", lambda e, c=c, inu=inu: e.dma_start(out=inu, in_=scr["up"][c, :, :]),
                 inc=chSR, inck=16)
            ldn = chSR.n
            P.op("vector", lambda e, ing=ing, g1=g1: e.tensor_scalar(
                g1, ing, SCL(I_RG1), MAGIC, ALU.mult, ALU.add),
                waits=[(chSR, ldn), (chDVE, s_g1[0])], inc=chDVE)
            P.op("vector", lambda e, g1=g1: e.tensor_scalar(
                g1, g1, MAGIC, SCL(I_SG1), ALU.subtract, ALU.mult),
                waits=[(chDVE, chDVE.n)], inc=chDVE)
            P.op("vector", lambda e, inu=inu, up=up: e.tensor_scalar(
                up, inu, SCL(I_RUP), MAGIC, ALU.mult, ALU.add), inc=chDVE)
            P.op("vector", lambda e, up=up: e.tensor_scalar(
                up, up, MAGIC, SCL(I_SUP), ALU.subtract, ALU.mult),
                waits=[(chDVE, chDVE.n)], inc=chDVE)
            p3_infree[par] = chDVE.n
            P.op("vector", lambda e, g1=g1, up=up, outb=outb: e.tensor_tensor(
                outb, g1, up, ALU.mult),
                waits=[(chSW, p3_outfree.get(par, 0)), (chDVE, chDVE.n)], inc=chDVE)
            P.op("vector", lambda e, outb=outb, c=c: e.tensor_reduce(
                pmx[:, 72 + c:72 + c + 1], outb, AX.XY, ALU.max,
                apply_absolute_value=True),
                waits=[(chDVE, chDVE.n)], inc=chDVE)
            n_out = chDVE.n
            P.op("scalar", lambda e, outb=outb, c=c: e.dma_start(
                out=scr["o"][c, :, :], in_=outb),
                waits=[(chDVE, n_out)], inc=chSW, inck=16)
            p3_outfree[par] = chSW.n

        for a in range(8):
            par = a % 2
            vh_ready = 0
            for wi, wn in enumerate(("q", "k", "v")):
                wt = wbslot(2 * wi + par)
                for p in range(4):
                    bkey = p % 2
                    bank = psA if bkey == 0 else psB
                    for dc in range(ND):
                        P.op("tensor", lambda e, p=p, dc=dc, bank=bank, wt=wt: e.matmul(
                            bank[:], wt[:, dc * 128:(dc + 1) * 128],
                            arA[:, dc * T + p * 512:dc * T + (p + 1) * 512],
                            start=(dc == 0), stop=(dc == ND - 1)),
                            waits=([(chGW, wload[a]), (chACT, bank_ev[bkey])]
                                   if dc == 0 else []), inc=chPE)
                    mmn = chPE.n
                    P.op("scalar", lambda e, p=p, bank=bank, wi=wi, a=a: e.activation(
                        qkscr[:, p * 512:(p + 1) * 512], bank[:], AF.Identity,
                        bias=bias4[:, wi * 8 + a:wi * 8 + a + 1], scale=1.0),
                        waits=[(chPE, mmn)], inc=chACT)
                    bank_ev[bkey] = chACT.n
                attw_free[(wi, par)] = chPE.n
                evn = chACT.n

                if wn in ("q", "k"):
                    lsr_w = lsq if wn == "q" else lsk
                    dstb = qlrb if wn == "q" else klrb
                    for p in range(4):
                        P.op("tensor", lambda e, p=p, a=a, lsr_w=lsr_w: e.matmul(
                            psD[0:64, :], lsr_w[:, a * 64:(a + 1) * 64],
                            qkscr[:, p * 512:(p + 1) * 512], start=True, stop=True),
                            waits=[(chACT, evn), (chGW, lsr_ready),
                                   (chACT, lsr_ev.get("any", 0)),
                                   (chPE, qlr_free.get(0, 0))], inc=chPE)
                        mmn = chPE.n
                        P.op("scalar", lambda e, p=p, dstb=dstb: e.activation(
                            dstb[:, p * 512:(p + 1) * 512],
                            psD[0:64, :], AF.Identity, bias=C_ZERO[0:64, :], scale=1.0),
                            waits=[(chPE, mmn)], inc=chACT)
                        lsr_ev["any"] = chACT.n
                else:
                    for g in range(4):
                        tpar = trS["i"] % 2
                        pdst = psE if tpar == 0 else psF
                        w = [(chACT, evn), (chDVE, trS["ev"].get(tpar, 0)),
                             (chPE, vhat_free.get(0, 0))]
                        for i in range(4):
                            tt = g * 4 + i
                            P.op("tensor", lambda e, i=i, tt=tt, pdst=pdst: e.transpose(
                                pdst[:, i * 128:(i + 1) * 128],
                                qkscr[:, tt * 128:(tt + 1) * 128], idb[:]),
                                waits=(w if i == 0 else []), inc=chPE)
                        trn = chPE.n
                        for half in range(2):
                            P.op("vector", lambda e, pdst=pdst, g=g, half=half:
                                 e.tensor_copy(
                                     vhat[:].rearrange("p (b w) -> p b w", w=130)
                                     [:, g * 4:(g + 1) * 4, half * 65:half * 65 + 64],
                                     pdst[:, 0:512].rearrange("p (a w) -> p a w", w=128)
                                     [:, :, half * 64:(half + 1) * 64]),
                                 waits=[(chPE, trn), (chGP, lsr_memset)], inc=chDVE)
                        trS["ev"][tpar] = chDVE.n
                        trS["i"] += 1
                    vh_ready = chDVE.n

            if a + 2 < 8:
                prefetch_attn(a + 2)
            qln = chACT.n
            for hh in range(2):
                qsl = qlrb[hh * 32:(hh + 1) * 32, :]
                ksl = klrb[hh * 32:(hh + 1) * 32, :]
                for ic in range(4):
                    njt = 4 * ic + 4
                    for jt in range(njt):
                        bkey = jt % 2
                        bank = psA if bkey == 0 else psB
                        P.op("tensor", lambda e, jt=jt, ic=ic, bank=bank, qsl=qsl, ksl=ksl:
                             e.matmul(bank[:], ksl[:, jt * 128:(jt + 1) * 128],
                                      qsl[:, ic * 512:(ic + 1) * 512],
                                      start=True, stop=True),
                             waits=[(chACT, qln), (chACT, bank_ev[bkey])], inc=chPE)
                        mmn = chPE.n
                        P.op("scalar", lambda e, jt=jt, bank=bank: e.activation(
                            pslot(jt), bank[:], AF.Exp, bias=C_ZERO, scale=C_RSQ32),
                            waits=[(chPE, mmn), (chPE, sc_used.get(jt % 2, 0))], inc=chACT)
                        bank_ev[bkey] = chACT.n
                        expn = chACT.n
                        mwait = [(chACT, expn)]
                        if jt >= 4 * ic:
                            kk = jt - 4 * ic
                            P.op("vector", lambda e, jt=jt, kk=kk: e.tensor_tensor(
                                pslot(jt), pslot(jt),
                                mask[:, 512 - 128 * kk:1024 - 128 * kk], ALU.mult),
                                waits=[(chACT, expn)], inc=chDVE)
                            mwait = [(chDVE, chDVE.n)]
                        P.op("tensor", lambda e, jt=jt, hh=hh, njt=njt: e.matmul(
                            psC[0:65, :],
                            vhat[:, jt * 130 + hh * 65:jt * 130 + hh * 65 + 65],
                            pslot(jt), start=(jt == 0), stop=(jt == njt - 1)),
                            waits=mwait + ([(chDVE, vh_ready), (chDVE, ynorm_done[0]),
                                            (chACT, yevac_done[0])] if jt == 0 else []),
                            inc=chPE)
                        sc_used[jt % 2] = chPE.n
                    avn = chPE.n
                    P.op("vector", lambda e: e.reciprocal(rcp[0:1, :], psC[64:65, :]),
                         waits=[(chPE, avn)], inc=chDVE)
                    rn = chDVE.n
                    P.op("tensor", lambda e: e.matmul(psH[0:64, :], onef[0:1, 0:64],
                                                      rcp[0:1, :], start=True, stop=True),
                         waits=[(chDVE, rn)], inc=chPE)
                    bn = chPE.n
                    P.op("scalar", lambda e: e.activation(ybf[:], psC[0:64, :], AF.Identity,
                                                          bias=C_ZERO[0:64, :], scale=1.0),
                         waits=[(chPE, bn)], inc=chACT)
                    yevac_done[0] = chACT.n
                    yn = chACT.n
                    P.op("vector", lambda e, a=a, hh=hh, ic=ic: e.tensor_tensor(
                        arC[hh * 64:(hh + 1) * 64, a * T + ic * 512:a * T + (ic + 1) * 512],
                        ybf[:], psH[0:64, :], ALU.mult),
                        waits=[(chACT, yn)], inc=chDVE)
                    ynorm_done[0] = chDVE.n
            qlr_free[0] = chPE.n
            vhat_free[0] = chPE.n

            if a == 1:
                for c in range(NCF):
                    emit_p2_chunk(c)
                m_g1 = part_max_to_ccs(pmx[:, 48:72], 0)
                c3 = collective(2, chDVE, m_g1)
                s_g1[0] = scale_math(0, I_SG1, I_RG1, c3)
            if a == 4:
                p2dv = chDVE.n
                for c in range(NCF):
                    emit_p3_chunk(c, p2dv)
                m_o = part_max_to_ccs(pmx[:, 72:96], 0)
                c4 = collective(3, chDVE, m_o)
                s_o[0] = scale_math(0, I_SO, I_RO, c4)

        qkv_all_done = chPE.n

        if DEBUG:
            P.op("sync", lambda e: e.dma_start(out=d_yall[:, :], in_=arC[:]),
                 waits=[(chDVE, chDVE.n)], inc=chDBG, inck=16)
            P.op("sync", lambda e: e.dma_start(out=d_scl[:, :], in_=scl[:]),
                 waits=[(chDVE, chDVE.n)], inc=chDBG, inck=16)
        P.op("sync", lambda e: e.dma_start(
            out=arAf.rearrange("p (t w) -> p t w", w=1024),
            in_=x_h[0:1024, :].rearrange("(t p) d -> p t d", p=128)),
            waits=[(chPE, qkv_all_done), (chDBG, chDBG.n)], inc=chACC, inck=16)
        P.op("sync", lambda e: e.dma_start(
            out=arBf.rearrange("p (t w) -> p t w", w=1024),
            in_=x_h[1024:2048, :].rearrange("(t p) d -> p t d", p=128)),
            inc=chACC, inck=16)
        acc_ld = chACC.n

        op_wfree = {}
        trf_ev = {0: chDVE.n, 1: chDVE.n}
        trf_add = [0]

        def evac_transpose_add(bank, bias_ap, gp, oc, mm_snap):
            P.op("scalar", lambda e, bank=bank, bias_ap=bias_ap: e.activation(
                xstage[:, 3 * 1024:3 * 1024 + 512], bank[:], AF.Identity,
                bias=bias_ap, scale=1.0),
                waits=[(chPE, mm_snap), (chDVE, trf_add[0])], inc=chACT)
            evn = chACT.n
            gpar = gp % 2
            pdst = psG if gpar == 0 else psH
            for i in range(4):
                P.op("tensor", lambda e, i=i, pdst=pdst: e.transpose(
                    pdst[:, i * 128:(i + 1) * 128],
                    xstage[:, 3 * 1024 + i * 128:3 * 1024 + (i + 1) * 128], idf[:]),
                    waits=([(chACT, evn), (chDVE, trf_ev[gpar])] if i == 0 else []),
                    inc=chPE)
            trn = chPE.n
            accv = (arAf if gp < 2 else arBf).rearrange("p (t w) -> p t w", w=1024)[
                :, (gp % 2) * 4:(gp % 2) * 4 + 4, oc * 128:(oc + 1) * 128]
            P.op("vector", lambda e, accv=accv, pdst=pdst: e.tensor_tensor(
                accv, accv, pdst[:].rearrange("p (a w) -> p a w", w=128), ALU.add),
                waits=[(chPE, trn), (chACC, acc_ld)], inc=chDVE)
            trf_ev[gpar] = chDVE.n
            trf_add[0] = chDVE.n

        for oc in range(ND):
            par = oc % 2
            P.op("gpsimd", lambda e, oc=oc, par=par: e.dma_start(
                out=wbslot(par).rearrange("p (c o) -> p c o", o=128),
                in_=wo_h[oc].rearrange("(c p) o -> p c o", p=128)),
                waits=[(chPE, attw_free.get((0, par), 0)),
                       (chPE, op_wfree.get(par, 0))], inc=chGW, inck=16)
            ldn = chGW.n
            for p in range(4):
                bkey = p % 2
                bank = psA if bkey == 0 else psB
                for dc in range(ND):
                    P.op("tensor", lambda e, p=p, dc=dc, bank=bank, par=par: e.matmul(
                        bank[:], wbslot(par)[:, dc * 128:(dc + 1) * 128],
                        arC[:, dc * T + p * 512:dc * T + (p + 1) * 512],
                        start=(dc == 0), stop=(dc == ND - 1)),
                        waits=([(chGW, ldn), (chACT, bank_ev[bkey]),
                                (chDVE, ynorm_done[0])] if dc == 0 else []), inc=chPE)
                mmn = chPE.n
                evac_transpose_add(bank, bias4[:, 24 + oc:24 + oc + 1], p, oc, mmn)
                bank_ev[bkey] = chACT.n
            op_wfree[par] = chPE.n
        oproj_pe_done = chPE.n
        oproj_done = chDVE.n

        def oT_view(ic):
            if ic < 16:
                return arC[:, ic * 1024:(ic + 1) * 1024]
            if ic < 18:
                return vhat[:, (ic - 16) * 1024:(ic - 15) * 1024]
            if ic < 20:
                return qkscr[:, (ic - 18) * 1024:(ic - 17) * 1024]
            return scbh(2 + (ic - 20))

        p3_dve_done = chDVE.n
        dn_nat_free = {}
        dn_dqs_free = [0]
        dn_wt_used = {}
        oT_used = {}
        sr_free = {}
        p4_ld_free = [0]
        dn_cached = {}

        for H in range(2):
            for ic in range(NCF):
                par = ic % 2
                inb = scbh(par)
                P.op("scalar", lambda e, ic=ic, H=H, inb=inb: e.dma_start(
                    out=inb, in_=scr["o"][ic, :, H * 1024:(H + 1) * 1024]),
                    waits=[(chSW, chSW.n), (chDVE, sr_free.get(par, 0)),
                           (chPE, oproj_pe_done)],
                    inc=chSR, inck=16)
                ldn = chSR.n
                ft = wfslot(7)
                P.op("vector", lambda e, inb=inb, ft=ft: e.tensor_scalar(
                    ft, inb, SCL(I_RO), MAGIC, ALU.mult, ALU.add),
                    waits=[(chSR, ldn), (chDVE, s_o[0]), (chPE, oproj_pe_done),
                           (chPE, oT_used.get(ic, 0)), (chDBG, chDBG.n)], inc=chDVE)
                P.op("vector", lambda e, ic=ic, ft=ft: e.tensor_scalar(
                    oT_view(ic), ft, MAGIC, SCL(I_SO), ALU.subtract, ALU.mult),
                    waits=[(chSW, chSW.n), (chDVE, chDVE.n)], inc=chDVE)
                sr_free[par] = chDVE.n
            oT_ready = chDVE.n

            for oc in range(ND):
                par = oc % 2
                wdnT = wbf[:, par * 3072:(par + 1) * 3072]
                if oc not in dn_cached:
                    P.op("sync", lambda e, oc=oc: e.dma_start(
                        out=wf32[:, 0:3 * 1024].rearrange("p (a w) -> p a w", w=1024),
                        in_=wdn_h[oc * 128:(oc + 1) * 128, :].rearrange(
                            "p (a w) -> p a w", w=1024)),
                        waits=[(chDVE, p3_dve_done), (chGP, p4_ld_free[0]),
                               (chPE, dn_wt_used.get(par, 0))],
                        inc=chL, inck=16)
                    ldn = chL.n
                    sq = dqs[:, 0:32]
                    rq = dqs[:, 32:64]
                    for pc in range(3):
                        fw = wfslot(pc)
                        nat = xsb[:, (pc % 2) * 1024:((pc % 2) + 1) * 1024]
                        P.op("vector", lambda e, fw=fw, sq=sq: e.tensor_reduce(
                            sq, fw.rearrange("p (b w) -> p b w", w=32), AX.X, ALU.max,
                            apply_absolute_value=True),
                            waits=[(chL, ldn), (chGP, dn_dqs_free[0]),
                                   (chACT, chACT.n if (H == 0 and oc == 0 and pc == 0) else 0)],
                            inc=chDVE)
                        P.op("vector", lambda e, sq=sq: e.tensor_scalar(
                            sq, sq, 1.0 / 7.0, 1e-12, ALU.mult, ALU.max),
                            waits=[(chDVE, chDVE.n)], inc=chDVE)
                        P.op("vector", lambda e, sq=sq, rq=rq: e.reciprocal(rq, sq),
                            waits=[(chDVE, chDVE.n)], inc=chDVE)
                        n_sr2 = chDVE.n
                        P.op("gpsimd", lambda e, fw=fw, rq=rq: e.tensor_tensor(
                            fw.rearrange("p (b w) -> p b w", w=32),
                            fw.rearrange("p (b w) -> p b w", w=32),
                            rq.unsqueeze(2).broadcast_to([128, 32, 32]), ALU.mult),
                            waits=[(chDVE, n_sr2)], inc=chGP)
                        n_gp2 = chGP.n
                        P.op("vector", lambda e, fw=fw: e.tensor_scalar(
                            fw, fw, MAGIC, MAGIC, ALU.add, ALU.subtract),
                            waits=[(chGP, n_gp2)], inc=chDVE)
                        n_rnd2 = chDVE.n
                        P.op("gpsimd", lambda e, fw=fw, nat=nat, sq=sq: e.tensor_tensor(
                            nat.rearrange("p (b w) -> p b w", w=32),
                            fw.rearrange("p (b w) -> p b w", w=32),
                            sq.unsqueeze(2).broadcast_to([128, 32, 32]), ALU.mult),
                            waits=[(chDVE, n_rnd2), (chPE, dn_nat_free.get(pc % 2, 0))],
                            inc=chGP)
                        dn_dqs_free[0] = chGP.n
                        n_nat2 = chGP.n
                        if pc == 2:
                            p4_ld_free[0] = chGP.n
                        for g in range(2):
                            tpar = trS["i"] % 2
                            pdst = psE if tpar == 0 else psF
                            w = [(chGP, n_nat2), (chDVE, trS["ev"].get(tpar, 0)),
                                 (chPE, dn_wt_used.get(par, 0))]
                            for i in range(4):
                                dc8 = g * 4 + i
                                P.op("tensor", lambda e, i=i, dc8=dc8, pdst=pdst, nat=nat:
                                     e.transpose(pdst[:, i * 128:(i + 1) * 128],
                                                 nat[:, dc8 * 128:(dc8 + 1) * 128], idb[:]),
                                     waits=(w if i == 0 else []), inc=chPE)
                            trn = chPE.n
                            dn_nat_free[pc % 2] = trn
                            P.op("vector", lambda e, pdst=pdst, wdnT=wdnT, pc=pc, g=g:
                                 e.tensor_copy(
                                     wdnT[:, pc * 1024 + g * 512:pc * 1024 + (g + 1) * 512],
                                     pdst[:, 0:512]),
                                 waits=[(chPE, trn)], inc=chDVE)
                            trS["ev"][tpar] = chDVE.n
                            trS["i"] += 1
                    wt_ready_dve = chDVE.n
                    P.op("sync", lambda e, oc=oc, wdnT=wdnT: e.dma_start(
                        out=dnT_h[oc, :, :], in_=wdnT),
                        waits=[(chDVE, wt_ready_dve)], inc=chL, inck=16)
                    dn_cached[oc] = (wt_ready_dve, chL.n)
                    wt_dep = [(chDVE, wt_ready_dve)]
                else:
                    P.op("sync", lambda e, oc=oc, wdnT=wdnT: e.dma_start(
                        out=wdnT, in_=dnT_h[oc, :, :]),
                        waits=[(chL, dn_cached[oc][1]), (chPE, dn_wt_used.get(par, 0))],
                        inc=chL, inck=16)
                    wt_dep = [(chL, chL.n)]

                for ph in range(2):
                    bkey = ph % 2
                    bank = psA if bkey == 0 else psB
                    for ic in range(NCF):
                        P.op("tensor", lambda e, ph=ph, ic=ic, bank=bank, wdnT=wdnT:
                             e.matmul(bank[:], wdnT[:, ic * 128:(ic + 1) * 128],
                                      oT_view(ic)[:, ph * 512:(ph + 1) * 512],
                                      start=(ic == 0), stop=(ic == NCF - 1)),
                             waits=(wt_dep + [(chDVE, oT_ready), (chACT, bank_ev[bkey])]
                                    if ic == 0 else []), inc=chPE)
                    mmn = chPE.n
                    dn_wt_used[par] = mmn
                    for ic2 in range(NCF):
                        oT_used[ic2] = mmn
                    evac_transpose_add(bank, C_ZERO, H * 2 + ph, oc, mmn)
                    bank_ev[bkey] = chACT.n

        ffn_done = chDVE.n

        P.op("sync", lambda e: e.dma_start(
            out=out_h[0:1024, :].rearrange("(t p) d -> p t d", p=128),
            in_=arAf.rearrange("p (t w) -> p t w", w=1024)),
            waits=[(chDVE, ffn_done)], inc=chOUT, inck=16)
        P.op("sync", lambda e: e.dma_start(
            out=out_h[1024:2048, :].rearrange("(t p) d -> p t d", p=128),
            in_=arBf.rearrange("p (t w) -> p t w", w=1024)),
            inc=chOUT, inck=16)
        P.op("sync", lambda e: e.nop(), waits=[(chOUT, chOUT.n), (chDBG, chDBG.n)])

        for c in P.chans:
            c.sem = ctx.enter_context(nc.semaphore(c.name))
        block = ctx.enter_context(nc.Block())
        P.emit(block)

    return nc


def kernel(L0_attn_x, L0_attn_Wq, L0_attn_bq, L0_attn_Wk, L0_attn_bk,
           L0_attn_Wv, L0_attn_bv, L0_attn_Wo, L0_attn_bo,
           L0_attn_Wq_lsr, L0_attn_Wk_lsr,
           L0_ffn_w_gate, L0_ffn_w_up, L0_ffn_w_down):
    f32 = np.float32
    x = np.ascontiguousarray(np.asarray(L0_attn_x, dtype=f32))

    def wt_chunks(W):
        WT = np.ascontiguousarray(np.asarray(W, dtype=f32).T)
        return np.ascontiguousarray(WT.reshape(D, ND, 128).transpose(1, 0, 2))

    shared = {
        "wq_t": wt_chunks(L0_attn_Wq),
        "wk_t": wt_chunks(L0_attn_Wk),
        "wv_t": wt_chunks(L0_attn_Wv),
        "wo_t": wt_chunks(L0_attn_Wo),
        "lsr_q": np.ascontiguousarray(np.asarray(L0_attn_Wq_lsr, dtype=f32)),
        "lsr_k": np.ascontiguousarray(np.asarray(L0_attn_Wk_lsr, dtype=f32)),
        "b_qkvo": np.ascontiguousarray(
            np.stack([np.asarray(b, dtype=f32) for b in
                      (L0_attn_bq, L0_attn_bk, L0_attn_bv, L0_attn_bo)])),
        "w_up": np.ascontiguousarray(np.asarray(L0_ffn_w_up, dtype=f32)),
        "w_gate": np.ascontiguousarray(np.asarray(L0_ffn_w_gate, dtype=f32)),
        "w_down": np.ascontiguousarray(np.asarray(L0_ffn_w_down, dtype=f32)),
        "idn_f": np.eye(128, dtype=f32),
        "idn_b": np.eye(128, dtype=ml_dtypes.bfloat16),
        "maskbig": (np.arange(1024)[None, :] >= (np.arange(128)[:, None] + 512)
                    ).astype(ml_dtypes.bfloat16),
        "ones_f": np.ones((1, 128), dtype=f32),
        "consts": np.broadcast_to(
            np.array([EPS, 1.0 / np.sqrt(32.0), 0.0, 1.0], dtype=f32), (128, 4)).copy(),
    }

    nc = build_nc()
    in_maps = [dict(shared, x=np.ascontiguousarray(x[b])) for b in range(8)]
    res = run_bass_kernel_spmd(nc, in_maps, core_ids=list(range(8)))
    out = np.stack([np.asarray(res.results[b]["out"]) for b in range(8)])
    return out.astype(np.float32)


# version 03
# speedup vs baseline: 1.1612x; 1.1612x over previous
import numpy as np
import ml_dtypes
from contextlib import ExitStack

import concourse.bass as bass
import concourse.mybir as mybir
from concourse.bass_utils import run_bass_kernel_spmd

F32 = mybir.dt.float32
BF16 = mybir.dt.bfloat16
AF = mybir.ActivationFunctionType
ALU = mybir.AluOpType
AX = mybir.AxisListType

T = 2048
D = 1024
NT = 16
ND = 8
FF = 3072
NCF = 24
MAGIC = float(1.5 * 2.0**23)
QMAX = 2.0**15 - 1.0
EPS = 1e-5
ENGS = ("sync", "gpsimd", "tensor", "vector", "scalar")
DEBUG = False


class Ch:
    __slots__ = ("name", "n", "sem")

    def __init__(self, name):
        self.name = name
        self.n = 0
        self.sem = None


class Prog:

    def __init__(self):
        self.ops = {e: [] for e in ENGS}
        self.chans = []
        self.guar = {e: {} for e in ENGS}

    def chan(self, name):
        c = Ch(name)
        self.chans.append(c)
        return c

    def op(self, eng, fn, waits=(), inc=None, inck=1):
        wl = []
        g = self.guar[eng]
        for ch, val in waits:
            if val is None:
                val = ch.n
            if val > 0 and g.get(ch.name, 0) < val:
                wl.append((ch, val))
                g[ch.name] = val
        incs = []
        if inc is not None:
            inc.n += inck
            incs.append((inc, inck))
        self.ops[eng].append((wl, fn, incs))

    def emit(self, block):
        for ename in ENGS:
            ops = self.ops[ename]

            def body(e, ops=ops):
                for wl, fn, incs in ops:
                    for ch, val in wl:
                        e.wait_ge(ch.sem, val)
                    ins = fn(e)
                    for ch, k in incs:
                        ins.then_inc(ch.sem, k)
            getattr(block, ename)(body)


def build_nc():
    nc = bass.Bass("TRN2", target_bir_lowering=False, debug=False)

    x_h = nc.dram_tensor("x", [T, D], F32, kind="ExternalInput")
    wq_h = nc.dram_tensor("wq_t", [ND, D, 128], F32, kind="ExternalInput")
    wk_h = nc.dram_tensor("wk_t", [ND, D, 128], F32, kind="ExternalInput")
    wv_h = nc.dram_tensor("wv_t", [ND, D, 128], F32, kind="ExternalInput")
    wo_h = nc.dram_tensor("wo_t", [ND, D, 128], F32, kind="ExternalInput")
    lsrq_h = nc.dram_tensor("lsr_q", [16, 64, 32], F32, kind="ExternalInput")
    lsrk_h = nc.dram_tensor("lsr_k", [16, 64, 32], F32, kind="ExternalInput")
    b_h = nc.dram_tensor("b_qkvo", [4, D], F32, kind="ExternalInput")
    wup_h = nc.dram_tensor("w_up", [FF, D], F32, kind="ExternalInput")
    wgt_h = nc.dram_tensor("w_gate", [FF, D], F32, kind="ExternalInput")
    wdn_h = nc.dram_tensor("w_down", [D, FF], F32, kind="ExternalInput")
    idf_h = nc.dram_tensor("idn_f", [128, 128], F32, kind="ExternalInput")
    idb_h = nc.dram_tensor("idn_b", [128, 128], BF16, kind="ExternalInput")
    msk_h = nc.dram_tensor("maskbig", [128, 1024], BF16, kind="ExternalInput")
    onef_h = nc.dram_tensor("ones_f", [1, 128], F32, kind="ExternalInput")
    cst_h = nc.dram_tensor("consts", [128, 4], F32, kind="ExternalInput")
    out_h = nc.dram_tensor("out", [T, D], F32, kind="ExternalOutput")

    kind = dict(kind="ExternalOutput") if DEBUG else {}
    scr = {
        "up": nc.dram_tensor("scr_up", [NCF, 128, T], BF16, **kind),
        "gt": nc.dram_tensor("scr_gt", [NCF, 128, T], BF16, **kind),
        "g1": nc.dram_tensor("scr_g1", [NCF, 128, T], BF16, **kind),
        "o": nc.dram_tensor("scr_o", [NCF, 128, T], BF16, **kind),
    }
    if DEBUG:
        d_xnT = nc.dram_tensor("d_xnT", [128, NT * 1024], BF16, kind="ExternalOutput")
        d_xqT = nc.dram_tensor("d_xqT", [128, NT * 1024], BF16, kind="ExternalOutput")
        d_yall = nc.dram_tensor("d_yall", [128, NT * 1024], BF16, kind="ExternalOutput")
        d_stats = nc.dram_tensor("d_stats", [128, 96], F32, kind="ExternalOutput")
        d_scl = nc.dram_tensor("d_scl", [128, 16], F32, kind="ExternalOutput")
        d_var = nc.dram_tensor("d_var", [128, 48], F32, kind="ExternalOutput")
    dnT_h = nc.dram_tensor("dnT_cache", [ND, 128, FF], BF16)
    cc_in = [nc.dram_tensor(f"cc_in{i}", [1, 16], F32) for i in range(4)]
    cc_out = [nc.dram_tensor(f"cc_out{i}", [1, 16], F32) for i in range(4)]

    P = Prog()
    chL = P.chan("chL")
    chACC = P.chan("chACC")
    chOUT = P.chan("chOUT")
    chSW = P.chan("chSW")
    chSR = P.chan("chSR")
    chGW = P.chan("chGW")
    chCC = P.chan("chCC")
    chPE = P.chan("chPE")
    chACT = P.chan("chACT")
    chDVE = P.chan("chDVE")
    chGP = P.chan("chGP")
    chDBG = P.chan("chDBG")

    with ExitStack() as ctx:
        sbt = lambda name, shape, dt: ctx.enter_context(nc.sbuf_tensor(name, shape, dt))
        pst = lambda name, shape, dt: ctx.enter_context(nc.psum_tensor(name, shape, dt))

        xstage = sbt("xstage", [128, 4 * 1024], F32)
        wf32 = sbt("wf32", [128, 8 * 1024], F32)
        arA = sbt("arA", [128, NT * 1024], BF16)
        arB = sbt("arB", [128, NT * 1024], BF16)
        arC = sbt("arC", [128, NT * 1024], BF16)
        scb = sbt("scb", [128, 6 * 1024], BF16)
        wbf = sbt("wbf", [128, 6 * 1024], BF16)
        vhat = sbt("vhat", [128, NT * 130], BF16)
        qkscr = sbt("qkscr", [128, T], BF16)
        qlrb = sbt("qlrb", [64, T], BF16)
        klrb = sbt("klrb", [64, T], BF16)
        ptile = sbt("ptile", [128, 4 * 512], BF16)
        ybf = sbt("ybf", [64, 512], BF16)
        lsq = sbt("lsq", [128, 8 * 64], BF16)
        lsk = sbt("lsk", [128, 8 * 64], BF16)
        idf = sbt("idf", [128, 128], F32)
        idb = sbt("idb", [128, 128], BF16)
        mask = sbt("mask", [128, 1024], BF16)
        onef = sbt("onef", [1, 128], F32)
        cst = sbt("cst", [128, 4], F32)
        sS1 = sbt("sS1", [128, 16], F32)
        sS2 = sbt("sS2", [128, 16], F32)
        sM = sbt("sM", [128, 16], F32)
        sR = sbt("sR", [128, 16], F32)
        sNB = sbt("sNB", [128, 16], F32)
        sMX = sbt("sMX", [128, 16], F32)
        sA = sbt("sA", [128, 16], F32)
        sBt = sbt("sBt", [128, 16], F32)
        tmp16 = sbt("tmp16", [128, 16], F32)
        tmp16b = sbt("tmp16b", [128, 16], F32)
        bias4 = sbt("bias4", [128, 4 * 8], F32)
        scl = sbt("scl", [128, 16], F32)
        pmx = sbt("pmx", [128, 4 * 24], F32)
        dqs = sbt("dqs", [128, 4 * 32], F32)
        ccs = sbt("ccs", [1, 16], F32)
        cco = sbt("cco", [1, 16], F32)
        rcp = sbt("rcp", [1, 512], F32)

        psA = pst("psA", [128, 512], F32)
        psB = pst("psB", [128, 512], F32)
        psC = pst("psC", [128, 512], F32)
        psD = pst("psD", [128, 512], F32)
        psE = pst("psE", [128, 1024], BF16)
        psF = pst("psF", [128, 1024], BF16)
        psG = pst("psG", [128, 512], F32)
        psH = pst("psH", [128, 512], F32)

        C_EPS = cst[:, 0:1]
        C_RSQ32 = cst[:, 1:2]
        C_ZERO = cst[:, 2:3]
        SCL = lambda i: scl[:, i:i + 1]
        I_SX, I_RX, I_SUP, I_RUP, I_SGT, I_RGT, I_SG1, I_RG1, I_SO, I_RO = range(10)

        arAf = arA[:].bitcast(F32)
        arBf = arB[:].bitcast(F32)
        wfb = wf32[:].bitcast(BF16)
        xsb = xstage[:].bitcast(BF16)

        xslot = lambda i: xstage[:, (i % 4) * 1024:((i % 4) + 1) * 1024]
        wfslot = lambda i: wf32[:, i * 1024:(i + 1) * 1024]
        wbslot = lambda i: wbf[:, i * 1024:(i + 1) * 1024]
        scbh = lambda i: scb[:, i * 1024:(i + 1) * 1024]
        scbs = lambda i: scb[:, i * 2048:(i + 1) * 2048]
        pslot = lambda i: ptile[:, (i % 4) * 512:((i % 4) + 1) * 512]

        for dst, src in ((idf, idf_h), (idb, idb_h), (mask, msk_h), (onef, onef_h), (cst, cst_h)):
            P.op("sync", lambda e, dst=dst, src=src: e.dma_start(out=dst[:], in_=src[:, :]),
                 inc=chL, inck=16)
        P.op("sync", lambda e: e.dma_start(
            out=xstage[0:1, :].rearrange("o (i d) -> o i d", i=4),
            in_=b_h[:, :].unsqueeze(0)), inc=chL, inck=16)
        ld_consts = chL.n

        P.op("gpsimd", lambda e: e.memset(lsq[:], 0.0), inc=chGP)
        P.op("gpsimd", lambda e: e.memset(lsk[:], 0.0), inc=chGP)
        P.op("gpsimd", lambda e: e.memset(
            vhat[:].rearrange("p (b w) -> p b w", w=130)[:, :, 64:65], 1.0), inc=chGP)
        P.op("gpsimd", lambda e: e.memset(
            vhat[:].rearrange("p (b w) -> p b w", w=130)[:, :, 129:130], 1.0), inc=chGP)
        lsr_memset = chGP.n
        for a in range(8):
            for dst, src in ((lsq, lsrq_h), (lsk, lsrk_h)):
                P.op("gpsimd", lambda e, dst=dst, src=src, a=a: e.dma_start(
                    out=dst[0:64, a * 64:a * 64 + 32], in_=src[2 * a, :, :]),
                    waits=[(chGP, lsr_memset)], inc=chGW, inck=16)
                P.op("gpsimd", lambda e, dst=dst, src=src, a=a: e.dma_start(
                    out=dst[64:128, a * 64 + 32:a * 64 + 64], in_=src[2 * a + 1, :, :]),
                    inc=chGW, inck=16)
        lsr_ready = chGW.n

        for i in range(4):
            for c in range(ND):
                P.op("tensor", lambda e, i=i, c=c: e.matmul(
                    psD[:, (i * 8 + c):(i * 8 + c) + 1],
                    xstage[0:1, i * 1024 + c * 128:i * 1024 + (c + 1) * 128],
                    onef[0:1, 0:1], start=True, stop=True),
                    waits=[(chL, ld_consts)], inc=chPE)
        bias_mm = chPE.n
        P.op("scalar", lambda e: e.activation(bias4[:], psD[:, 0:32], AF.Identity,
                                              bias=C_ZERO, scale=1.0),
             waits=[(chPE, bias_mm), (chL, ld_consts)], inc=chACT)
        bias_ready = chACT.n

        stats_done = {}
        for t in range(NT):
            P.op("sync", lambda e, t=t: e.dma_start(out=xslot(t), in_=x_h[t * 128:(t + 1) * 128, :]),
                 waits=([(chDVE, stats_done[t - 4])] if t >= 4 else [(chACT, bias_ready)]),
                 inc=chL, inck=16)
            ldn = chL.n
            P.op("vector", lambda e, t=t: e.tensor_reduce(
                sS1[:, t:t + 1], xslot(t), AX.XY, ALU.add),
                waits=[(chL, ldn), (chDVE, chDVE.n)], inc=chDVE)
            n_s1 = chDVE.n
            P.op("scalar", lambda e, t=t: e.activation(
                xslot(t), xslot(t), AF.Square, bias=C_ZERO, scale=1.0),
                waits=[(chDVE, n_s1)], inc=chACT)
            n_sq = chACT.n
            P.op("vector", lambda e, t=t: e.tensor_reduce(
                sS2[:, t:t + 1], xslot(t), AX.XY, ALU.add),
                waits=[(chACT, n_sq), (chDVE, chDVE.n)], inc=chDVE)
            stats_done[t] = chDVE.n
        stats_all = chDVE.n

        P.op("vector", lambda e: e.tensor_scalar(sM[:], sS1[:], 1.0 / D, None, ALU.mult),
             waits=[(chDVE, stats_all), (chDVE, chDVE.n)], inc=chDVE)
        P.op("vector", lambda e: e.tensor_scalar(tmp16[:], sS2[:], 1.0 / D, None, ALU.mult), inc=chDVE)
        P.op("vector", lambda e: e.tensor_tensor(tmp16b[:], sM[:], sM[:], ALU.mult),
             waits=[(chDVE, chDVE.n)], inc=chDVE)
        P.op("vector", lambda e: e.tensor_tensor(tmp16[:], tmp16[:], tmp16b[:], ALU.subtract),
             waits=[(chDVE, chDVE.n)], inc=chDVE)
        var_done = chDVE.n
        if DEBUG:
            P.op("sync", lambda e: e.dma_start(out=d_var[:, 0:16], in_=tmp16[:]),
                 waits=[(chDVE, var_done)], inc=chDBG, inck=16)
            P.op("sync", lambda e: e.dma_start(out=d_var[:, 16:32], in_=tmp16b[:]),
                 inc=chDBG, inck=16)
        P.op("scalar", lambda e: e.activation(sR[:], tmp16[:], AF.Sqrt, bias=C_EPS, scale=1.0),
             waits=[(chDVE, var_done), (chDBG, chDBG.n)], inc=chACT)
        sd_done = chACT.n
        if DEBUG:
            P.op("sync", lambda e: e.dma_start(out=d_var[:, 32:48], in_=sR[:]),
                 waits=[(chACT, sd_done)], inc=chDBG, inck=16)
        P.op("vector", lambda e: e.reciprocal(sR[:], sR[:]),
             waits=[(chACT, sd_done), (chDBG, chDBG.n)], inc=chDVE)
        P.op("vector", lambda e: e.tensor_tensor(sNB[:], sM[:], sR[:], ALU.mult),
             waits=[(chDVE, chDVE.n)], inc=chDVE)
        P.op("vector", lambda e: e.tensor_scalar(sNB[:], sNB[:], -1.0, None, ALU.mult),
             waits=[(chDVE, chDVE.n)], inc=chDVE)
        ln_ready = chDVE.n

        trS = {"i": 0, "ev": {}}
        sc_bank_ev = {}
        av_free = {}
        av_evac = {}

        def transpose_tile_to_arena(t, src_ap, dst_ar, dep_waits):
            for g in range(2):
                par = trS["i"] % 2
                pdst = psE if par == 0 else psF
                w = list(dep_waits) + [(chDVE, trS["ev"].get(par, 0)), (chL, ld_consts),
                                       (chACT, sc_bank_ev.get(3, 0) if par == 0 else
                                        av_evac.get(1, 0)),
                                       (chDVE, av_free.get(1, 0) if par == 1 else 0)]
                for i in range(4):
                    c = g * 4 + i
                    P.op("tensor", lambda e, i=i, c=c, pdst=pdst, src_ap=src_ap: e.transpose(
                        pdst[:, i * 128:(i + 1) * 128], src_ap[:, c * 128:(c + 1) * 128], idb[:]),
                        waits=(w if i == 0 else []), inc=chPE)
                trn = chPE.n
                dv = dst_ar[:].rearrange("p (c w) -> p c w", w=T)[
                    :, g * 4:(g + 1) * 4, t * 128:(t + 1) * 128]
                P.op("vector", lambda e, pdst=pdst, dv=dv: e.tensor_copy(
                    dv, pdst[:, 0:512].rearrange("p (a w) -> p a w", w=128)),
                    waits=[(chPE, trn)], inc=chDVE)
                trS["ev"][par] = chDVE.n
                trS["i"] += 1

        xn_done = {}
        for t in range(NT):
            P.op("sync", lambda e, t=t: e.dma_start(out=xslot(t), in_=x_h[t * 128:(t + 1) * 128, :]),
                 waits=([(chDVE, xn_done[t - 4])] if t >= 4 else [(chDVE, stats_all)]),
                 inc=chL, inck=16)
            ldn = chL.n
            dst = scbh(t % 2)
            P.op("scalar", lambda e, t=t, dst=dst: e.activation(
                dst, xslot(t), AF.Identity, bias=sNB[:, t:t + 1], scale=sR[:, t:t + 1]),
                waits=[(chL, ldn), (chDVE, ln_ready), (chDVE, xn_done.get(t - 2, 0))],
                inc=chACT)
            act_n = chACT.n
            transpose_tile_to_arena(t, dst, arA, [(chACT, act_n)])
            P.op("vector", lambda e, t=t, dst=dst: e.tensor_reduce(
                sMX[:, t:t + 1], dst, AX.XY, ALU.max, apply_absolute_value=True), inc=chDVE)
            xn_done[t] = chDVE.n
        xnT_ready = chDVE.n
        if DEBUG:
            P.op("sync", lambda e: e.dma_start(out=d_xnT[:, :], in_=arA[:]),
                 waits=[(chDVE, xnT_ready)], inc=chDBG, inck=16)
            for i, st_ in enumerate((sM, sR, sNB, sMX, sS1, sS2)):
                P.op("sync", lambda e, i=i, st_=st_: e.dma_start(
                    out=d_stats[:, i * 16:(i + 1) * 16], in_=st_[:]), inc=chDBG, inck=16)

        def part_max_to_ccs(src_ap, col, extra_waits=()):
            P.op("vector", lambda e, src_ap=src_ap: e.tensor_reduce(
                tmp16[:, 0:1], src_ap, AX.X, ALU.max),
                waits=list(extra_waits) + [(chGW, chGW.n), (chDVE, chDVE.n)], inc=chDVE)
            n1 = chDVE.n
            P.op("tensor", lambda e: e.transpose(
                psG[0:1, 0:128], tmp16[:, 0:1], idf[:]),
                waits=[(chDVE, n1)], inc=chPE)
            n2 = chPE.n
            P.op("vector", lambda e, col=col: e.tensor_reduce(
                ccs[0:1, col:col + 1], psG[0:1, 0:128], AX.X, ALU.max),
                waits=[(chPE, n2)], inc=chDVE)
            return chDVE.n

        def collective(idx, dep_ch, dep_val):
            P.op("gpsimd", lambda e, idx=idx: e.dma_start(out=cc_in[idx][:, :], in_=ccs[:, :]),
                 waits=[(dep_ch, dep_val)], inc=chGW, inck=16)
            n = chGW.n
            P.op("gpsimd", lambda e, idx=idx: e.collective_compute(
                "AllReduce", ALU.max, replica_groups=[list(range(8))],
                ins=[cc_in[idx].ap()], outs=[cc_out[idx].ap()]),
                waits=[(chGW, n)], inc=chCC)
            nn = chCC.n
            P.op("gpsimd", lambda e, idx=idx: e.dma_start(out=cco[:, :], in_=cc_out[idx][:, :]),
                 waits=[(chCC, nn)], inc=chGW, inck=16)
            return chGW.n

        def scale_math(col, s_idx, r_idx, dep):
            P.op("vector", lambda e, col=col: e.tensor_scalar(
                ccs[0:1, 8:9], cco[0:1, col:col + 1], 1.0 / QMAX, 1e-12, ALU.mult, ALU.max),
                waits=[(chGW, dep)], inc=chDVE)
            P.op("vector", lambda e: e.reciprocal(ccs[0:1, 9:10], ccs[0:1, 8:9]),
                 waits=[(chDVE, chDVE.n)], inc=chDVE)
            n1 = chDVE.n
            P.op("tensor", lambda e: e.matmul(psG[:, 0:1], onef[0:1, :], ccs[0:1, 8:9],
                                              start=True, stop=True),
                 waits=[(chDVE, n1)], inc=chPE)
            P.op("tensor", lambda e: e.matmul(psG[:, 1:2], onef[0:1, :], ccs[0:1, 9:10],
                                              start=True, stop=True), inc=chPE)
            n2 = chPE.n
            P.op("vector", lambda e, s_idx=s_idx: e.tensor_copy(SCL(s_idx), psG[:, 0:1]),
                 waits=[(chPE, n2)], inc=chDVE)
            P.op("vector", lambda e, r_idx=r_idx: e.tensor_copy(SCL(r_idx), psG[:, 1:2]), inc=chDVE)
            return chDVE.n

        mx_done = part_max_to_ccs(sMX[:], 0, [(chDVE, xnT_ready)])
        c1 = collective(0, chDVE, mx_done)
        sx_ready = scale_math(0, I_SX, I_RX, c1)

        P.op("vector", lambda e: e.tensor_scalar(sA[:], sR[:], SCL(I_RX), None, ALU.mult),
             waits=[(chDVE, sx_ready)], inc=chDVE)
        P.op("vector", lambda e: e.tensor_scalar(sBt[:], sNB[:], SCL(I_RX), -1.0, ALU.mult, ALU.mult),
             inc=chDVE)
        xq_scales = chDVE.n

        xq_done = {}
        for t in range(NT):
            P.op("sync", lambda e, t=t: e.dma_start(out=xslot(t), in_=x_h[t * 128:(t + 1) * 128, :]),
                 waits=([(chDVE, xq_done[t - 4])] if t >= 4 else [(chDVE, xnT_ready)]),
                 inc=chL, inck=16)
            ldn = chL.n
            dst = scbh(t % 2)
            ftmp = wfslot(t % 2)
            P.op("vector", lambda e, t=t, ftmp=ftmp: e.tensor_scalar(
                ftmp, xslot(t), sA[:, t:t + 1], sBt[:, t:t + 1], ALU.mult, ALU.subtract),
                waits=[(chL, ldn), (chDVE, xq_scales), (chDVE, xq_done.get(t - 2, 0))],
                inc=chDVE)
            P.op("vector", lambda e, ftmp=ftmp: e.tensor_scalar(
                ftmp, ftmp, MAGIC, MAGIC, ALU.add, ALU.subtract),
                waits=[(chDVE, chDVE.n)], inc=chDVE)
            P.op("vector", lambda e, dst=dst, ftmp=ftmp: e.tensor_scalar(
                dst, ftmp, SCL(I_SX), None, ALU.mult),
                waits=[(chDVE, chDVE.n)], inc=chDVE)
            dve_n = chDVE.n
            transpose_tile_to_arena(t, dst, arB, [(chDVE, dve_n)])
            xq_done[t] = chDVE.n
        xqT_ready = chDVE.n
        if DEBUG:
            P.op("sync", lambda e: e.dma_start(out=d_xqT[:, :], in_=arB[:]),
                 waits=[(chDVE, xqT_ready)], inc=chDBG, inck=16)

        WCFG = {"up": dict(wh=wup_h, wsl=(0, 1), wtsl=(2, 3), nat=0, dq=0),
                "gt": dict(wh=wgt_h, wsl=(2, 3), wtsl=(4, 5), nat=1, dq=1)}
        dq_free = {}
        nat_free = {}
        wt_free = {}
        ev_free = {}
        dqs_free = {}
        bank_ev = {0: 0, 1: 0}

        for c in range(NCF):
            for wi, wn in enumerate(("up", "gt")):
                cfg = WCFG[wn]
                par = c % 2
                fw = wfslot(cfg["wsl"][par])
                nat = wbslot(cfg["nat"])
                wt = wbslot(cfg["wtsl"][par])
                sq = dqs[:, (2 * cfg["dq"]) * 32:(2 * cfg["dq"] + 1) * 32]
                rq = dqs[:, (2 * cfg["dq"] + 1) * 32:(2 * cfg["dq"] + 2) * 32]

                P.op("sync", lambda e, c=c, wh=cfg["wh"], fw=fw: e.dma_start(
                    out=fw, in_=wh[c * 128:(c + 1) * 128, :]),
                    waits=[(chGP, dq_free.get((wn, c - 2), 0)),
                           (chDVE, xqT_ready)],
                    inc=chL, inck=16)
                ldn = chL.n

                P.op("vector", lambda e, fw=fw, sq=sq: e.tensor_reduce(
                    sq, fw.rearrange("p (b w) -> p b w", w=32), AX.X, ALU.max,
                    apply_absolute_value=True),
                    waits=[(chL, ldn), (chGP, dqs_free.get(wn, 0))], inc=chDVE)
                P.op("vector", lambda e, sq=sq: e.tensor_scalar(
                    sq, sq, 1.0 / 7.0, 1e-12, ALU.mult, ALU.max),
                    waits=[(chDVE, chDVE.n)], inc=chDVE)
                P.op("vector", lambda e, sq=sq, rq=rq: e.reciprocal(rq, sq),
                    waits=[(chDVE, chDVE.n)], inc=chDVE)
                n_sr = chDVE.n
                P.op("gpsimd", lambda e, fw=fw, rq=rq: e.tensor_tensor(
                    fw.rearrange("p (b w) -> p b w", w=32),
                    fw.rearrange("p (b w) -> p b w", w=32),
                    rq.unsqueeze(2).broadcast_to([128, 32, 32]), ALU.mult),
                    waits=[(chDVE, n_sr)], inc=chGP)
                n_gp1 = chGP.n
                P.op("vector", lambda e, fw=fw: e.tensor_scalar(
                    fw, fw, MAGIC, MAGIC, ALU.add, ALU.subtract),
                    waits=[(chGP, n_gp1)], inc=chDVE)
                n_rnd = chDVE.n
                P.op("gpsimd", lambda e, fw=fw, nat=nat, sq=sq: e.tensor_tensor(
                    nat.rearrange("p (b w) -> p b w", w=32),
                    fw.rearrange("p (b w) -> p b w", w=32),
                    sq.unsqueeze(2).broadcast_to([128, 32, 32]), ALU.mult),
                    waits=[(chDVE, n_rnd), (chPE, nat_free.get(wn, 0))], inc=chGP)
                dq_free[(wn, c)] = chGP.n
                dqs_free[wn] = chGP.n
                n_nat = chGP.n

                for g in range(2):
                    tpar = trS["i"] % 2
                    pdst = psE if tpar == 0 else psF
                    w = [(chGP, n_nat), (chDVE, trS["ev"].get(tpar, 0)),
                         (chPE, wt_free.get((wn, par), 0))]
                    for i in range(4):
                        dc = g * 4 + i
                        P.op("tensor", lambda e, i=i, dc=dc, pdst=pdst, nat=nat: e.transpose(
                            pdst[:, i * 128:(i + 1) * 128], nat[:, dc * 128:(dc + 1) * 128],
                            idb[:]), waits=(w if i == 0 else []), inc=chPE)
                    trn = chPE.n
                    if g == 1:
                        nat_free[wn] = trn
                    P.op("vector", lambda e, pdst=pdst, wt=wt, g=g: e.tensor_copy(
                        wt[:, g * 512:(g + 1) * 512], pdst[:, 0:512]),
                        waits=[(chPE, trn)], inc=chDVE)
                    trS["ev"][tpar] = chDVE.n
                    trS["i"] += 1
                n_wt = chDVE.n

                ev = scbs(cfg["nat"])
                for p in range(4):
                    bkey = p % 2
                    bank = psA if bkey == 0 else psB
                    for dc in range(ND):
                        P.op("tensor", lambda e, p=p, dc=dc, bank=bank, wt=wt: e.matmul(
                            bank[:], wt[:, dc * 128:(dc + 1) * 128],
                            arB[:, dc * T + p * 512:dc * T + (p + 1) * 512],
                            start=(dc == 0), stop=(dc == ND - 1)),
                            waits=([(chDVE, n_wt), (chACT, bank_ev[bkey])]
                                   if dc == 0 else []), inc=chPE)
                    mmn = chPE.n
                    P.op("scalar", lambda e, p=p, bank=bank, ev=ev: e.activation(
                        ev[:, p * 512:(p + 1) * 512], bank[:], AF.Identity,
                        bias=C_ZERO, scale=1.0),
                        waits=[(chPE, mmn), (chSW, ev_free.get(wn, 0))], inc=chACT)
                    bank_ev[bkey] = chACT.n
                wt_free[(wn, par)] = chPE.n
                evn = chACT.n

                P.op("vector", lambda e, ev=ev, c=c, wi=wi: e.tensor_reduce(
                    pmx[:, wi * 24 + c:wi * 24 + c + 1], ev, AX.XY, ALU.max,
                    apply_absolute_value=True),
                    waits=[(chACT, evn)], inc=chDVE)
                P.op("scalar", lambda e, ev=ev, c=c, wn=wn: e.dma_start(
                    out=scr[wn][c, :, :], in_=ev),
                    waits=[(chACT, evn)], inc=chSW, inck=16)
                ev_free[wn] = chSW.n

        p1_gp_done = chGP.n
        p1_sw_done = chSW.n
        m_up = part_max_to_ccs(pmx[:, 0:24], 0)
        m_gt = part_max_to_ccs(pmx[:, 24:48], 1)
        c2 = collective(1, chDVE, m_gt)
        s_up = scale_math(0, I_SUP, I_RUP, c2)
        s_gt = scale_math(1, I_SGT, I_RGT, s_up)

        attw_free = {}
        qlr_free = {}
        vhat_free = {}
        lsr_ev = {}
        sc_used = {}
        yevac_done = [0]
        ynorm_done = [0]
        wload = {}

        def prefetch_attn(a):
            par = a % 2
            extra = []
            if a == 0:
                extra = [(chPE, nat_free["up"]), (chPE, wt_free[("up", 0)]),
                         (chPE, wt_free[("gt", 0)])]
            elif a == 1:
                extra = [(chPE, nat_free["gt"]), (chPE, wt_free[("up", 1)]),
                         (chPE, wt_free[("gt", 1)])]
            for wi, wh in enumerate((wq_h, wk_h, wv_h)):
                P.op("gpsimd", lambda e, wh=wh, wi=wi, par=par, a=a: e.dma_start(
                    out=wbslot(2 * wi + par).rearrange("p (c o) -> p c o", o=128),
                    in_=wh[a].rearrange("(c p) o -> p c o", p=128)),
                    waits=extra + [(chPE, attw_free.get((wi, par), 0)),
                                   (chGP, p1_gp_done)],
                    inc=chGW, inck=16)
            wload[a] = chGW.n

        prefetch_attn(0)
        prefetch_attn(1)

        p2_infree = {}
        p2_outfree = {0: p1_sw_done, 1: p1_sw_done}
        p2_sig_done = {}
        p3_infree = {}
        p3_outfree = {}
        s_g1 = [0]
        s_o = [0]
        p3_seed = [0]

        def emit_p2_chunk(c):
            par = c % 2
            inb = wfb[:, par * 2048:(par + 1) * 2048]
            gate0 = wf32[:, 2 * 1024:4 * 1024]
            sig = wf32[:, 4 * 1024:6 * 1024]
            outb = scbs(par)
            P.op("scalar", lambda e, c=c, inb=inb: e.dma_start(out=inb, in_=scr["gt"][c, :, :]),
                 waits=[(chSW, p1_sw_done), (chDVE, p2_infree.get(par, 0)),
                        (chGP, p1_gp_done)],
                 inc=chSR, inck=16)
            ldn = chSR.n
            P.op("vector", lambda e, inb=inb, gate0=gate0: e.tensor_scalar(
                gate0, inb, SCL(I_RGT), MAGIC, ALU.mult, ALU.add),
                waits=[(chSR, ldn), (chDVE, s_gt), (chACT, p2_sig_done.get(c - 1, 0))],
                inc=chDVE)
            P.op("vector", lambda e, gate0=gate0: e.tensor_scalar(
                gate0, gate0, MAGIC, SCL(I_SGT), ALU.subtract, ALU.mult),
                waits=[(chDVE, chDVE.n)], inc=chDVE)
            p2_infree[par] = chDVE.n
            n_g0 = chDVE.n
            P.op("scalar", lambda e, gate0=gate0, sig=sig: e.activation(
                sig, gate0, AF.Tanh, bias=C_ZERO, scale=0.5),
                waits=[(chDVE, n_g0)], inc=chACT)
            p2_sig_done[c] = chACT.n
            P.op("vector", lambda e, sig=sig: e.tensor_scalar(
                sig, sig, 32768.0, MAGIC + 32768.0, ALU.mult, ALU.add),
                waits=[(chACT, p2_sig_done[c])], inc=chDVE)
            P.op("vector", lambda e, sig=sig: e.tensor_scalar(
                sig, sig, MAGIC, 65535.0, ALU.subtract, ALU.min),
                waits=[(chDVE, chDVE.n)], inc=chDVE)
            P.op("vector", lambda e, sig=sig: e.tensor_scalar(
                sig, sig, 1.0 / 65536.0, None, ALU.mult),
                waits=[(chDVE, chDVE.n)], inc=chDVE)
            P.op("vector", lambda e, gate0=gate0, sig=sig, outb=outb: e.tensor_tensor(
                outb, gate0, sig, ALU.mult),
                waits=[(chSW, p2_outfree.get(par, 0)), (chDVE, chDVE.n)], inc=chDVE)
            P.op("vector", lambda e, outb=outb, c=c: e.tensor_reduce(
                pmx[:, 48 + c:48 + c + 1], outb, AX.XY, ALU.max,
                apply_absolute_value=True),
                waits=[(chDVE, chDVE.n)], inc=chDVE)
            n_out = chDVE.n
            P.op("scalar", lambda e, outb=outb, c=c: e.dma_start(
                out=scr["g1"][c, :, :], in_=outb),
                waits=[(chDVE, n_out)], inc=chSW, inck=16)
            p2_outfree[par] = chSW.n

        def emit_p3_chunk(c, p2_dve_done):
            par = c % 2
            ing = wfb[:, par * 2048:(par + 1) * 2048]
            inu = wfb[:, 4096 + par * 2048:4096 + (par + 1) * 2048]
            g1 = wf32[:, 4 * 1024:6 * 1024]
            up = wf32[:, 6 * 1024:8 * 1024]
            outb = scbs(par)
            P.op("scalar", lambda e, c=c, ing=ing: e.dma_start(out=ing, in_=scr["g1"][c, :, :]),
                 waits=[(chSW, chSW.n), (chDVE, p3_infree.get(par, p2_dve_done))],
                 inc=chSR, inck=16)
            P.op("scalar", lambda e, c=c, inu=inu: e.dma_start(out=inu, in_=scr["up"][c, :, :]),
                 inc=chSR, inck=16)
            ldn = chSR.n
            P.op("vector", lambda e, ing=ing, g1=g1: e.tensor_scalar(
                g1, ing, SCL(I_RG1), MAGIC, ALU.mult, ALU.add),
                waits=[(chSR, ldn), (chDVE, s_g1[0])], inc=chDVE)
            P.op("vector", lambda e, g1=g1: e.tensor_scalar(
                g1, g1, MAGIC, SCL(I_SG1), ALU.subtract, ALU.mult),
                waits=[(chDVE, chDVE.n)], inc=chDVE)
            P.op("vector", lambda e, inu=inu, up=up: e.tensor_scalar(
                up, inu, SCL(I_RUP), MAGIC, ALU.mult, ALU.add), inc=chDVE)
            P.op("vector", lambda e, up=up: e.tensor_scalar(
                up, up, MAGIC, SCL(I_SUP), ALU.subtract, ALU.mult),
                waits=[(chDVE, chDVE.n)], inc=chDVE)
            p3_infree[par] = chDVE.n
            P.op("vector", lambda e, g1=g1, up=up, outb=outb: e.tensor_tensor(
                outb, g1, up, ALU.mult),
                waits=[(chSW, p3_outfree.get(par, 0)), (chDVE, chDVE.n)], inc=chDVE)
            P.op("vector", lambda e, outb=outb, c=c: e.tensor_reduce(
                pmx[:, 72 + c:72 + c + 1], outb, AX.XY, ALU.max,
                apply_absolute_value=True),
                waits=[(chDVE, chDVE.n)], inc=chDVE)
            n_out = chDVE.n
            P.op("scalar", lambda e, outb=outb, c=c: e.dma_start(
                out=scr["o"][c, :, :], in_=outb),
                waits=[(chDVE, n_out)], inc=chSW, inck=16)
            p3_outfree[par] = chSW.n

        sc_banks = [psA[:], psB[:], psD[:], psE[:].bitcast(F32)]
        bc_free = {0: chDVE.n, 1: chDVE.n}
        for a in range(8):
            par = a % 2
            vh_ready = 0
            for wi, wn in enumerate(("q", "k", "v")):
                wt = wbslot(2 * wi + par)
                for p in range(4):
                    bkey = p % 2
                    bank = psA if bkey == 0 else psB
                    for dc in range(ND):
                        P.op("tensor", lambda e, p=p, dc=dc, bank=bank, wt=wt: e.matmul(
                            bank[:], wt[:, dc * 128:(dc + 1) * 128],
                            arA[:, dc * T + p * 512:dc * T + (p + 1) * 512],
                            start=(dc == 0), stop=(dc == ND - 1)),
                            waits=([(chGW, wload[a]), (chACT, bank_ev[bkey])]
                                   if dc == 0 else []), inc=chPE)
                    mmn = chPE.n
                    P.op("scalar", lambda e, p=p, bank=bank, wi=wi, a=a: e.activation(
                        qkscr[:, p * 512:(p + 1) * 512], bank[:], AF.Identity,
                        bias=bias4[:, wi * 8 + a:wi * 8 + a + 1], scale=1.0),
                        waits=[(chPE, mmn)], inc=chACT)
                    bank_ev[bkey] = chACT.n
                attw_free[(wi, par)] = chPE.n
                evn = chACT.n

                if wn in ("q", "k"):
                    lsr_w = lsq if wn == "q" else lsk
                    dstb = qlrb if wn == "q" else klrb
                    for p in range(4):
                        P.op("tensor", lambda e, p=p, a=a, lsr_w=lsr_w: e.matmul(
                            psD[0:64, :], lsr_w[:, a * 64:(a + 1) * 64],
                            qkscr[:, p * 512:(p + 1) * 512], start=True, stop=True),
                            waits=[(chACT, evn), (chGW, lsr_ready),
                                   (chACT, lsr_ev.get("any", 0)),
                                   (chACT, sc_bank_ev.get(2, 0)),
                                   (chPE, qlr_free.get(0, 0))], inc=chPE)
                        mmn = chPE.n
                        P.op("scalar", lambda e, p=p, dstb=dstb: e.activation(
                            dstb[:, p * 512:(p + 1) * 512],
                            psD[0:64, :], AF.Identity, bias=C_ZERO[0:64, :], scale=1.0),
                            waits=[(chPE, mmn)], inc=chACT)
                        lsr_ev["any"] = chACT.n
                else:
                    for g in range(4):
                        tpar = trS["i"] % 2
                        pdst = psE if tpar == 0 else psF
                        w = [(chACT, evn), (chDVE, trS["ev"].get(tpar, 0)),
                             (chPE, vhat_free.get(0, 0)),
                             (chACT, sc_bank_ev.get(3, 0) if tpar == 0 else
                              av_evac.get(1, 0)),
                             (chDVE, av_free.get(1, 0) if tpar == 1 else 0)]
                        for i in range(4):
                            tt = g * 4 + i
                            P.op("tensor", lambda e, i=i, tt=tt, pdst=pdst: e.transpose(
                                pdst[:, i * 128:(i + 1) * 128],
                                qkscr[:, tt * 128:(tt + 1) * 128], idb[:]),
                                waits=(w if i == 0 else []), inc=chPE)
                        trn = chPE.n
                        for half in range(2):
                            P.op("vector", lambda e, pdst=pdst, g=g, half=half:
                                 e.tensor_copy(
                                     vhat[:].rearrange("p (b w) -> p b w", w=130)
                                     [:, g * 4:(g + 1) * 4, half * 65:half * 65 + 64],
                                     pdst[:, 0:512].rearrange("p (a w) -> p a w", w=128)
                                     [:, :, half * 64:(half + 1) * 64]),
                                 waits=[(chPE, trn), (chGP, lsr_memset)], inc=chDVE)
                        trS["ev"][tpar] = chDVE.n
                        trS["i"] += 1
                    vh_ready = chDVE.n

            if a + 2 < 8:
                prefetch_attn(a + 2)
            qln = chACT.n
            for hh in range(2):
                qsl = qlrb[hh * 32:(hh + 1) * 32, :]
                ksl = klrb[hh * 32:(hh + 1) * 32, :]
                for ic in range(4):
                    njt = 4 * ic + 4
                    acc_i = (hh * 4 + ic) % 2
                    acc = psC[0:65, :] if acc_i == 0 else psF[:].bitcast(F32)[0:65, :]
                    mwaits = {}
                    for step in range(njt + 2):
                        if step < njt:
                            jt = step
                            bk = jt % 4
                            bank = sc_banks[bk]
                            P.op("tensor", lambda e, jt=jt, ic=ic, bank=bank, qsl=qsl,
                                 ksl=ksl: e.matmul(
                                     bank, ksl[:, jt * 128:(jt + 1) * 128],
                                     qsl[:, ic * 512:(ic + 1) * 512],
                                     start=True, stop=True),
                                 waits=[(chACT, qln), (chACT, sc_bank_ev.get(bk, 0)),
                                        (chACT, bank_ev[bk] if bk < 2 else 0)],
                                 inc=chPE)
                            mmn = chPE.n
                            P.op("scalar", lambda e, jt=jt, bank=bank: e.activation(
                                pslot(jt), bank, AF.Exp, bias=C_ZERO, scale=C_RSQ32),
                                waits=[(chPE, mmn), (chPE, sc_used.get(jt % 4, 0))],
                                inc=chACT)
                            sc_bank_ev[bk] = chACT.n
                            if bk < 2:
                                bank_ev[bk] = chACT.n
                            expn = chACT.n
                            mwaits[jt] = [(chACT, expn)]
                            if jt >= 4 * ic:
                                kk = jt - 4 * ic
                                P.op("gpsimd", lambda e, jt=jt, kk=kk: e.tensor_tensor(
                                    pslot(jt), pslot(jt),
                                    mask[:, 512 - 128 * kk:1024 - 128 * kk], ALU.mult),
                                    waits=[(chACT, expn)], inc=chGP)
                                mwaits[jt] = [(chGP, chGP.n)]
                        if step >= 2:
                            jt = step - 2
                            P.op("tensor", lambda e, jt=jt, hh=hh, njt=njt, acc=acc: e.matmul(
                                acc, vhat[:, jt * 130 + hh * 65:jt * 130 + hh * 65 + 65],
                                pslot(jt), start=(jt == 0), stop=(jt == njt - 1)),
                                waits=mwaits[jt] + ([(chDVE, vh_ready),
                                                     (chDVE, av_free.get(acc_i, 0)),
                                                     (chACT, av_evac.get(acc_i, 0))]
                                                    if jt == 0 else []),
                                inc=chPE)
                            sc_used[jt % 4] = chPE.n
                    avn = chPE.n
                    bc_i = (hh * 4 + ic) % 2
                    bcb = psH if bc_i == 0 else psG
                    P.op("vector", lambda e, acc=acc: e.reciprocal(rcp[0:1, :], acc[64:65, :]),
                         waits=[(chPE, avn)], inc=chDVE)
                    rn = chDVE.n
                    P.op("tensor", lambda e, bcb=bcb: e.matmul(
                        bcb[0:64, :], onef[0:1, 0:64], rcp[0:1, :], start=True, stop=True),
                         waits=[(chDVE, rn), (chDVE, bc_free.get(bc_i, 0))], inc=chPE)
                    bn = chPE.n
                    P.op("scalar", lambda e, acc=acc: e.activation(
                        ybf[:], acc[0:64, :], AF.Identity,
                        bias=C_ZERO[0:64, :], scale=1.0),
                         waits=[(chPE, bn), (chDVE, ynorm_done[0])], inc=chACT)
                    av_evac[acc_i] = chACT.n
                    yn = chACT.n
                    P.op("vector", lambda e, a=a, hh=hh, ic=ic, bcb=bcb: e.tensor_tensor(
                        arC[hh * 64:(hh + 1) * 64, a * T + ic * 512:a * T + (ic + 1) * 512],
                        ybf[:], bcb[0:64, :], ALU.mult),
                        waits=[(chACT, yn)], inc=chDVE)
                    ynorm_done[0] = chDVE.n
                    av_free[acc_i] = chDVE.n
                    bc_free[bc_i] = chDVE.n
            qlr_free[0] = chPE.n
            vhat_free[0] = chPE.n

            if a in (2, 3, 4):
                for c in range((a - 2) * 8, (a - 1) * 8):
                    emit_p2_chunk(c)
                if a == 4:
                    m_g1 = part_max_to_ccs(pmx[:, 48:72], 0)
                    c3 = collective(2, chDVE, m_g1)
                    s_g1[0] = scale_math(0, I_SG1, I_RG1, c3)
            if a in (5, 6, 7):
                if a == 5:
                    p3_seed[0] = chDVE.n
                for c in range((a - 5) * 8, (a - 4) * 8):
                    emit_p3_chunk(c, p3_seed[0])
                if a == 7:
                    m_o = part_max_to_ccs(pmx[:, 72:96], 0)
                    c4 = collective(3, chDVE, m_o)
                    s_o[0] = scale_math(0, I_SO, I_RO, c4)

        qkv_all_done = chPE.n

        if DEBUG:
            P.op("sync", lambda e: e.dma_start(out=d_yall[:, :], in_=arC[:]),
                 waits=[(chDVE, chDVE.n)], inc=chDBG, inck=16)
            P.op("sync", lambda e: e.dma_start(out=d_scl[:, :], in_=scl[:]),
                 waits=[(chDVE, chDVE.n)], inc=chDBG, inck=16)
        P.op("sync", lambda e: e.dma_start(
            out=arAf.rearrange("p (t w) -> p t w", w=1024),
            in_=x_h[0:1024, :].rearrange("(t p) d -> p t d", p=128)),
            waits=[(chPE, qkv_all_done), (chDBG, chDBG.n)], inc=chACC, inck=16)
        P.op("sync", lambda e: e.dma_start(
            out=arBf.rearrange("p (t w) -> p t w", w=1024),
            in_=x_h[1024:2048, :].rearrange("(t p) d -> p t d", p=128)),
            inc=chACC, inck=16)
        acc_ld = chACC.n

        op_wfree = {}
        trf_ev = {0: chDVE.n, 1: chDVE.n}
        trf_add = [0]

        def evac_transpose_add(bank, bias_ap, gp, oc, mm_snap):
            P.op("scalar", lambda e, bank=bank, bias_ap=bias_ap: e.activation(
                xstage[:, 3 * 1024:3 * 1024 + 512], bank[:], AF.Identity,
                bias=bias_ap, scale=1.0),
                waits=[(chPE, mm_snap), (chDVE, trf_add[0])], inc=chACT)
            evn = chACT.n
            gpar = gp % 2
            pdst = psG if gpar == 0 else psH
            for i in range(4):
                P.op("tensor", lambda e, i=i, pdst=pdst: e.transpose(
                    pdst[:, i * 128:(i + 1) * 128],
                    xstage[:, 3 * 1024 + i * 128:3 * 1024 + (i + 1) * 128], idf[:]),
                    waits=([(chACT, evn), (chDVE, trf_ev[gpar])] if i == 0 else []),
                    inc=chPE)
            trn = chPE.n
            accv = (arAf if gp < 2 else arBf).rearrange("p (t w) -> p t w", w=1024)[
                :, (gp % 2) * 4:(gp % 2) * 4 + 4, oc * 128:(oc + 1) * 128]
            P.op("vector", lambda e, accv=accv, pdst=pdst: e.tensor_tensor(
                accv, accv, pdst[:].rearrange("p (a w) -> p a w", w=128), ALU.add),
                waits=[(chPE, trn), (chACC, acc_ld)], inc=chDVE)
            trf_ev[gpar] = chDVE.n
            trf_add[0] = chDVE.n

        for oc in range(ND):
            par = oc % 2
            P.op("gpsimd", lambda e, oc=oc, par=par: e.dma_start(
                out=wbslot(par).rearrange("p (c o) -> p c o", o=128),
                in_=wo_h[oc].rearrange("(c p) o -> p c o", p=128)),
                waits=[(chPE, attw_free.get((0, par), 0)),
                       (chPE, op_wfree.get(par, 0))], inc=chGW, inck=16)
            ldn = chGW.n
            for p in range(4):
                bkey = p % 2
                bank = psA if bkey == 0 else psB
                for dc in range(ND):
                    P.op("tensor", lambda e, p=p, dc=dc, bank=bank, par=par: e.matmul(
                        bank[:], wbslot(par)[:, dc * 128:(dc + 1) * 128],
                        arC[:, dc * T + p * 512:dc * T + (p + 1) * 512],
                        start=(dc == 0), stop=(dc == ND - 1)),
                        waits=([(chGW, ldn), (chACT, bank_ev[bkey]),
                                (chDVE, ynorm_done[0])] if dc == 0 else []), inc=chPE)
                mmn = chPE.n
                evac_transpose_add(bank, bias4[:, 24 + oc:24 + oc + 1], p, oc, mmn)
                bank_ev[bkey] = chACT.n
            op_wfree[par] = chPE.n
        oproj_pe_done = chPE.n
        oproj_done = chDVE.n

        def oT_view(ic):
            if ic < 16:
                return arC[:, ic * 1024:(ic + 1) * 1024]
            if ic < 18:
                return vhat[:, (ic - 16) * 1024:(ic - 15) * 1024]
            if ic < 20:
                return qkscr[:, (ic - 18) * 1024:(ic - 17) * 1024]
            return scbh(2 + (ic - 20))

        p3_dve_done = chDVE.n
        dn_nat_free = {}
        dn_dqs_free = [0]
        dn_wt_used = {}
        oT_used = {}
        sr_free = {}
        p4_ld_free = [0]
        dn_cached = {}

        for H in range(2):
            for ic in range(NCF):
                par = ic % 2
                inb = scbh(par)
                P.op("scalar", lambda e, ic=ic, H=H, inb=inb: e.dma_start(
                    out=inb, in_=scr["o"][ic, :, H * 1024:(H + 1) * 1024]),
                    waits=[(chSW, chSW.n), (chDVE, sr_free.get(par, 0)),
                           (chPE, oproj_pe_done)],
                    inc=chSR, inck=16)
                ldn = chSR.n
                ft = wfslot(7)
                P.op("vector", lambda e, inb=inb, ft=ft: e.tensor_scalar(
                    ft, inb, SCL(I_RO), MAGIC, ALU.mult, ALU.add),
                    waits=[(chSR, ldn), (chDVE, s_o[0]), (chPE, oproj_pe_done),
                           (chPE, oT_used.get(ic, 0)), (chDBG, chDBG.n)], inc=chDVE)
                P.op("vector", lambda e, ic=ic, ft=ft: e.tensor_scalar(
                    oT_view(ic), ft, MAGIC, SCL(I_SO), ALU.subtract, ALU.mult),
                    waits=[(chSW, chSW.n), (chDVE, chDVE.n)], inc=chDVE)
                sr_free[par] = chDVE.n
            oT_ready = chDVE.n

            for oc in range(ND):
                par = oc % 2
                wdnT = wbf[:, par * 3072:(par + 1) * 3072]
                if oc not in dn_cached:
                    P.op("sync", lambda e, oc=oc: e.dma_start(
                        out=wf32[:, 0:3 * 1024].rearrange("p (a w) -> p a w", w=1024),
                        in_=wdn_h[oc * 128:(oc + 1) * 128, :].rearrange(
                            "p (a w) -> p a w", w=1024)),
                        waits=[(chDVE, p3_dve_done), (chGP, p4_ld_free[0]),
                               (chPE, dn_wt_used.get(par, 0))],
                        inc=chL, inck=16)
                    ldn = chL.n
                    sq = dqs[:, 0:32]
                    rq = dqs[:, 32:64]
                    for pc in range(3):
                        fw = wfslot(pc)
                        nat = xsb[:, (pc % 2) * 1024:((pc % 2) + 1) * 1024]
                        P.op("vector", lambda e, fw=fw, sq=sq: e.tensor_reduce(
                            sq, fw.rearrange("p (b w) -> p b w", w=32), AX.X, ALU.max,
                            apply_absolute_value=True),
                            waits=[(chL, ldn), (chGP, dn_dqs_free[0]),
                                   (chACT, chACT.n if (H == 0 and oc == 0 and pc == 0) else 0)],
                            inc=chDVE)
                        P.op("vector", lambda e, sq=sq: e.tensor_scalar(
                            sq, sq, 1.0 / 7.0, 1e-12, ALU.mult, ALU.max),
                            waits=[(chDVE, chDVE.n)], inc=chDVE)
                        P.op("vector", lambda e, sq=sq, rq=rq: e.reciprocal(rq, sq),
                            waits=[(chDVE, chDVE.n)], inc=chDVE)
                        n_sr2 = chDVE.n
                        P.op("gpsimd", lambda e, fw=fw, rq=rq: e.tensor_tensor(
                            fw.rearrange("p (b w) -> p b w", w=32),
                            fw.rearrange("p (b w) -> p b w", w=32),
                            rq.unsqueeze(2).broadcast_to([128, 32, 32]), ALU.mult),
                            waits=[(chDVE, n_sr2)], inc=chGP)
                        n_gp2 = chGP.n
                        P.op("vector", lambda e, fw=fw: e.tensor_scalar(
                            fw, fw, MAGIC, MAGIC, ALU.add, ALU.subtract),
                            waits=[(chGP, n_gp2)], inc=chDVE)
                        n_rnd2 = chDVE.n
                        P.op("gpsimd", lambda e, fw=fw, nat=nat, sq=sq: e.tensor_tensor(
                            nat.rearrange("p (b w) -> p b w", w=32),
                            fw.rearrange("p (b w) -> p b w", w=32),
                            sq.unsqueeze(2).broadcast_to([128, 32, 32]), ALU.mult),
                            waits=[(chDVE, n_rnd2), (chPE, dn_nat_free.get(pc % 2, 0))],
                            inc=chGP)
                        dn_dqs_free[0] = chGP.n
                        n_nat2 = chGP.n
                        if pc == 2:
                            p4_ld_free[0] = chGP.n
                        for g in range(2):
                            tpar = trS["i"] % 2
                            pdst = psE if tpar == 0 else psF
                            w = [(chGP, n_nat2), (chDVE, trS["ev"].get(tpar, 0)),
                                 (chPE, dn_wt_used.get(par, 0))]
                            for i in range(4):
                                dc8 = g * 4 + i
                                P.op("tensor", lambda e, i=i, dc8=dc8, pdst=pdst, nat=nat:
                                     e.transpose(pdst[:, i * 128:(i + 1) * 128],
                                                 nat[:, dc8 * 128:(dc8 + 1) * 128], idb[:]),
                                     waits=(w if i == 0 else []), inc=chPE)
                            trn = chPE.n
                            dn_nat_free[pc % 2] = trn
                            P.op("vector", lambda e, pdst=pdst, wdnT=wdnT, pc=pc, g=g:
                                 e.tensor_copy(
                                     wdnT[:, pc * 1024 + g * 512:pc * 1024 + (g + 1) * 512],
                                     pdst[:, 0:512]),
                                 waits=[(chPE, trn)], inc=chDVE)
                            trS["ev"][tpar] = chDVE.n
                            trS["i"] += 1
                    wt_ready_dve = chDVE.n
                    P.op("sync", lambda e, oc=oc, wdnT=wdnT: e.dma_start(
                        out=dnT_h[oc, :, :], in_=wdnT),
                        waits=[(chDVE, wt_ready_dve)], inc=chL, inck=16)
                    dn_cached[oc] = (wt_ready_dve, chL.n)
                    wt_dep = [(chDVE, wt_ready_dve)]
                else:
                    P.op("sync", lambda e, oc=oc, wdnT=wdnT: e.dma_start(
                        out=wdnT, in_=dnT_h[oc, :, :]),
                        waits=[(chL, dn_cached[oc][1]), (chPE, dn_wt_used.get(par, 0))],
                        inc=chL, inck=16)
                    wt_dep = [(chL, chL.n)]

                for ph in range(2):
                    bkey = ph % 2
                    bank = psA if bkey == 0 else psB
                    for ic in range(NCF):
                        P.op("tensor", lambda e, ph=ph, ic=ic, bank=bank, wdnT=wdnT:
                             e.matmul(bank[:], wdnT[:, ic * 128:(ic + 1) * 128],
                                      oT_view(ic)[:, ph * 512:(ph + 1) * 512],
                                      start=(ic == 0), stop=(ic == NCF - 1)),
                             waits=(wt_dep + [(chDVE, oT_ready), (chACT, bank_ev[bkey])]
                                    if ic == 0 else []), inc=chPE)
                    mmn = chPE.n
                    dn_wt_used[par] = mmn
                    for ic2 in range(NCF):
                        oT_used[ic2] = mmn
                    evac_transpose_add(bank, C_ZERO, H * 2 + ph, oc, mmn)
                    bank_ev[bkey] = chACT.n

        ffn_done = chDVE.n

        P.op("sync", lambda e: e.dma_start(
            out=out_h[0:1024, :].rearrange("(t p) d -> p t d", p=128),
            in_=arAf.rearrange("p (t w) -> p t w", w=1024)),
            waits=[(chDVE, ffn_done)], inc=chOUT, inck=16)
        P.op("sync", lambda e: e.dma_start(
            out=out_h[1024:2048, :].rearrange("(t p) d -> p t d", p=128),
            in_=arBf.rearrange("p (t w) -> p t w", w=1024)),
            inc=chOUT, inck=16)
        P.op("sync", lambda e: e.nop(), waits=[(chOUT, chOUT.n), (chDBG, chDBG.n)])

        for c in P.chans:
            c.sem = ctx.enter_context(nc.semaphore(c.name))
        block = ctx.enter_context(nc.Block())
        P.emit(block)

    return nc


def kernel(L0_attn_x, L0_attn_Wq, L0_attn_bq, L0_attn_Wk, L0_attn_bk,
           L0_attn_Wv, L0_attn_bv, L0_attn_Wo, L0_attn_bo,
           L0_attn_Wq_lsr, L0_attn_Wk_lsr,
           L0_ffn_w_gate, L0_ffn_w_up, L0_ffn_w_down):
    f32 = np.float32
    x = np.ascontiguousarray(np.asarray(L0_attn_x, dtype=f32))

    def wt_chunks(W):
        WT = np.ascontiguousarray(np.asarray(W, dtype=f32).T)
        return np.ascontiguousarray(WT.reshape(D, ND, 128).transpose(1, 0, 2))

    shared = {
        "wq_t": wt_chunks(L0_attn_Wq),
        "wk_t": wt_chunks(L0_attn_Wk),
        "wv_t": wt_chunks(L0_attn_Wv),
        "wo_t": wt_chunks(L0_attn_Wo),
        "lsr_q": np.ascontiguousarray(np.asarray(L0_attn_Wq_lsr, dtype=f32)),
        "lsr_k": np.ascontiguousarray(np.asarray(L0_attn_Wk_lsr, dtype=f32)),
        "b_qkvo": np.ascontiguousarray(
            np.stack([np.asarray(b, dtype=f32) for b in
                      (L0_attn_bq, L0_attn_bk, L0_attn_bv, L0_attn_bo)])),
        "w_up": np.ascontiguousarray(np.asarray(L0_ffn_w_up, dtype=f32)),
        "w_gate": np.ascontiguousarray(np.asarray(L0_ffn_w_gate, dtype=f32)),
        "w_down": np.ascontiguousarray(np.asarray(L0_ffn_w_down, dtype=f32)),
        "idn_f": np.eye(128, dtype=f32),
        "idn_b": np.eye(128, dtype=ml_dtypes.bfloat16),
        "maskbig": (np.arange(1024)[None, :] >= (np.arange(128)[:, None] + 512)
                    ).astype(ml_dtypes.bfloat16),
        "ones_f": np.ones((1, 128), dtype=f32),
        "consts": np.broadcast_to(
            np.array([EPS, 1.0 / np.sqrt(32.0), 0.0, 1.0], dtype=f32), (128, 4)).copy(),
    }

    nc = build_nc()
    in_maps = [dict(shared, x=np.ascontiguousarray(x[b])) for b in range(8)]
    res = run_bass_kernel_spmd(nc, in_maps, core_ids=list(range(8)))
    out = np.stack([np.asarray(res.results[b]["out"]) for b in range(8)])
    return out.astype(np.float32)


# version 05
# speedup vs baseline: 1.1636x; 1.0021x over previous
import numpy as np
import ml_dtypes
from contextlib import ExitStack

import concourse.bass as bass
import concourse.mybir as mybir
from concourse.bass_utils import run_bass_kernel_spmd

F32 = mybir.dt.float32
BF16 = mybir.dt.bfloat16
AF = mybir.ActivationFunctionType
ALU = mybir.AluOpType
AX = mybir.AxisListType

T = 2048
D = 1024
NT = 16
ND = 8
FF = 3072
NCF = 24
MAGIC = float(1.5 * 2.0**23)
QMAX = 2.0**15 - 1.0
EPS = 1e-5
ENGS = ("sync", "gpsimd", "tensor", "vector", "scalar")
DEBUG = False


class Ch:
    __slots__ = ("name", "n", "sem")

    def __init__(self, name):
        self.name = name
        self.n = 0
        self.sem = None


class Prog:

    def __init__(self):
        self.ops = {e: [] for e in ENGS}
        self.chans = []
        self.guar = {e: {} for e in ENGS}

    def chan(self, name):
        c = Ch(name)
        self.chans.append(c)
        return c

    def op(self, eng, fn, waits=(), inc=None, inck=1):
        wl = []
        g = self.guar[eng]
        for ch, val in waits:
            if val is None:
                val = ch.n
            if val > 0 and g.get(ch.name, 0) < val:
                wl.append((ch, val))
                g[ch.name] = val
        incs = []
        if inc is not None:
            inc.n += inck
            incs.append((inc, inck))
        self.ops[eng].append((wl, fn, incs))

    def emit(self, block):
        for ename in ENGS:
            ops = self.ops[ename]

            def body(e, ops=ops):
                for wl, fn, incs in ops:
                    for ch, val in wl:
                        e.wait_ge(ch.sem, val)
                    ins = fn(e)
                    for ch, k in incs:
                        ins.then_inc(ch.sem, k)
            getattr(block, ename)(body)


def build_nc():
    nc = bass.Bass("TRN2", target_bir_lowering=False, debug=False)

    x_h = nc.dram_tensor("x", [T, D], F32, kind="ExternalInput")
    wq_h = nc.dram_tensor("wq_t", [ND, D, 128], F32, kind="ExternalInput")
    wk_h = nc.dram_tensor("wk_t", [ND, D, 128], F32, kind="ExternalInput")
    wv_h = nc.dram_tensor("wv_t", [ND, D, 128], F32, kind="ExternalInput")
    wo_h = nc.dram_tensor("wo_t", [ND, D, 128], F32, kind="ExternalInput")
    lsrq_h = nc.dram_tensor("lsr_q", [16, 64, 32], F32, kind="ExternalInput")
    lsrk_h = nc.dram_tensor("lsr_k", [16, 64, 32], F32, kind="ExternalInput")
    b_h = nc.dram_tensor("b_qkvo", [4, D], F32, kind="ExternalInput")
    wup_h = nc.dram_tensor("w_up", [FF, D], F32, kind="ExternalInput")
    wgt_h = nc.dram_tensor("w_gate", [FF, D], F32, kind="ExternalInput")
    wdn_h = nc.dram_tensor("w_down", [D, FF], F32, kind="ExternalInput")
    idf_h = nc.dram_tensor("idn_f", [128, 128], F32, kind="ExternalInput")
    idb_h = nc.dram_tensor("idn_b", [128, 128], BF16, kind="ExternalInput")
    msk_h = nc.dram_tensor("maskbig", [128, 1024], BF16, kind="ExternalInput")
    onef_h = nc.dram_tensor("ones_f", [1, 128], F32, kind="ExternalInput")
    cst_h = nc.dram_tensor("consts", [128, 4], F32, kind="ExternalInput")
    out_h = nc.dram_tensor("out", [T, D], F32, kind="ExternalOutput")

    kind = dict(kind="ExternalOutput") if DEBUG else {}
    scr = {
        "up": nc.dram_tensor("scr_up", [NCF, 128, T], BF16, **kind),
        "gt": nc.dram_tensor("scr_gt", [NCF, 128, T], BF16, **kind),
        "g1": nc.dram_tensor("scr_g1", [NCF, 128, T], BF16, **kind),
        "o": nc.dram_tensor("scr_o", [NCF, 128, T], BF16, **kind),
    }
    if DEBUG:
        d_xnT = nc.dram_tensor("d_xnT", [128, NT * 1024], BF16, kind="ExternalOutput")
        d_xqT = nc.dram_tensor("d_xqT", [128, NT * 1024], BF16, kind="ExternalOutput")
        d_yall = nc.dram_tensor("d_yall", [128, NT * 1024], BF16, kind="ExternalOutput")
        d_stats = nc.dram_tensor("d_stats", [128, 96], F32, kind="ExternalOutput")
        d_scl = nc.dram_tensor("d_scl", [128, 16], F32, kind="ExternalOutput")
        d_var = nc.dram_tensor("d_var", [128, 48], F32, kind="ExternalOutput")
    dnT_h = nc.dram_tensor("dnT_cache", [ND, 128, FF], BF16)
    cc_in = [nc.dram_tensor(f"cc_in{i}", [1, 16], F32) for i in range(4)]
    cc_out = [nc.dram_tensor(f"cc_out{i}", [1, 16], F32) for i in range(4)]

    P = Prog()
    chL = P.chan("chL")
    chACC = P.chan("chACC")
    chOUT = P.chan("chOUT")
    chSW = P.chan("chSW")
    chSR = P.chan("chSR")
    chGW = P.chan("chGW")
    chCC = P.chan("chCC")
    chPE = P.chan("chPE")
    chACT = P.chan("chACT")
    chDVE = P.chan("chDVE")
    chGP = P.chan("chGP")
    chDBG = P.chan("chDBG")

    with ExitStack() as ctx:
        sbt = lambda name, shape, dt: ctx.enter_context(nc.sbuf_tensor(name, shape, dt))
        pst = lambda name, shape, dt: ctx.enter_context(nc.psum_tensor(name, shape, dt))

        xstage = sbt("xstage", [128, 4 * 1024], F32)
        wf32 = sbt("wf32", [128, 8 * 1024], F32)
        arA = sbt("arA", [128, NT * 1024], BF16)
        arB = sbt("arB", [128, NT * 1024], BF16)
        arC = sbt("arC", [128, NT * 1024], BF16)
        scb = sbt("scb", [128, 6 * 1024], BF16)
        wbf = sbt("wbf", [128, 6 * 1024], BF16)
        vhat = sbt("vhat", [128, NT * 130], BF16)
        qkscr = sbt("qkscr", [128, T], BF16)
        qlrb = sbt("qlrb", [64, T], BF16)
        klrb = sbt("klrb", [64, T], BF16)
        ptile = sbt("ptile", [128, 4 * 512], BF16)
        ybf = sbt("ybf", [64, 512], BF16)
        lsq = sbt("lsq", [128, 8 * 64], BF16)
        lsk = sbt("lsk", [128, 8 * 64], BF16)
        idf = sbt("idf", [128, 128], F32)
        idb = sbt("idb", [128, 128], BF16)
        mask = sbt("mask", [128, 1024], BF16)
        onef = sbt("onef", [1, 128], F32)
        cst = sbt("cst", [128, 4], F32)
        sS1 = sbt("sS1", [128, 16], F32)
        sS2 = sbt("sS2", [128, 16], F32)
        sM = sbt("sM", [128, 16], F32)
        sR = sbt("sR", [128, 16], F32)
        sNB = sbt("sNB", [128, 16], F32)
        sMX = sbt("sMX", [128, 16], F32)
        sA = sbt("sA", [128, 16], F32)
        sBt = sbt("sBt", [128, 16], F32)
        tmp16 = sbt("tmp16", [128, 16], F32)
        tmp16b = sbt("tmp16b", [128, 16], F32)
        bias4 = sbt("bias4", [128, 4 * 8], F32)
        scl = sbt("scl", [128, 16], F32)
        pmx = sbt("pmx", [128, 4 * 24], F32)
        dqs = sbt("dqs", [128, 4 * 32], F32)
        ccs = sbt("ccs", [1, 16], F32)
        cco = sbt("cco", [1, 16], F32)
        rcp = sbt("rcp", [1, 512], F32)
        den = sbt("den", [1, 512], F32)
        r4 = sbt("r4", [128, 4], F32)

        psAB = pst("psAB", [128, 1024], F32)
        psCD = pst("psCD", [128, 1024], F32)
        psC = pst("psC", [128, 512], F32)
        psC2 = pst("psC2", [128, 512], F32)
        psD = psC2
        psE = pst("psE", [128, 1024], BF16)
        psF = pst("psF", [128, 1024], BF16)

        C_EPS = cst[:, 0:1]
        C_RSQ32 = cst[:, 1:2]
        C_ZERO = cst[:, 2:3]
        SCL = lambda i: scl[:, i:i + 1]
        I_SX, I_RX, I_SUP, I_RUP, I_SGT, I_RGT, I_SG1, I_RG1, I_SO, I_RO = range(10)

        arAf = arA[:].bitcast(F32)
        arBf = arB[:].bitcast(F32)
        wfb = wf32[:].bitcast(BF16)
        xsb = xstage[:].bitcast(BF16)

        xslot = lambda i: xstage[:, (i % 4) * 1024:((i % 4) + 1) * 1024]
        wfslot = lambda i: wf32[:, i * 1024:(i + 1) * 1024]
        wbslot = lambda i: wbf[:, i * 1024:(i + 1) * 1024]
        scbh = lambda i: scb[:, i * 1024:(i + 1) * 1024]
        scbs = lambda i: scb[:, i * 2048:(i + 1) * 2048]
        pslot = lambda i: ptile[:, (i % 4) * 512:((i % 4) + 1) * 512]

        for dst, src in ((idf, idf_h), (idb, idb_h), (mask, msk_h), (onef, onef_h), (cst, cst_h)):
            P.op("sync", lambda e, dst=dst, src=src: e.dma_start(out=dst[:], in_=src[:, :]),
                 inc=chL, inck=16)
        P.op("sync", lambda e: e.dma_start(
            out=xstage[0:1, :].rearrange("o (i d) -> o i d", i=4),
            in_=b_h[:, :].unsqueeze(0)), inc=chL, inck=16)
        ld_consts = chL.n

        P.op("gpsimd", lambda e: e.memset(lsq[:], 0.0), inc=chGP)
        P.op("gpsimd", lambda e: e.memset(lsk[:], 0.0), inc=chGP)
        P.op("gpsimd", lambda e: e.memset(
            vhat[:].rearrange("p (b w) -> p b w", w=130)[:, :, 64:65], 1.0), inc=chGP)
        P.op("gpsimd", lambda e: e.memset(
            vhat[:].rearrange("p (b w) -> p b w", w=130)[:, :, 129:130], 1.0), inc=chGP)
        lsr_memset = chGP.n
        for a in range(8):
            for dst, src in ((lsq, lsrq_h), (lsk, lsrk_h)):
                P.op("gpsimd", lambda e, dst=dst, src=src, a=a: e.dma_start(
                    out=dst[0:64, a * 64:a * 64 + 32], in_=src[2 * a, :, :]),
                    waits=[(chGP, lsr_memset)], inc=chGW, inck=16)
                P.op("gpsimd", lambda e, dst=dst, src=src, a=a: e.dma_start(
                    out=dst[64:128, a * 64 + 32:a * 64 + 64], in_=src[2 * a + 1, :, :]),
                    inc=chGW, inck=16)
        lsr_ready = chGW.n

        for i in range(4):
            for c in range(ND):
                P.op("tensor", lambda e, i=i, c=c: e.matmul(
                    psD[:, (i * 8 + c):(i * 8 + c) + 1],
                    xstage[0:1, i * 1024 + c * 128:i * 1024 + (c + 1) * 128],
                    onef[0:1, 0:1], start=True, stop=True),
                    waits=[(chL, ld_consts)], inc=chPE)
        bias_mm = chPE.n
        P.op("scalar", lambda e: e.activation(bias4[:], psD[:, 0:32], AF.Identity,
                                              bias=C_ZERO, scale=1.0),
             waits=[(chPE, bias_mm), (chL, ld_consts)], inc=chACT)
        bias_ready = chACT.n

        stats_done = {}
        for t in range(NT):
            P.op("sync", lambda e, t=t: e.dma_start(out=xslot(t), in_=x_h[t * 128:(t + 1) * 128, :]),
                 waits=([(chDVE, stats_done[t - 4])] if t >= 4 else [(chACT, bias_ready)]),
                 inc=chL, inck=16)
            ldn = chL.n
            P.op("vector", lambda e, t=t: e.tensor_reduce(
                sS1[:, t:t + 1], xslot(t), AX.XY, ALU.add),
                waits=[(chL, ldn), (chDVE, chDVE.n)], inc=chDVE)
            n_s1 = chDVE.n
            P.op("scalar", lambda e, t=t: e.activation(
                xslot(t), xslot(t), AF.Square, bias=C_ZERO, scale=1.0),
                waits=[(chDVE, n_s1)], inc=chACT)
            n_sq = chACT.n
            P.op("vector", lambda e, t=t: e.tensor_reduce(
                sS2[:, t:t + 1], xslot(t), AX.XY, ALU.add),
                waits=[(chACT, n_sq), (chDVE, chDVE.n)], inc=chDVE)
            stats_done[t] = chDVE.n
        stats_all = chDVE.n

        P.op("vector", lambda e: e.tensor_scalar(sM[:], sS1[:], 1.0 / D, None, ALU.mult),
             waits=[(chDVE, stats_all), (chDVE, chDVE.n)], inc=chDVE)
        P.op("vector", lambda e: e.tensor_scalar(tmp16[:], sS2[:], 1.0 / D, None, ALU.mult), inc=chDVE)
        P.op("vector", lambda e: e.tensor_tensor(tmp16b[:], sM[:], sM[:], ALU.mult),
             waits=[(chDVE, chDVE.n)], inc=chDVE)
        P.op("vector", lambda e: e.tensor_tensor(tmp16[:], tmp16[:], tmp16b[:], ALU.subtract),
             waits=[(chDVE, chDVE.n)], inc=chDVE)
        var_done = chDVE.n
        if DEBUG:
            P.op("sync", lambda e: e.dma_start(out=d_var[:, 0:16], in_=tmp16[:]),
                 waits=[(chDVE, var_done)], inc=chDBG, inck=16)
            P.op("sync", lambda e: e.dma_start(out=d_var[:, 16:32], in_=tmp16b[:]),
                 inc=chDBG, inck=16)
        P.op("scalar", lambda e: e.activation(sR[:], tmp16[:], AF.Sqrt, bias=C_EPS, scale=1.0),
             waits=[(chDVE, var_done), (chDBG, chDBG.n)], inc=chACT)
        sd_done = chACT.n
        if DEBUG:
            P.op("sync", lambda e: e.dma_start(out=d_var[:, 32:48], in_=sR[:]),
                 waits=[(chACT, sd_done)], inc=chDBG, inck=16)
        P.op("vector", lambda e: e.reciprocal(sR[:], sR[:]),
             waits=[(chACT, sd_done), (chDBG, chDBG.n)], inc=chDVE)
        P.op("vector", lambda e: e.tensor_tensor(sNB[:], sM[:], sR[:], ALU.mult),
             waits=[(chDVE, chDVE.n)], inc=chDVE)
        P.op("vector", lambda e: e.tensor_scalar(sNB[:], sNB[:], -1.0, None, ALU.mult),
             waits=[(chDVE, chDVE.n)], inc=chDVE)
        ln_ready = chDVE.n

        trS = {"i": 0, "ev": {}}
        psD_dve = {}
        lsr_ev = {}
        sc_bank_ev = {}
        av_free = {}
        av_evac = {}

        def transpose_tile_to_arena(t, src_ap, dst_ar, dep_waits):
            for g in range(2):
                par = trS["i"] % 2
                pdst = psE if par == 0 else psF
                w = list(dep_waits) + [(chDVE, trS["ev"].get(par, 0)), (chL, ld_consts)]
                for i in range(4):
                    c = g * 4 + i
                    P.op("tensor", lambda e, i=i, c=c, pdst=pdst, src_ap=src_ap: e.transpose(
                        pdst[:, i * 128:(i + 1) * 128], src_ap[:, c * 128:(c + 1) * 128], idb[:]),
                        waits=(w if i == 0 else []), inc=chPE)
                trn = chPE.n
                dv = dst_ar[:].rearrange("p (c w) -> p c w", w=T)[
                    :, g * 4:(g + 1) * 4, t * 128:(t + 1) * 128]
                P.op("vector", lambda e, pdst=pdst, dv=dv: e.tensor_copy(
                    dv, pdst[:, 0:512].rearrange("p (a w) -> p a w", w=128)),
                    waits=[(chPE, trn)], inc=chDVE)
                trS["ev"][par] = chDVE.n
                trS["i"] += 1

        xn_done = {}
        for t in range(NT):
            P.op("sync", lambda e, t=t: e.dma_start(out=xslot(t), in_=x_h[t * 128:(t + 1) * 128, :]),
                 waits=([(chDVE, xn_done[t - 4])] if t >= 4 else [(chDVE, stats_all)]),
                 inc=chL, inck=16)
            ldn = chL.n
            dst = scbh(t % 2)
            P.op("scalar", lambda e, t=t, dst=dst: e.activation(
                dst, xslot(t), AF.Identity, bias=sNB[:, t:t + 1], scale=sR[:, t:t + 1]),
                waits=[(chL, ldn), (chDVE, ln_ready), (chDVE, xn_done.get(t - 2, 0))],
                inc=chACT)
            act_n = chACT.n
            transpose_tile_to_arena(t, dst, arA, [(chACT, act_n)])
            P.op("vector", lambda e, t=t, dst=dst: e.tensor_reduce(
                sMX[:, t:t + 1], dst, AX.XY, ALU.max, apply_absolute_value=True), inc=chDVE)
            xn_done[t] = chDVE.n
        xnT_ready = chDVE.n
        if DEBUG:
            P.op("sync", lambda e: e.dma_start(out=d_xnT[:, :], in_=arA[:]),
                 waits=[(chDVE, xnT_ready)], inc=chDBG, inck=16)
            for i, st_ in enumerate((sM, sR, sNB, sMX, sS1, sS2)):
                P.op("sync", lambda e, i=i, st_=st_: e.dma_start(
                    out=d_stats[:, i * 16:(i + 1) * 16], in_=st_[:]), inc=chDBG, inck=16)

        def part_max_to_ccs(src_ap, col, extra_waits=()):
            P.op("vector", lambda e, src_ap=src_ap: e.tensor_reduce(
                tmp16[:, 0:1], src_ap, AX.X, ALU.max),
                waits=list(extra_waits) + [(chGW, chGW.n), (chDVE, chDVE.n)], inc=chDVE)
            n1 = chDVE.n
            P.op("tensor", lambda e: e.transpose(
                psD[0:1, 0:128], tmp16[:, 0:1], idf[:]),
                waits=[(chDVE, n1), (chACT, lsr_ev.get("any", 0))], inc=chPE)
            n2 = chPE.n
            P.op("vector", lambda e, col=col: e.tensor_reduce(
                ccs[0:1, col:col + 1], psD[0:1, 0:128], AX.X, ALU.max),
                waits=[(chPE, n2)], inc=chDVE)
            psD_dve[0] = chDVE.n
            return chDVE.n

        def collective(idx, dep_ch, dep_val):
            P.op("gpsimd", lambda e, idx=idx: e.dma_start(out=cc_in[idx][:, :], in_=ccs[:, :]),
                 waits=[(dep_ch, dep_val)], inc=chGW, inck=16)
            n = chGW.n
            P.op("gpsimd", lambda e, idx=idx: e.collective_compute(
                "AllReduce", ALU.max, replica_groups=[list(range(8))],
                ins=[cc_in[idx].ap()], outs=[cc_out[idx].ap()]),
                waits=[(chGW, n)], inc=chCC)
            nn = chCC.n
            P.op("gpsimd", lambda e, idx=idx: e.dma_start(out=cco[:, :], in_=cc_out[idx][:, :]),
                 waits=[(chCC, nn)], inc=chGW, inck=16)
            return chGW.n

        def scale_math(col, s_idx, r_idx, dep):
            P.op("vector", lambda e, col=col: e.tensor_scalar(
                ccs[0:1, 8:9], cco[0:1, col:col + 1], 1.0 / QMAX, 1e-12, ALU.mult, ALU.max),
                waits=[(chGW, dep)], inc=chDVE)
            P.op("vector", lambda e: e.reciprocal(ccs[0:1, 9:10], ccs[0:1, 8:9]),
                 waits=[(chDVE, chDVE.n)], inc=chDVE)
            n1 = chDVE.n
            P.op("tensor", lambda e: e.matmul(psD[:, 0:1], onef[0:1, :], ccs[0:1, 8:9],
                                              start=True, stop=True),
                 waits=[(chDVE, n1), (chACT, lsr_ev.get("any", 0))], inc=chPE)
            P.op("tensor", lambda e: e.matmul(psD[:, 1:2], onef[0:1, :], ccs[0:1, 9:10],
                                              start=True, stop=True), inc=chPE)
            n2 = chPE.n
            P.op("vector", lambda e, s_idx=s_idx: e.tensor_copy(SCL(s_idx), psD[:, 0:1]),
                 waits=[(chPE, n2)], inc=chDVE)
            P.op("vector", lambda e, r_idx=r_idx: e.tensor_copy(SCL(r_idx), psD[:, 1:2]), inc=chDVE)
            psD_dve[0] = chDVE.n
            return chDVE.n

        mx_done = part_max_to_ccs(sMX[:], 0, [(chDVE, xnT_ready)])
        c1 = collective(0, chDVE, mx_done)
        sx_ready = scale_math(0, I_SX, I_RX, c1)

        P.op("vector", lambda e: e.tensor_scalar(sA[:], sR[:], SCL(I_RX), None, ALU.mult),
             waits=[(chDVE, sx_ready)], inc=chDVE)
        P.op("vector", lambda e: e.tensor_scalar(sBt[:], sNB[:], SCL(I_RX), -1.0, ALU.mult, ALU.mult),
             inc=chDVE)
        xq_scales = chDVE.n

        xq_done = {}
        for t in range(NT):
            P.op("sync", lambda e, t=t: e.dma_start(out=xslot(t), in_=x_h[t * 128:(t + 1) * 128, :]),
                 waits=([(chDVE, xq_done[t - 4])] if t >= 4 else [(chDVE, xnT_ready)]),
                 inc=chL, inck=16)
            ldn = chL.n
            dst = scbh(t % 2)
            ftmp = wfslot(t % 2)
            P.op("vector", lambda e, t=t, ftmp=ftmp: e.tensor_scalar(
                ftmp, xslot(t), sA[:, t:t + 1], sBt[:, t:t + 1], ALU.mult, ALU.subtract),
                waits=[(chL, ldn), (chDVE, xq_scales), (chDVE, xq_done.get(t - 2, 0))],
                inc=chDVE)
            P.op("vector", lambda e, ftmp=ftmp: e.tensor_scalar(
                ftmp, ftmp, MAGIC, MAGIC, ALU.add, ALU.subtract),
                waits=[(chDVE, chDVE.n)], inc=chDVE)
            P.op("vector", lambda e, dst=dst, ftmp=ftmp: e.tensor_scalar(
                dst, ftmp, SCL(I_SX), None, ALU.mult),
                waits=[(chDVE, chDVE.n)], inc=chDVE)
            dve_n = chDVE.n
            transpose_tile_to_arena(t, dst, arB, [(chDVE, dve_n)])
            xq_done[t] = chDVE.n
        xqT_ready = chDVE.n
        if DEBUG:
            P.op("sync", lambda e: e.dma_start(out=d_xqT[:, :], in_=arB[:]),
                 waits=[(chDVE, xqT_ready)], inc=chDBG, inck=16)

        WCFG = {"up": dict(wh=wup_h, wsl=(0, 1), wtsl=(2, 3), nat=0, dq=0),
                "gt": dict(wh=wgt_h, wsl=(2, 3), wtsl=(4, 5), nat=1, dq=1)}
        dq_free = {}
        nat_free = {}
        wt_free = {}
        ev_free = {}
        dqs_free = {}
        bank_ev = {0: 0, 1: 0}

        for c in range(NCF):
            for wi, wn in enumerate(("up", "gt")):
                cfg = WCFG[wn]
                par = c % 2
                fw = wfslot(cfg["wsl"][par])
                nat = wbslot(cfg["nat"])
                wt = wbslot(cfg["wtsl"][par])
                sq = dqs[:, (2 * cfg["dq"]) * 32:(2 * cfg["dq"] + 1) * 32]
                rq = dqs[:, (2 * cfg["dq"] + 1) * 32:(2 * cfg["dq"] + 2) * 32]

                P.op("sync", lambda e, c=c, wh=cfg["wh"], fw=fw: e.dma_start(
                    out=fw, in_=wh[c * 128:(c + 1) * 128, :]),
                    waits=[(chDVE, dq_free.get((wn, c - 2), 0)),
                           (chDVE, xqT_ready)],
                    inc=chL, inck=16)
                ldn = chL.n

                P.op("vector", lambda e, fw=fw, sq=sq: e.tensor_reduce(
                    sq, fw.rearrange("p (b w) -> p b w", w=32), AX.X, ALU.max,
                    apply_absolute_value=True),
                    waits=[(chL, ldn), (chDVE, dqs_free.get(wn, 0))], inc=chDVE)
                P.op("vector", lambda e, sq=sq: e.tensor_scalar(
                    sq, sq, 1.0 / 7.0, 1e-12, ALU.mult, ALU.max),
                    waits=[(chDVE, chDVE.n)], inc=chDVE)
                P.op("vector", lambda e, sq=sq, rq=rq: e.reciprocal(rq, sq),
                    waits=[(chDVE, chDVE.n)], inc=chDVE)
                n_sr = chDVE.n
                P.op("vector", lambda e, fw=fw, rq=rq: e.tensor_tensor(
                    fw.rearrange("p (b w) -> p b w", w=32),
                    fw.rearrange("p (b w) -> p b w", w=32),
                    rq.unsqueeze(2).broadcast_to([128, 32, 32]), ALU.mult),
                    waits=[(chDVE, n_sr)], inc=chDVE)
                P.op("vector", lambda e, fw=fw: e.tensor_scalar(
                    fw, fw, MAGIC, MAGIC, ALU.add, ALU.subtract),
                    waits=[(chDVE, chDVE.n)], inc=chDVE)
                P.op("vector", lambda e, fw=fw, nat=nat, sq=sq: e.tensor_tensor(
                    nat.rearrange("p (b w) -> p b w", w=32),
                    fw.rearrange("p (b w) -> p b w", w=32),
                    sq.unsqueeze(2).broadcast_to([128, 32, 32]), ALU.mult),
                    waits=[(chDVE, chDVE.n), (chPE, nat_free.get(wn, 0))], inc=chDVE)
                dq_free[(wn, c)] = chDVE.n
                dqs_free[wn] = chDVE.n
                n_nat = chDVE.n

                for g in range(2):
                    tpar = trS["i"] % 2
                    pdst = psE if tpar == 0 else psF
                    w = [(chDVE, n_nat), (chDVE, trS["ev"].get(tpar, 0)),
                         (chPE, wt_free.get((wn, par), 0))]
                    for i in range(4):
                        dc = g * 4 + i
                        P.op("tensor", lambda e, i=i, dc=dc, pdst=pdst, nat=nat: e.transpose(
                            pdst[:, i * 128:(i + 1) * 128], nat[:, dc * 128:(dc + 1) * 128],
                            idb[:]), waits=(w if i == 0 else []), inc=chPE)
                    trn = chPE.n
                    if g == 1:
                        nat_free[wn] = trn
                    P.op("vector", lambda e, pdst=pdst, wt=wt, g=g: e.tensor_copy(
                        wt[:, g * 512:(g + 1) * 512], pdst[:, 0:512]),
                        waits=[(chPE, trn)], inc=chDVE)
                    trS["ev"][tpar] = chDVE.n
                    trS["i"] += 1
                n_wt = chDVE.n

                ev = scbs(cfg["nat"])
                for pp in range(2):
                    bankT = psAB if pp == 0 else psCD
                    for half in range(2):
                        p = pp * 2 + half
                        for dc in range(ND):
                            P.op("tensor", lambda e, p=p, dc=dc, bankT=bankT, half=half,
                                 wt=wt: e.matmul(
                                     bankT[:, half * 512:(half + 1) * 512],
                                     wt[:, dc * 128:(dc + 1) * 128],
                                     arB[:, dc * T + p * 512:dc * T + (p + 1) * 512],
                                     start=(dc == 0), stop=(dc == ND - 1)),
                                 waits=([(chDVE, n_wt), (chACT, bank_ev[pp])]
                                        if (dc == 0 and half == 0) else []), inc=chPE)
                    mmn = chPE.n
                    P.op("scalar", lambda e, pp=pp, bankT=bankT, ev=ev: e.activation(
                        ev[:, pp * 1024:(pp + 1) * 1024], bankT[:], AF.Identity,
                        bias=C_ZERO, scale=1.0),
                        waits=[(chPE, mmn), (chSW, ev_free.get(wn, 0))], inc=chACT)
                    bank_ev[pp] = chACT.n
                wt_free[(wn, par)] = chPE.n
                evn = chACT.n

                P.op("vector", lambda e, ev=ev, c=c, wi=wi: e.tensor_reduce(
                    pmx[:, wi * 24 + c:wi * 24 + c + 1], ev, AX.XY, ALU.max,
                    apply_absolute_value=True),
                    waits=[(chACT, evn)], inc=chDVE)
                P.op("sync", lambda e, ev=ev, c=c, wn=wn: e.dma_start(
                    out=scr[wn][c, :, :], in_=ev),
                    waits=[(chACT, evn)], inc=chSW, inck=16)
                ev_free[wn] = chSW.n

        p1_gp_done = chGP.n
        p1_sw_done = chSW.n
        m_up = part_max_to_ccs(pmx[:, 0:24], 0)
        m_gt = part_max_to_ccs(pmx[:, 24:48], 1)
        c2 = collective(1, chDVE, m_gt)
        s_up = scale_math(0, I_SUP, I_RUP, c2)
        s_gt = scale_math(1, I_SGT, I_RGT, s_up)

        attw_free = {}
        qlr_free = {}
        vhat_free = {}
        sc_used = {}
        yevac_done = [0]
        ynorm_done = [0]
        wload = {}

        def prefetch_attn(a):
            par = a % 2
            extra = []
            if a == 0:
                extra = [(chPE, nat_free["up"]), (chPE, wt_free[("up", 0)]),
                         (chPE, wt_free[("gt", 0)])]
            elif a == 1:
                extra = [(chPE, nat_free["gt"]), (chPE, wt_free[("up", 1)]),
                         (chPE, wt_free[("gt", 1)])]
            for wi, wh in enumerate((wq_h, wk_h, wv_h)):
                P.op("gpsimd", lambda e, wh=wh, wi=wi, par=par, a=a: e.dma_start(
                    out=wbslot(2 * wi + par).rearrange("p (c o) -> p c o", o=128),
                    in_=wh[a].rearrange("(c p) o -> p c o", p=128)),
                    waits=extra + [(chPE, attw_free.get((wi, par), 0)),
                                   (chGP, p1_gp_done)],
                    inc=chGW, inck=16)
            wload[a] = chGW.n

        prefetch_attn(0)
        prefetch_attn(1)

        p2_infree = {}
        p2_outfree = {0: p1_sw_done, 1: p1_sw_done}
        p2_sig_done = {}
        p3_infree = {}
        p3_outfree = {}
        s_g1 = [0]
        s_o = [0]
        p3_seed = [0]

        def emit_p2_chunk(c):
            par = c % 2
            inb = wfb[:, par * 2048:(par + 1) * 2048]
            gate0 = wf32[:, 2 * 1024:4 * 1024]
            sig = wf32[:, 4 * 1024:6 * 1024]
            outb = scbs(par)
            P.op("sync", lambda e, c=c, inb=inb: e.dma_start(out=inb, in_=scr["gt"][c, :, :]),
                 waits=[(chSW, p1_sw_done), (chDVE, p2_infree.get(par, 0)),
                        (chGP, p1_gp_done)],
                 inc=chSR, inck=16)
            ldn = chSR.n
            P.op("vector", lambda e, inb=inb, gate0=gate0: e.tensor_scalar(
                gate0, inb, SCL(I_RGT), MAGIC, ALU.mult, ALU.add),
                waits=[(chSR, ldn), (chDVE, s_gt), (chACT, p2_sig_done.get(c - 1, 0))],
                inc=chDVE)
            P.op("vector", lambda e, gate0=gate0: e.tensor_scalar(
                gate0, gate0, MAGIC, SCL(I_SGT), ALU.subtract, ALU.mult),
                waits=[(chDVE, chDVE.n)], inc=chDVE)
            p2_infree[par] = chDVE.n
            n_g0 = chDVE.n
            P.op("scalar", lambda e, gate0=gate0, sig=sig: e.activation(
                sig, gate0, AF.Tanh, bias=C_ZERO, scale=0.5),
                waits=[(chDVE, n_g0)], inc=chACT)
            p2_sig_done[c] = chACT.n
            P.op("vector", lambda e, sig=sig: e.tensor_scalar(
                sig, sig, 32768.0, MAGIC + 32768.0, ALU.mult, ALU.add),
                waits=[(chACT, p2_sig_done[c])], inc=chDVE)
            P.op("vector", lambda e, sig=sig: e.tensor_scalar(
                sig, sig, MAGIC, 65535.0, ALU.subtract, ALU.min),
                waits=[(chDVE, chDVE.n)], inc=chDVE)
            P.op("vector", lambda e, sig=sig: e.tensor_scalar(
                sig, sig, 1.0 / 65536.0, None, ALU.mult),
                waits=[(chDVE, chDVE.n)], inc=chDVE)
            P.op("vector", lambda e, gate0=gate0, sig=sig, outb=outb: e.tensor_tensor(
                outb, gate0, sig, ALU.mult),
                waits=[(chSW, p2_outfree.get(par, 0)), (chDVE, chDVE.n)], inc=chDVE)
            P.op("vector", lambda e, outb=outb, c=c: e.tensor_reduce(
                pmx[:, 48 + c:48 + c + 1], outb, AX.XY, ALU.max,
                apply_absolute_value=True),
                waits=[(chDVE, chDVE.n)], inc=chDVE)
            n_out = chDVE.n
            P.op("sync", lambda e, outb=outb, c=c: e.dma_start(
                out=scr["g1"][c, :, :], in_=outb),
                waits=[(chDVE, n_out)], inc=chSW, inck=16)
            p2_outfree[par] = chSW.n

        def emit_p3_chunk(c, p2_dve_done):
            par = c % 2
            ing = wfb[:, par * 2048:(par + 1) * 2048]
            inu = wfb[:, 4096 + par * 2048:4096 + (par + 1) * 2048]
            g1 = wf32[:, 4 * 1024:6 * 1024]
            up = wf32[:, 6 * 1024:8 * 1024]
            outb = scbs(par)
            P.op("sync", lambda e, c=c, ing=ing: e.dma_start(out=ing, in_=scr["g1"][c, :, :]),
                 waits=[(chSW, chSW.n), (chDVE, p3_infree.get(par, p2_dve_done))],
                 inc=chSR, inck=16)
            P.op("sync", lambda e, c=c, inu=inu: e.dma_start(out=inu, in_=scr["up"][c, :, :]),
                 inc=chSR, inck=16)
            ldn = chSR.n
            P.op("vector", lambda e, ing=ing, g1=g1: e.tensor_scalar(
                g1, ing, SCL(I_RG1), MAGIC, ALU.mult, ALU.add),
                waits=[(chSR, ldn), (chDVE, s_g1[0])], inc=chDVE)
            P.op("vector", lambda e, g1=g1: e.tensor_scalar(
                g1, g1, MAGIC, SCL(I_SG1), ALU.subtract, ALU.mult),
                waits=[(chDVE, chDVE.n)], inc=chDVE)
            P.op("vector", lambda e, inu=inu, up=up: e.tensor_scalar(
                up, inu, SCL(I_RUP), MAGIC, ALU.mult, ALU.add), inc=chDVE)
            P.op("vector", lambda e, up=up: e.tensor_scalar(
                up, up, MAGIC, SCL(I_SUP), ALU.subtract, ALU.mult),
                waits=[(chDVE, chDVE.n)], inc=chDVE)
            p3_infree[par] = chDVE.n
            P.op("vector", lambda e, g1=g1, up=up, outb=outb: e.tensor_tensor(
                outb, g1, up, ALU.mult),
                waits=[(chSW, p3_outfree.get(par, 0)), (chDVE, chDVE.n)], inc=chDVE)
            P.op("vector", lambda e, outb=outb, c=c: e.tensor_reduce(
                pmx[:, 72 + c:72 + c + 1], outb, AX.XY, ALU.max,
                apply_absolute_value=True),
                waits=[(chDVE, chDVE.n)], inc=chDVE)
            n_out = chDVE.n
            P.op("sync", lambda e, outb=outb, c=c: e.dma_start(
                out=scr["o"][c, :, :], in_=outb),
                waits=[(chDVE, n_out)], inc=chSW, inck=16)
            p3_outfree[par] = chSW.n

        den_free = [0]
        for a in range(8):
            par = a % 2
            vh_ready = 0
            for wi, wn in enumerate(("q", "k", "v")):
                wt = wbslot(2 * wi + par)
                for pp in range(2):
                    bankT = psAB if pp == 0 else psCD
                    for half in range(2):
                        p = pp * 2 + half
                        for dc in range(ND):
                            P.op("tensor", lambda e, p=p, dc=dc, bankT=bankT, half=half,
                                 wt=wt: e.matmul(
                                     bankT[:, half * 512:(half + 1) * 512],
                                     wt[:, dc * 128:(dc + 1) * 128],
                                     arA[:, dc * T + p * 512:dc * T + (p + 1) * 512],
                                     start=(dc == 0), stop=(dc == ND - 1)),
                                 waits=([(chGW, wload[a]), (chACT, bank_ev[pp])]
                                        if (dc == 0 and half == 0) else []), inc=chPE)
                    mmn = chPE.n
                    P.op("scalar", lambda e, pp=pp, bankT=bankT, wi=wi, a=a: e.activation(
                        qkscr[:, pp * 1024:(pp + 1) * 1024], bankT[:], AF.Identity,
                        bias=bias4[:, wi * 8 + a:wi * 8 + a + 1], scale=1.0),
                        waits=[(chPE, mmn)], inc=chACT)
                    bank_ev[pp] = chACT.n
                attw_free[(wi, par)] = chPE.n
                evn = chACT.n

                if wn in ("q", "k"):
                    lsr_w = lsq if wn == "q" else lsk
                    dstb = qlrb if wn == "q" else klrb
                    for p in range(4):
                        P.op("tensor", lambda e, p=p, a=a, lsr_w=lsr_w: e.matmul(
                            psD[0:64, :], lsr_w[:, a * 64:(a + 1) * 64],
                            qkscr[:, p * 512:(p + 1) * 512], start=True, stop=True),
                            waits=[(chACT, evn), (chGW, lsr_ready),
                                   (chACT, lsr_ev.get("any", 0)),
                                   (chDVE, psD_dve.get(0, 0)),
                                   (chDVE, av_free.get(1, 0)),
                                   (chACT, av_evac.get(1, 0)),
                                   (chPE, qlr_free.get(0, 0))], inc=chPE)
                        mmn = chPE.n
                        P.op("scalar", lambda e, p=p, dstb=dstb: e.activation(
                            dstb[:, p * 512:(p + 1) * 512],
                            psD[0:64, :], AF.Identity, bias=C_ZERO[0:64, :], scale=1.0),
                            waits=[(chPE, mmn)], inc=chACT)
                        lsr_ev["any"] = chACT.n
                else:
                    for g in range(4):
                        tpar = trS["i"] % 2
                        pdst = psE if tpar == 0 else psF
                        w = [(chACT, evn), (chDVE, trS["ev"].get(tpar, 0)),
                             (chPE, vhat_free.get(0, 0))]
                        for i in range(4):
                            tt = g * 4 + i
                            P.op("tensor", lambda e, i=i, tt=tt, pdst=pdst: e.transpose(
                                pdst[:, i * 128:(i + 1) * 128],
                                qkscr[:, tt * 128:(tt + 1) * 128], idb[:]),
                                waits=(w if i == 0 else []), inc=chPE)
                        trn = chPE.n
                        for half in range(2):
                            P.op("vector", lambda e, pdst=pdst, g=g, half=half:
                                 e.tensor_copy(
                                     vhat[:].rearrange("p (b w) -> p b w", w=130)
                                     [:, g * 4:(g + 1) * 4, half * 65:half * 65 + 64],
                                     pdst[:, 0:512].rearrange("p (a w) -> p a w", w=128)
                                     [:, :, half * 64:(half + 1) * 64]),
                                 waits=[(chPE, trn), (chGP, lsr_memset)], inc=chDVE)
                        trS["ev"][tpar] = chDVE.n
                        trS["i"] += 1
                    vh_ready = chDVE.n

            if a + 2 < 8:
                prefetch_attn(a + 2)
            qln = chACT.n
            for hh in range(2):
                qsl = qlrb[hh * 32:(hh + 1) * 32, :]
                ksl = klrb[hh * 32:(hh + 1) * 32, :]
                for ic in range(4):
                    njt = 4 * ic + 4
                    npair = njt // 2
                    acc_i = (hh * 4 + ic) % 2
                    accT = psC if acc_i == 0 else psC2
                    acc = accT[0:65, :]
                    mwaits = {}
                    for step in range(npair + 1):
                        if step < npair:
                            pk = step
                            bankT = psAB if pk % 2 == 0 else psCD
                            for half in range(2):
                                jt = pk * 2 + half
                                P.op("tensor", lambda e, jt=jt, ic=ic, bankT=bankT,
                                     half=half, qsl=qsl, ksl=ksl: e.matmul(
                                         bankT[:, half * 512:(half + 1) * 512],
                                         ksl[:, jt * 128:(jt + 1) * 128],
                                         qsl[:, ic * 512:(ic + 1) * 512],
                                         start=True, stop=True),
                                     waits=([(chACT, qln), (chACT, bank_ev[pk % 2])]
                                            if half == 0 else []), inc=chPE)
                            mmn = chPE.n
                            ps2 = ptile[:, ((2 * pk) % 4) * 512:((2 * pk) % 4) * 512 + 1024]
                            P.op("scalar", lambda e, bankT=bankT, ps2=ps2: e.activation(
                                ps2, bankT[:], AF.Exp, bias=C_ZERO, scale=C_RSQ32),
                                waits=[(chPE, mmn), (chPE, sc_used.get(pk % 2, 0))],
                                inc=chACT)
                            bank_ev[pk % 2] = chACT.n
                            expn = chACT.n
                            mwaits[2 * pk] = [(chACT, expn)]
                            mwaits[2 * pk + 1] = [(chACT, expn)]
                            for half in range(2):
                                jt = pk * 2 + half
                                if jt >= 4 * ic:
                                    kk = jt - 4 * ic
                                    P.op("gpsimd", lambda e, jt=jt, kk=kk: e.tensor_tensor(
                                        pslot(jt), pslot(jt),
                                        mask[:, 512 - 128 * kk:1024 - 128 * kk], ALU.mult),
                                        waits=[(chACT, expn)], inc=chGP)
                                    mwaits[jt] = [(chGP, chGP.n)]
                        if step >= 1:
                            for half in range(2):
                                jt = (step - 1) * 2 + half
                                P.op("tensor", lambda e, jt=jt, hh=hh, njt=njt, acc=acc:
                                     e.matmul(
                                         acc,
                                         vhat[:, jt * 130 + hh * 65:jt * 130 + hh * 65 + 65],
                                         pslot(jt), start=(jt == 0), stop=(jt == njt - 1)),
                                     waits=mwaits[jt] + ([(chDVE, vh_ready),
                                                          (chDVE, av_free.get(acc_i, 0)),
                                                          (chACT, av_evac.get(acc_i, 0))]
                                                         + ([(chACT, lsr_ev.get("any", 0)),
                                                             (chDVE, psD_dve.get(0, 0))]
                                                            if acc_i == 1 else [])
                                                         if jt == 0 else []),
                                     inc=chPE)
                            sc_used[(step - 1) % 2] = chPE.n
                    avn = chPE.n
                    P.op("scalar", lambda e, accT=accT: e.activation(
                        den[0:1, :], accT[64:65, :], AF.Identity,
                        bias=C_ZERO[0:1, :], scale=1.0),
                        waits=[(chPE, avn), (chPE, den_free[0])], inc=chACT)
                    P.op("scalar", lambda e, accT=accT: e.activation(
                        ybf[:], accT[0:64, :], AF.Identity,
                        bias=C_ZERO[0:64, :], scale=1.0),
                        waits=[(chDVE, ynorm_done[0])], inc=chACT)
                    ev2 = chACT.n
                    av_evac[acc_i] = ev2
                    for c in range(4):
                        P.op("tensor", lambda e, c=c, accT=accT: e.transpose(
                            accT[:, c:c + 1], den[0:1, c * 128:(c + 1) * 128],
                            idf[0:1, 0:1]),
                            waits=([(chACT, ev2)] if c == 0 else []), inc=chPE)
                    den_free[0] = chPE.n
                    trin = chPE.n
                    P.op("vector", lambda e, accT=accT: e.reciprocal(r4[:], accT[:, 0:4]),
                         waits=[(chPE, trin)], inc=chDVE)
                    rn = chDVE.n
                    for c in range(4):
                        P.op("tensor", lambda e, c=c, accT=accT: e.transpose(
                            accT[0:1, c * 128:(c + 1) * 128], r4[:, c:c + 1], idf[:]),
                            waits=([(chDVE, rn)] if c == 0 else []), inc=chPE)
                    trout = chPE.n
                    P.op("vector", lambda e, accT=accT: e.tensor_copy(
                        rcp[0:1, :], accT[0:1, 0:512]),
                        waits=[(chPE, trout)], inc=chDVE)
                    rc = chDVE.n
                    P.op("tensor", lambda e, accT=accT: e.matmul(
                        accT[0:64, :], onef[0:1, 0:64], rcp[0:1, :],
                        start=True, stop=True),
                        waits=[(chDVE, rc)], inc=chPE)
                    bn = chPE.n
                    P.op("vector", lambda e, a=a, hh=hh, ic=ic, accT=accT: e.tensor_tensor(
                        arC[hh * 64:(hh + 1) * 64, a * T + ic * 512:a * T + (ic + 1) * 512],
                        ybf[:], accT[0:64, :], ALU.mult),
                        waits=[(chPE, bn), (chACT, ev2)], inc=chDVE)
                    ynorm_done[0] = chDVE.n
                    av_free[acc_i] = chDVE.n
            qlr_free[0] = chPE.n
            vhat_free[0] = chPE.n

            if a in (2, 3, 4):
                for c in range((a - 2) * 8, (a - 1) * 8):
                    emit_p2_chunk(c)
                if a == 4:
                    m_g1 = part_max_to_ccs(pmx[:, 48:72], 0)
                    c3 = collective(2, chDVE, m_g1)
                    s_g1[0] = scale_math(0, I_SG1, I_RG1, c3)
            if a in (5, 6, 7):
                if a == 5:
                    p3_seed[0] = chDVE.n
                for c in range((a - 5) * 8, (a - 4) * 8):
                    emit_p3_chunk(c, p3_seed[0])
                if a == 7:
                    m_o = part_max_to_ccs(pmx[:, 72:96], 0)
                    c4 = collective(3, chDVE, m_o)
                    s_o[0] = scale_math(0, I_SO, I_RO, c4)

        qkv_all_done = chPE.n

        if DEBUG:
            P.op("sync", lambda e: e.dma_start(out=d_yall[:, :], in_=arC[:]),
                 waits=[(chDVE, chDVE.n)], inc=chDBG, inck=16)
            P.op("sync", lambda e: e.dma_start(out=d_scl[:, :], in_=scl[:]),
                 waits=[(chDVE, chDVE.n)], inc=chDBG, inck=16)
        P.op("sync", lambda e: e.dma_start(
            out=arAf.rearrange("p (t w) -> p t w", w=1024),
            in_=x_h[0:1024, :].rearrange("(t p) d -> p t d", p=128)),
            waits=[(chPE, qkv_all_done), (chDBG, chDBG.n)], inc=chACC, inck=16)
        P.op("sync", lambda e: e.dma_start(
            out=arBf.rearrange("p (t w) -> p t w", w=1024),
            in_=x_h[1024:2048, :].rearrange("(t p) d -> p t d", p=128)),
            inc=chACC, inck=16)
        acc_ld = chACC.n

        op_wfree = {}
        trf_ev = {0: chDVE.n, 1: chDVE.n}
        trf_add = [0]

        def evac_pair_add(bankT, bias_ap, pair, oc, mm_snap):
            P.op("scalar", lambda e, bankT=bankT, bias_ap=bias_ap: e.activation(
                xstage[:, 3 * 1024:4 * 1024], bankT[:], AF.Identity,
                bias=bias_ap, scale=1.0),
                waits=[(chPE, mm_snap), (chDVE, trf_add[0])], inc=chACT)
            evn = chACT.n
            arf = arAf if pair == 0 else arBf
            for g in range(2):
                tpar = trS["i"] % 2
                pdst = (psE if tpar == 0 else psF)[:].bitcast(F32)
                for i in range(4):
                    P.op("tensor", lambda e, g=g, i=i, pdst=pdst: e.transpose(
                        pdst[:, i * 128:(i + 1) * 128],
                        xstage[:, 3 * 1024 + (g * 4 + i) * 128:
                               3 * 1024 + (g * 4 + i + 1) * 128], idf[:]),
                        waits=([(chACT, evn), (chDVE, trS["ev"].get(tpar, 0))]
                               if i == 0 else []), inc=chPE)
                trn = chPE.n
                accv = arf.rearrange("p (t w) -> p t w", w=1024)[
                    :, g * 4:(g + 1) * 4, oc * 128:(oc + 1) * 128]
                P.op("vector", lambda e, accv=accv, pdst=pdst: e.tensor_tensor(
                    accv, accv, pdst[:, 0:512].rearrange("p (a w) -> p a w", w=128),
                    ALU.add),
                    waits=[(chPE, trn), (chACC, acc_ld)], inc=chDVE)
                trS["ev"][tpar] = chDVE.n
                trf_add[0] = chDVE.n
                trS["i"] += 1

        for oc in range(ND):
            par = oc % 2
            P.op("gpsimd", lambda e, oc=oc, par=par: e.dma_start(
                out=wbslot(par).rearrange("p (c o) -> p c o", o=128),
                in_=wo_h[oc].rearrange("(c p) o -> p c o", p=128)),
                waits=[(chPE, attw_free.get((0, par), 0)),
                       (chPE, op_wfree.get(par, 0))], inc=chGW, inck=16)
            ldn = chGW.n
            for pp in range(2):
                bankT = psAB if pp == 0 else psCD
                for half in range(2):
                    p = pp * 2 + half
                    for dc in range(ND):
                        P.op("tensor", lambda e, p=p, dc=dc, bankT=bankT, half=half,
                             par=par: e.matmul(
                                 bankT[:, half * 512:(half + 1) * 512],
                                 wbslot(par)[:, dc * 128:(dc + 1) * 128],
                                 arC[:, dc * T + p * 512:dc * T + (p + 1) * 512],
                                 start=(dc == 0), stop=(dc == ND - 1)),
                             waits=([(chGW, ldn), (chACT, bank_ev[pp]),
                                     (chDVE, ynorm_done[0])]
                                    if (dc == 0 and half == 0) else []), inc=chPE)
                mmn = chPE.n
                evac_pair_add(bankT, bias4[:, 24 + oc:24 + oc + 1], pp, oc, mmn)
                bank_ev[pp] = chACT.n
            op_wfree[par] = chPE.n
        oproj_pe_done = chPE.n
        oproj_done = chDVE.n

        def oT_view(ic):
            if ic < 16:
                return arC[:, ic * 1024:(ic + 1) * 1024]
            if ic < 18:
                return vhat[:, (ic - 16) * 1024:(ic - 15) * 1024]
            if ic < 20:
                return qkscr[:, (ic - 18) * 1024:(ic - 17) * 1024]
            return scbh(2 + (ic - 20))

        p3_dve_done = chDVE.n
        dn_nat_free = {}
        dn_dqs_free = [0]
        dn_wt_used = {}
        oT_used = {}
        sr_free = {}
        p4_ld_free = [0]
        dn_cached = {}

        for H in range(2):
            for ic in range(NCF):
                par = ic % 2
                inb = scbh(par)
                P.op("sync", lambda e, ic=ic, H=H, inb=inb: e.dma_start(
                    out=inb, in_=scr["o"][ic, :, H * 1024:(H + 1) * 1024]),
                    waits=[(chSW, chSW.n), (chDVE, sr_free.get(par, 0)),
                           (chPE, oproj_pe_done)],
                    inc=chSR, inck=16)
                ldn = chSR.n
                ft = wfslot(7)
                P.op("vector", lambda e, inb=inb, ft=ft: e.tensor_scalar(
                    ft, inb, SCL(I_RO), MAGIC, ALU.mult, ALU.add),
                    waits=[(chSR, ldn), (chDVE, s_o[0]), (chPE, oproj_pe_done),
                           (chPE, oT_used.get(ic, 0)), (chDBG, chDBG.n)], inc=chDVE)
                P.op("vector", lambda e, ic=ic, ft=ft: e.tensor_scalar(
                    oT_view(ic), ft, MAGIC, SCL(I_SO), ALU.subtract, ALU.mult),
                    waits=[(chSW, chSW.n), (chDVE, chDVE.n)], inc=chDVE)
                sr_free[par] = chDVE.n
            oT_ready = chDVE.n

            for oc in range(ND):
                par = oc % 2
                wdnT = wbf[:, par * 3072:(par + 1) * 3072]
                if oc not in dn_cached:
                    P.op("sync", lambda e, oc=oc: e.dma_start(
                        out=wf32[:, 0:3 * 1024].rearrange("p (a w) -> p a w", w=1024),
                        in_=wdn_h[oc * 128:(oc + 1) * 128, :].rearrange(
                            "p (a w) -> p a w", w=1024)),
                        waits=[(chDVE, p3_dve_done), (chDVE, p4_ld_free[0]),
                               (chPE, dn_wt_used.get(par, 0))],
                        inc=chL, inck=16)
                    ldn = chL.n
                    sq = dqs[:, 0:32]
                    rq = dqs[:, 32:64]
                    for pc in range(3):
                        fw = wfslot(pc)
                        nat = xsb[:, (pc % 2) * 1024:((pc % 2) + 1) * 1024]
                        P.op("vector", lambda e, fw=fw, sq=sq: e.tensor_reduce(
                            sq, fw.rearrange("p (b w) -> p b w", w=32), AX.X, ALU.max,
                            apply_absolute_value=True),
                            waits=[(chL, ldn), (chDVE, dn_dqs_free[0]),
                                   (chACT, chACT.n if (H == 0 and oc == 0 and pc == 0) else 0)],
                            inc=chDVE)
                        P.op("vector", lambda e, sq=sq: e.tensor_scalar(
                            sq, sq, 1.0 / 7.0, 1e-12, ALU.mult, ALU.max),
                            waits=[(chDVE, chDVE.n)], inc=chDVE)
                        P.op("vector", lambda e, sq=sq, rq=rq: e.reciprocal(rq, sq),
                            waits=[(chDVE, chDVE.n)], inc=chDVE)
                        n_sr2 = chDVE.n
                        P.op("vector", lambda e, fw=fw, rq=rq: e.tensor_tensor(
                            fw.rearrange("p (b w) -> p b w", w=32),
                            fw.rearrange("p (b w) -> p b w", w=32),
                            rq.unsqueeze(2).broadcast_to([128, 32, 32]), ALU.mult),
                            waits=[(chDVE, n_sr2)], inc=chDVE)
                        P.op("vector", lambda e, fw=fw: e.tensor_scalar(
                            fw, fw, MAGIC, MAGIC, ALU.add, ALU.subtract),
                            waits=[(chDVE, chDVE.n)], inc=chDVE)
                        P.op("vector", lambda e, fw=fw, nat=nat, sq=sq: e.tensor_tensor(
                            nat.rearrange("p (b w) -> p b w", w=32),
                            fw.rearrange("p (b w) -> p b w", w=32),
                            sq.unsqueeze(2).broadcast_to([128, 32, 32]), ALU.mult),
                            waits=[(chDVE, chDVE.n), (chPE, dn_nat_free.get(pc % 2, 0))],
                            inc=chDVE)
                        dn_dqs_free[0] = chDVE.n
                        n_nat2 = chDVE.n
                        if pc == 2:
                            p4_ld_free[0] = chDVE.n
                        for g in range(2):
                            tpar = trS["i"] % 2
                            pdst = psE if tpar == 0 else psF
                            w = [(chDVE, n_nat2), (chDVE, trS["ev"].get(tpar, 0)),
                                 (chPE, dn_wt_used.get(par, 0))]
                            for i in range(4):
                                dc8 = g * 4 + i
                                P.op("tensor", lambda e, i=i, dc8=dc8, pdst=pdst, nat=nat:
                                     e.transpose(pdst[:, i * 128:(i + 1) * 128],
                                                 nat[:, dc8 * 128:(dc8 + 1) * 128], idb[:]),
                                     waits=(w if i == 0 else []), inc=chPE)
                            trn = chPE.n
                            dn_nat_free[pc % 2] = trn
                            P.op("vector", lambda e, pdst=pdst, wdnT=wdnT, pc=pc, g=g:
                                 e.tensor_copy(
                                     wdnT[:, pc * 1024 + g * 512:pc * 1024 + (g + 1) * 512],
                                     pdst[:, 0:512]),
                                 waits=[(chPE, trn)], inc=chDVE)
                            trS["ev"][tpar] = chDVE.n
                            trS["i"] += 1
                    wt_ready_dve = chDVE.n
                    P.op("sync", lambda e, oc=oc, wdnT=wdnT: e.dma_start(
                        out=dnT_h[oc, :, :], in_=wdnT),
                        waits=[(chDVE, wt_ready_dve)], inc=chL, inck=16)
                    dn_cached[oc] = (wt_ready_dve, chL.n)
                    wt_dep = [(chDVE, wt_ready_dve)]
                else:
                    P.op("sync", lambda e, oc=oc, wdnT=wdnT: e.dma_start(
                        out=wdnT, in_=dnT_h[oc, :, :]),
                        waits=[(chL, dn_cached[oc][1]), (chPE, dn_wt_used.get(par, 0))],
                        inc=chL, inck=16)
                    wt_dep = [(chL, chL.n)]

                pp = oc % 2
                bankT = psAB if pp == 0 else psCD
                for half in range(2):
                    for ic in range(NCF):
                        P.op("tensor", lambda e, half=half, ic=ic, bankT=bankT, wdnT=wdnT:
                             e.matmul(bankT[:, half * 512:(half + 1) * 512],
                                      wdnT[:, ic * 128:(ic + 1) * 128],
                                      oT_view(ic)[:, half * 512:(half + 1) * 512],
                                      start=(ic == 0), stop=(ic == NCF - 1)),
                             waits=(wt_dep + [(chDVE, oT_ready), (chACT, bank_ev[pp])]
                                    if (ic == 0 and half == 0) else []), inc=chPE)
                mmn = chPE.n
                dn_wt_used[par] = mmn
                for ic2 in range(NCF):
                    oT_used[ic2] = mmn
                evac_pair_add(bankT, C_ZERO, H, oc, mmn)
                bank_ev[pp] = chACT.n

        ffn_done = chDVE.n

        P.op("sync", lambda e: e.dma_start(
            out=out_h[0:1024, :].rearrange("(t p) d -> p t d", p=128),
            in_=arAf.rearrange("p (t w) -> p t w", w=1024)),
            waits=[(chDVE, ffn_done)], inc=chOUT, inck=16)
        P.op("sync", lambda e: e.dma_start(
            out=out_h[1024:2048, :].rearrange("(t p) d -> p t d", p=128),
            in_=arBf.rearrange("p (t w) -> p t w", w=1024)),
            inc=chOUT, inck=16)
        P.op("sync", lambda e: e.nop(), waits=[(chOUT, chOUT.n), (chDBG, chDBG.n)])

        for c in P.chans:
            c.sem = ctx.enter_context(nc.semaphore(c.name))
        block = ctx.enter_context(nc.Block())
        P.emit(block)

    return nc


def kernel(L0_attn_x, L0_attn_Wq, L0_attn_bq, L0_attn_Wk, L0_attn_bk,
           L0_attn_Wv, L0_attn_bv, L0_attn_Wo, L0_attn_bo,
           L0_attn_Wq_lsr, L0_attn_Wk_lsr,
           L0_ffn_w_gate, L0_ffn_w_up, L0_ffn_w_down):
    f32 = np.float32
    x = np.ascontiguousarray(np.asarray(L0_attn_x, dtype=f32))

    def wt_chunks(W):
        WT = np.ascontiguousarray(np.asarray(W, dtype=f32).T)
        return np.ascontiguousarray(WT.reshape(D, ND, 128).transpose(1, 0, 2))

    shared = {
        "wq_t": wt_chunks(L0_attn_Wq),
        "wk_t": wt_chunks(L0_attn_Wk),
        "wv_t": wt_chunks(L0_attn_Wv),
        "wo_t": wt_chunks(L0_attn_Wo),
        "lsr_q": np.ascontiguousarray(np.asarray(L0_attn_Wq_lsr, dtype=f32)),
        "lsr_k": np.ascontiguousarray(np.asarray(L0_attn_Wk_lsr, dtype=f32)),
        "b_qkvo": np.ascontiguousarray(
            np.stack([np.asarray(b, dtype=f32) for b in
                      (L0_attn_bq, L0_attn_bk, L0_attn_bv, L0_attn_bo)])),
        "w_up": np.ascontiguousarray(np.asarray(L0_ffn_w_up, dtype=f32)),
        "w_gate": np.ascontiguousarray(np.asarray(L0_ffn_w_gate, dtype=f32)),
        "w_down": np.ascontiguousarray(np.asarray(L0_ffn_w_down, dtype=f32)),
        "idn_f": np.eye(128, dtype=f32),
        "idn_b": np.eye(128, dtype=ml_dtypes.bfloat16),
        "maskbig": (np.arange(1024)[None, :] >= (np.arange(128)[:, None] + 512)
                    ).astype(ml_dtypes.bfloat16),
        "ones_f": np.ones((1, 128), dtype=f32),
        "consts": np.broadcast_to(
            np.array([EPS, 1.0 / np.sqrt(32.0), 0.0, 1.0], dtype=f32), (128, 4)).copy(),
    }

    nc = build_nc()
    in_maps = [dict(shared, x=np.ascontiguousarray(x[b])) for b in range(8)]
    res = run_bass_kernel_spmd(nc, in_maps, core_ids=list(range(8)))
    out = np.stack([np.asarray(res.results[b]["out"]) for b in range(8)])
    return out.astype(np.float32)


# version 09
# speedup vs baseline: 1.3340x; 1.1465x over previous
import numpy as np
import ml_dtypes
from contextlib import ExitStack

import concourse.bass as bass
import concourse.mybir as mybir
from concourse.bass_utils import run_bass_kernel_spmd

F32 = mybir.dt.float32
BF16 = mybir.dt.bfloat16
AF = mybir.ActivationFunctionType
ALU = mybir.AluOpType
AX = mybir.AxisListType

T = 2048
D = 1024
NT = 16
ND = 8
FF = 3072
NCF = 24
MAGIC = float(1.5 * 2.0**23)
QMAX = 2.0**15 - 1.0
EPS = 1e-5
ENGS = ("sync", "gpsimd", "tensor", "vector", "scalar")
DEBUG = False


class Ch:
    __slots__ = ("name", "n", "sem")

    def __init__(self, name):
        self.name = name
        self.n = 0
        self.sem = None


class Prog:

    def __init__(self):
        self.ops = {e: [] for e in ENGS}
        self.chans = []
        self.guar = {e: {} for e in ENGS}

    def chan(self, name):
        c = Ch(name)
        self.chans.append(c)
        return c

    def op(self, eng, fn, waits=(), inc=None, inck=1):
        wl = []
        g = self.guar[eng]
        for ch, val in waits:
            if val is None:
                val = ch.n
            if val > 0 and g.get(ch.name, 0) < val:
                wl.append((ch, val))
                g[ch.name] = val
        incs = []
        if inc is not None:
            inc.n += inck
            incs.append((inc, inck))
        self.ops[eng].append((wl, fn, incs))

    def emit(self, block):
        for ename in ENGS:
            ops = self.ops[ename]

            def body(e, ops=ops):
                for wl, fn, incs in ops:
                    for ch, val in wl:
                        e.wait_ge(ch.sem, val)
                    ins = fn(e)
                    for ch, k in incs:
                        ins.then_inc(ch.sem, k)
            getattr(block, ename)(body)


def build_nc():
    nc = bass.Bass("TRN2", target_bir_lowering=False, debug=False)

    x_h = nc.dram_tensor("x", [T, D], F32, kind="ExternalInput")
    wq_h = nc.dram_tensor("wq_t", [ND, D, 128], F32, kind="ExternalInput")
    wk_h = nc.dram_tensor("wk_t", [ND, D, 128], F32, kind="ExternalInput")
    wv_h = nc.dram_tensor("wv_t", [ND, D, 128], F32, kind="ExternalInput")
    wo_h = nc.dram_tensor("wo_t", [ND, D, 128], F32, kind="ExternalInput")
    lsrq_h = nc.dram_tensor("lsr_q", [16, 64, 32], F32, kind="ExternalInput")
    lsrk_h = nc.dram_tensor("lsr_k", [16, 64, 32], F32, kind="ExternalInput")
    b_h = nc.dram_tensor("b_qkvo", [4, D], F32, kind="ExternalInput")
    wup_h = nc.dram_tensor("w_up", [FF, D], F32, kind="ExternalInput")
    wgt_h = nc.dram_tensor("w_gate", [FF, D], F32, kind="ExternalInput")
    wdn_h = nc.dram_tensor("w_down", [D, FF], F32, kind="ExternalInput")
    idf_h = nc.dram_tensor("idn_f", [128, 128], F32, kind="ExternalInput")
    idb_h = nc.dram_tensor("idn_b", [128, 128], BF16, kind="ExternalInput")
    msk_h = nc.dram_tensor("maskbig", [128, 1024], BF16, kind="ExternalInput")
    onef_h = nc.dram_tensor("ones_f", [1, 128], F32, kind="ExternalInput")
    cst_h = nc.dram_tensor("consts", [128, 4], F32, kind="ExternalInput")
    out_h = nc.dram_tensor("out", [T, D], F32, kind="ExternalOutput")

    kind = dict(kind="ExternalOutput") if DEBUG else {}
    scr = {
        "up": nc.dram_tensor("scr_up", [NCF, 128, T], BF16, **kind),
        "gt": nc.dram_tensor("scr_gt", [NCF, 128, T], BF16, **kind),
        "g1": nc.dram_tensor("scr_g1", [NCF, 128, T], BF16, **kind),
        "o": nc.dram_tensor("scr_o", [NCF, 128, T], BF16, **kind),
    }
    if DEBUG:
        d_xnT = nc.dram_tensor("d_xnT", [128, NT * 1024], BF16, kind="ExternalOutput")
        d_xqT = nc.dram_tensor("d_xqT", [128, NT * 1024], BF16, kind="ExternalOutput")
        d_yall = nc.dram_tensor("d_yall", [128, NT * 1024], BF16, kind="ExternalOutput")
        d_stats = nc.dram_tensor("d_stats", [128, 96], F32, kind="ExternalOutput")
        d_scl = nc.dram_tensor("d_scl", [128, 16], F32, kind="ExternalOutput")
        d_var = nc.dram_tensor("d_var", [128, 48], F32, kind="ExternalOutput")
    dnT_h = nc.dram_tensor("dnT_cache", [ND, 128, FF], BF16)
    cc_in = [nc.dram_tensor(f"cc_in{i}", [1, 16], F32) for i in range(5)]
    cc_out = [nc.dram_tensor(f"cc_out{i}", [1, 16], F32) for i in range(5)]

    P = Prog()
    chL = P.chan("chL")
    chACC = P.chan("chACC")
    chOUT = P.chan("chOUT")
    chSW = P.chan("chSW")
    chSR = P.chan("chSR")
    chGW = P.chan("chGW")
    chCC = P.chan("chCC")
    chPE = P.chan("chPE")
    chACT = P.chan("chACT")
    chDVE = P.chan("chDVE")
    chGP = P.chan("chGP")
    chDBG = P.chan("chDBG")

    with ExitStack() as ctx:
        sbt = lambda name, shape, dt: ctx.enter_context(nc.sbuf_tensor(name, shape, dt))
        pst = lambda name, shape, dt: ctx.enter_context(nc.psum_tensor(name, shape, dt))

        xstage = sbt("xstage", [128, 4 * 1024], F32)
        wf32 = sbt("wf32", [128, 8 * 1024], F32)
        arA = sbt("arA", [128, NT * 1024], BF16)
        arB = sbt("arB", [128, NT * 1024], BF16)
        arC = sbt("arC", [128, NT * 1024], BF16)
        scb = sbt("scb", [128, 6 * 1024], BF16)
        wbf = sbt("wbf", [128, 6 * 1024], BF16)
        vhat = sbt("vhat", [128, NT * 130], BF16)
        qkscr = sbt("qkscr", [128, T], BF16)
        qlrb = sbt("qlrb", [64, T], BF16)
        klrb = sbt("klrb", [64, T], BF16)
        ptile = sbt("ptile", [128, 4 * 512], BF16)
        ybf = sbt("ybf", [64, 512], BF16)
        lsq = sbt("lsq", [128, 8 * 64], BF16)
        lsk = sbt("lsk", [128, 8 * 64], BF16)
        idf = sbt("idf", [128, 128], F32)
        idb = sbt("idb", [128, 128], BF16)
        mask = sbt("mask", [128, 1024], BF16)
        onef = sbt("onef", [1, 128], F32)
        cst = sbt("cst", [128, 4], F32)
        sS1 = sbt("sS1", [128, 16], F32)
        sS2 = sbt("sS2", [128, 16], F32)
        sM = sbt("sM", [128, 16], F32)
        sR = sbt("sR", [128, 16], F32)
        sNB = sbt("sNB", [128, 16], F32)
        sMX = sbt("sMX", [128, 16], F32)
        sA = sbt("sA", [128, 16], F32)
        sBt = sbt("sBt", [128, 16], F32)
        tmp16 = sbt("tmp16", [128, 16], F32)
        tmp16b = sbt("tmp16b", [128, 16], F32)
        bias4 = sbt("bias4", [128, 4 * 8], F32)
        scl = sbt("scl", [128, 16], F32)
        pmx = sbt("pmx", [128, 4 * 24], F32)
        dqs = sbt("dqs", [128, 4 * 32], F32)
        ccs = sbt("ccs", [1, 16], F32)
        cco = sbt("cco", [1, 16], F32)
        rcp = sbt("rcp", [1, 512], F32)
        den = sbt("den", [1, 512], F32)
        r4 = sbt("r4", [128, 4], F32)

        psAB = pst("psAB", [128, 1024], F32)
        psCD = pst("psCD", [128, 1024], F32)
        psC = pst("psC", [128, 512], F32)
        psC2 = pst("psC2", [128, 512], F32)
        psD = psC2
        psE = pst("psE", [128, 1024], BF16)
        psF = pst("psF", [128, 1024], BF16)

        C_EPS = cst[:, 0:1]
        C_RSQ32 = cst[:, 1:2]
        C_ZERO = cst[:, 2:3]
        SCL = lambda i: scl[:, i:i + 1]
        I_SX, I_RX, I_SUP, I_RUP, I_SGT, I_RGT, I_SG1, I_RG1, I_SO, I_RO = range(10)

        arAf = arA[:].bitcast(F32)
        arBf = arB[:].bitcast(F32)
        wfb = wf32[:].bitcast(BF16)
        xsb = xstage[:].bitcast(BF16)

        xslot = lambda i: xstage[:, (i % 4) * 1024:((i % 4) + 1) * 1024]
        wfslot = lambda i: wf32[:, i * 1024:(i + 1) * 1024]
        wbslot = lambda i: wbf[:, i * 1024:(i + 1) * 1024]
        scbh = lambda i: scb[:, i * 1024:(i + 1) * 1024]
        scbs = lambda i: scb[:, i * 2048:(i + 1) * 2048]
        pslot = lambda i: ptile[:, (i % 4) * 512:((i % 4) + 1) * 512]

        for dst, src in ((idf, idf_h), (idb, idb_h), (mask, msk_h), (onef, onef_h), (cst, cst_h)):
            P.op("sync", lambda e, dst=dst, src=src: e.dma_start(out=dst[:], in_=src[:, :]),
                 inc=chL, inck=16)
        P.op("sync", lambda e: e.dma_start(
            out=xstage[0:1, :].rearrange("o (i d) -> o i d", i=4),
            in_=b_h[:, :].unsqueeze(0)), inc=chL, inck=16)
        ld_consts = chL.n

        P.op("gpsimd", lambda e: e.collective_compute(
            "AllReduce", ALU.max, replica_groups=[list(range(8))],
            ins=[cc_in[4].ap()], outs=[cc_out[4].ap()]), inc=chCC)
        P.op("gpsimd", lambda e: e.memset(lsq[:], 0.0), inc=chGP)
        P.op("gpsimd", lambda e: e.memset(lsk[:], 0.0), inc=chGP)
        P.op("gpsimd", lambda e: e.memset(
            vhat[:].rearrange("p (b w) -> p b w", w=130)[:, :, 64:65], 1.0), inc=chGP)
        P.op("gpsimd", lambda e: e.memset(
            vhat[:].rearrange("p (b w) -> p b w", w=130)[:, :, 129:130], 1.0), inc=chGP)
        lsr_memset = chGP.n
        for a in range(8):
            for dst, src in ((lsq, lsrq_h), (lsk, lsrk_h)):
                P.op("gpsimd", lambda e, dst=dst, src=src, a=a: e.dma_start(
                    out=dst[0:64, a * 64:a * 64 + 32], in_=src[2 * a, :, :]),
                    waits=[(chGP, lsr_memset)], inc=chGW, inck=16)
                P.op("gpsimd", lambda e, dst=dst, src=src, a=a: e.dma_start(
                    out=dst[64:128, a * 64 + 32:a * 64 + 64], in_=src[2 * a + 1, :, :]),
                    inc=chGW, inck=16)
        lsr_ready = chGW.n

        for i in range(4):
            for c in range(ND):
                P.op("tensor", lambda e, i=i, c=c: e.matmul(
                    psD[:, (i * 8 + c):(i * 8 + c) + 1],
                    xstage[0:1, i * 1024 + c * 128:i * 1024 + (c + 1) * 128],
                    onef[0:1, 0:1], start=True, stop=True),
                    waits=[(chL, ld_consts)], inc=chPE)
        bias_mm = chPE.n
        P.op("scalar", lambda e: e.activation(bias4[:], psD[:, 0:32], AF.Identity,
                                              bias=C_ZERO, scale=1.0),
             waits=[(chPE, bias_mm), (chL, ld_consts)], inc=chACT)
        bias_ready = chACT.n

        stats_done = {}
        for t in range(NT):
            P.op("sync", lambda e, t=t: e.dma_start(out=xslot(t), in_=x_h[t * 128:(t + 1) * 128, :]),
                 waits=([(chDVE, stats_done[t - 4])] if t >= 4 else [(chACT, bias_ready)]),
                 inc=chL, inck=16)
            ldn = chL.n
            P.op("vector", lambda e, t=t: e.tensor_reduce(
                sS1[:, t:t + 1], xslot(t), AX.XY, ALU.add),
                waits=[(chL, ldn), (chDVE, chDVE.n)], inc=chDVE)
            n_s1 = chDVE.n
            P.op("scalar", lambda e, t=t: e.activation(
                xslot(t), xslot(t), AF.Square, bias=C_ZERO, scale=1.0),
                waits=[(chDVE, n_s1)], inc=chACT)
            n_sq = chACT.n
            P.op("vector", lambda e, t=t: e.tensor_reduce(
                sS2[:, t:t + 1], xslot(t), AX.XY, ALU.add),
                waits=[(chACT, n_sq), (chDVE, chDVE.n)], inc=chDVE)
            stats_done[t] = chDVE.n
        stats_all = chDVE.n

        P.op("vector", lambda e: e.tensor_scalar(sM[:], sS1[:], 1.0 / D, None, ALU.mult),
             waits=[(chDVE, stats_all), (chDVE, chDVE.n)], inc=chDVE)
        P.op("vector", lambda e: e.tensor_scalar(tmp16[:], sS2[:], 1.0 / D, None, ALU.mult), inc=chDVE)
        P.op("vector", lambda e: e.tensor_tensor(tmp16b[:], sM[:], sM[:], ALU.mult),
             waits=[(chDVE, chDVE.n)], inc=chDVE)
        P.op("vector", lambda e: e.tensor_tensor(tmp16[:], tmp16[:], tmp16b[:], ALU.subtract),
             waits=[(chDVE, chDVE.n)], inc=chDVE)
        var_done = chDVE.n
        if DEBUG:
            P.op("sync", lambda e: e.dma_start(out=d_var[:, 0:16], in_=tmp16[:]),
                 waits=[(chDVE, var_done)], inc=chDBG, inck=16)
            P.op("sync", lambda e: e.dma_start(out=d_var[:, 16:32], in_=tmp16b[:]),
                 inc=chDBG, inck=16)
        P.op("scalar", lambda e: e.activation(sR[:], tmp16[:], AF.Sqrt, bias=C_EPS, scale=1.0),
             waits=[(chDVE, var_done), (chDBG, chDBG.n)], inc=chACT)
        sd_done = chACT.n
        if DEBUG:
            P.op("sync", lambda e: e.dma_start(out=d_var[:, 32:48], in_=sR[:]),
                 waits=[(chACT, sd_done)], inc=chDBG, inck=16)
        P.op("vector", lambda e: e.reciprocal(sR[:], sR[:]),
             waits=[(chACT, sd_done), (chDBG, chDBG.n)], inc=chDVE)
        P.op("vector", lambda e: e.tensor_tensor(sNB[:], sM[:], sR[:], ALU.mult),
             waits=[(chDVE, chDVE.n)], inc=chDVE)
        P.op("vector", lambda e: e.tensor_scalar(sNB[:], sNB[:], -1.0, None, ALU.mult),
             waits=[(chDVE, chDVE.n)], inc=chDVE)
        ln_ready = chDVE.n

        trS = {"i": 0, "ev": {}}
        psD_dve = {}
        lsr_ev = {}
        sc_bank_ev = {}
        av_free = {}
        av_evac = {}

        def transpose_tile_to_arena(t, src_ap, dst_ar, dep_waits):
            for g in range(2):
                par = trS["i"] % 2
                pdst = psE if par == 0 else psF
                w = list(dep_waits) + [(chDVE, trS["ev"].get(par, 0)), (chL, ld_consts)]
                for i in range(4):
                    c = g * 4 + i
                    P.op("tensor", lambda e, i=i, c=c, pdst=pdst, src_ap=src_ap: e.transpose(
                        pdst[:, i * 128:(i + 1) * 128], src_ap[:, c * 128:(c + 1) * 128], idb[:]),
                        waits=(w if i == 0 else []), inc=chPE)
                trn = chPE.n
                dv = dst_ar[:].rearrange("p (c w) -> p c w", w=T)[
                    :, g * 4:(g + 1) * 4, t * 128:(t + 1) * 128]
                P.op("vector", lambda e, pdst=pdst, dv=dv: e.tensor_copy(
                    dv, pdst[:, 0:512].rearrange("p (a w) -> p a w", w=128)),
                    waits=[(chPE, trn)], inc=chDVE)
                trS["ev"][par] = chDVE.n
                trS["i"] += 1

        xn_done = {}
        for t in range(NT):
            P.op("sync", lambda e, t=t: e.dma_start(out=xslot(t), in_=x_h[t * 128:(t + 1) * 128, :]),
                 waits=([(chDVE, xn_done[t - 4])] if t >= 4 else [(chDVE, stats_all)]),
                 inc=chL, inck=16)
            ldn = chL.n
            dst = scbh(t % 2)
            P.op("scalar", lambda e, t=t, dst=dst: e.activation(
                dst, xslot(t), AF.Identity, bias=sNB[:, t:t + 1], scale=sR[:, t:t + 1]),
                waits=[(chL, ldn), (chDVE, ln_ready), (chDVE, xn_done.get(t - 2, 0))],
                inc=chACT)
            act_n = chACT.n
            transpose_tile_to_arena(t, dst, arA, [(chACT, act_n)])
            P.op("vector", lambda e, t=t, dst=dst: e.tensor_reduce(
                sMX[:, t:t + 1], dst, AX.XY, ALU.max, apply_absolute_value=True), inc=chDVE)
            xn_done[t] = chDVE.n
        xnT_ready = chDVE.n
        if DEBUG:
            P.op("sync", lambda e: e.dma_start(out=d_xnT[:, :], in_=arA[:]),
                 waits=[(chDVE, xnT_ready)], inc=chDBG, inck=16)
            for i, st_ in enumerate((sM, sR, sNB, sMX, sS1, sS2)):
                P.op("sync", lambda e, i=i, st_=st_: e.dma_start(
                    out=d_stats[:, i * 16:(i + 1) * 16], in_=st_[:]), inc=chDBG, inck=16)

        def part_max_to_ccs(src_ap, col, extra_waits=()):
            P.op("vector", lambda e, src_ap=src_ap: e.tensor_reduce(
                tmp16[:, 0:1], src_ap, AX.X, ALU.max),
                waits=list(extra_waits) + [(chGW, chGW.n), (chDVE, chDVE.n)], inc=chDVE)
            n1 = chDVE.n
            P.op("tensor", lambda e: e.transpose(
                psD[0:1, 0:128], tmp16[:, 0:1], idf[:]),
                waits=[(chDVE, n1), (chACT, lsr_ev.get("any", 0))], inc=chPE)
            n2 = chPE.n
            P.op("vector", lambda e, col=col: e.tensor_reduce(
                ccs[0:1, col:col + 1], psD[0:1, 0:128], AX.X, ALU.max),
                waits=[(chPE, n2)], inc=chDVE)
            psD_dve[0] = chDVE.n
            return chDVE.n

        def collective(idx, dep_ch, dep_val):
            P.op("gpsimd", lambda e, idx=idx: e.dma_start(out=cc_in[idx][:, :], in_=ccs[:, :]),
                 waits=[(dep_ch, dep_val)], inc=chGW, inck=16)
            n = chGW.n
            P.op("gpsimd", lambda e, idx=idx: e.collective_compute(
                "AllReduce", ALU.max, replica_groups=[list(range(8))],
                ins=[cc_in[idx].ap()], outs=[cc_out[idx].ap()]),
                waits=[(chGW, n)], inc=chCC)
            nn = chCC.n
            P.op("gpsimd", lambda e, idx=idx: e.dma_start(out=cco[:, :], in_=cc_out[idx][:, :]),
                 waits=[(chCC, nn)], inc=chGW, inck=16)
            return chGW.n

        def scale_math(col, s_idx, r_idx, dep):
            P.op("vector", lambda e, col=col: e.tensor_scalar(
                ccs[0:1, 8:9], cco[0:1, col:col + 1], 1.0 / QMAX, 1e-12, ALU.mult, ALU.max),
                waits=[(chGW, dep)], inc=chDVE)
            P.op("vector", lambda e: e.reciprocal(ccs[0:1, 9:10], ccs[0:1, 8:9]),
                 waits=[(chDVE, chDVE.n)], inc=chDVE)
            n1 = chDVE.n
            P.op("tensor", lambda e: e.matmul(psD[:, 0:1], onef[0:1, :], ccs[0:1, 8:9],
                                              start=True, stop=True),
                 waits=[(chDVE, n1), (chACT, lsr_ev.get("any", 0))], inc=chPE)
            P.op("tensor", lambda e: e.matmul(psD[:, 1:2], onef[0:1, :], ccs[0:1, 9:10],
                                              start=True, stop=True), inc=chPE)
            n2 = chPE.n
            P.op("vector", lambda e, s_idx=s_idx: e.tensor_copy(SCL(s_idx), psD[:, 0:1]),
                 waits=[(chPE, n2)], inc=chDVE)
            P.op("vector", lambda e, r_idx=r_idx: e.tensor_copy(SCL(r_idx), psD[:, 1:2]), inc=chDVE)
            psD_dve[0] = chDVE.n
            return chDVE.n

        mx_done = part_max_to_ccs(sMX[:], 0, [(chDVE, xnT_ready)])
        c1 = collective(0, chDVE, mx_done)
        sx_ready = scale_math(0, I_SX, I_RX, c1)
        P.op("gpsimd", lambda e: e.memset(ccs[0:1, 0:2], 0.0),
             waits=[(chGW, chGW.n)], inc=chGP)

        P.op("vector", lambda e: e.tensor_scalar(sA[:], sR[:], SCL(I_RX), None, ALU.mult),
             waits=[(chDVE, sx_ready)], inc=chDVE)
        P.op("vector", lambda e: e.tensor_scalar(sBt[:], sNB[:], SCL(I_RX), -1.0, ALU.mult, ALU.mult),
             inc=chDVE)
        xq_scales = chDVE.n

        xq_done = {}
        for t in range(NT):
            P.op("sync", lambda e, t=t: e.dma_start(out=xslot(t), in_=x_h[t * 128:(t + 1) * 128, :]),
                 waits=([(chDVE, xq_done[t - 4])] if t >= 4 else [(chDVE, xnT_ready)]),
                 inc=chL, inck=16)
            ldn = chL.n
            dst = scbh(t % 2)
            ftmp = wfslot(t % 2)
            P.op("vector", lambda e, t=t, ftmp=ftmp: e.tensor_scalar(
                ftmp, xslot(t), sA[:, t:t + 1], sBt[:, t:t + 1], ALU.mult, ALU.subtract),
                waits=[(chL, ldn), (chDVE, xq_scales), (chDVE, xq_done.get(t - 2, 0))],
                inc=chDVE)
            P.op("vector", lambda e, ftmp=ftmp: e.tensor_scalar(
                ftmp, ftmp, MAGIC, MAGIC, ALU.add, ALU.subtract),
                waits=[(chDVE, chDVE.n)], inc=chDVE)
            P.op("vector", lambda e, dst=dst, ftmp=ftmp: e.tensor_scalar(
                dst, ftmp, SCL(I_SX), None, ALU.mult),
                waits=[(chDVE, chDVE.n)], inc=chDVE)
            dve_n = chDVE.n
            transpose_tile_to_arena(t, dst, arB, [(chDVE, dve_n)])
            xq_done[t] = chDVE.n
        xqT_ready = chDVE.n
        if DEBUG:
            P.op("sync", lambda e: e.dma_start(out=d_xqT[:, :], in_=arB[:]),
                 waits=[(chDVE, xqT_ready)], inc=chDBG, inck=16)

        WCFG = {"up": dict(wh=wup_h, wsl=(0, 1), wtsl=(2, 3), nat=0, dq=0),
                "gt": dict(wh=wgt_h, wsl=(2, 3), wtsl=(4, 5), nat=1, dq=1)}
        dq_free = {}
        nat_free = {}
        wt_free = {}
        ev_free = {}
        dqs_free = {}
        bank_ev = {0: 0, 1: 0}

        for c in range(NCF):
            for wi, wn in enumerate(("up", "gt")):
                cfg = WCFG[wn]
                par = c % 2
                fw = wfslot(cfg["wsl"][par])
                nat = wbslot(cfg["nat"])
                wt = wbslot(cfg["wtsl"][par])
                sq = dqs[:, (2 * cfg["dq"]) * 32:(2 * cfg["dq"] + 1) * 32]
                rq = dqs[:, (2 * cfg["dq"] + 1) * 32:(2 * cfg["dq"] + 2) * 32]

                P.op("sync", lambda e, c=c, wh=cfg["wh"], fw=fw: e.dma_start(
                    out=fw, in_=wh[c * 128:(c + 1) * 128, :]),
                    waits=[(chDVE, dq_free.get((wn, c - 2), 0)),
                           (chDVE, xqT_ready)],
                    inc=chL, inck=16)
                ldn = chL.n

                P.op("vector", lambda e, fw=fw, sq=sq: e.tensor_reduce(
                    sq, fw.rearrange("p (b w) -> p b w", w=32), AX.X, ALU.max,
                    apply_absolute_value=True),
                    waits=[(chL, ldn), (chDVE, dqs_free.get(wn, 0))], inc=chDVE)
                P.op("vector", lambda e, sq=sq: e.tensor_scalar(
                    sq, sq, 1.0 / 7.0, 1e-12, ALU.mult, ALU.max),
                    waits=[(chDVE, chDVE.n)], inc=chDVE)
                P.op("vector", lambda e, sq=sq, rq=rq: e.reciprocal(rq, sq),
                    waits=[(chDVE, chDVE.n)], inc=chDVE)
                n_sr = chDVE.n
                P.op("vector", lambda e, fw=fw, rq=rq: e.tensor_tensor(
                    fw.rearrange("p (b w) -> p b w", w=32),
                    fw.rearrange("p (b w) -> p b w", w=32),
                    rq.unsqueeze(2).broadcast_to([128, 32, 32]), ALU.mult),
                    waits=[(chDVE, n_sr)], inc=chDVE)
                P.op("vector", lambda e, fw=fw: e.tensor_scalar(
                    fw, fw, MAGIC, MAGIC, ALU.add, ALU.subtract),
                    waits=[(chDVE, chDVE.n)], inc=chDVE)
                P.op("vector", lambda e, fw=fw, nat=nat, sq=sq: e.tensor_tensor(
                    nat.rearrange("p (b w) -> p b w", w=32),
                    fw.rearrange("p (b w) -> p b w", w=32),
                    sq.unsqueeze(2).broadcast_to([128, 32, 32]), ALU.mult),
                    waits=[(chDVE, chDVE.n), (chPE, nat_free.get(wn, 0))], inc=chDVE)
                dq_free[(wn, c)] = chDVE.n
                dqs_free[wn] = chDVE.n
                n_nat = chDVE.n

                for g in range(2):
                    tpar = trS["i"] % 2
                    pdst = psE if tpar == 0 else psF
                    w = [(chDVE, n_nat), (chDVE, trS["ev"].get(tpar, 0)),
                         (chPE, wt_free.get((wn, par), 0))]
                    for i in range(4):
                        dc = g * 4 + i
                        P.op("tensor", lambda e, i=i, dc=dc, pdst=pdst, nat=nat: e.transpose(
                            pdst[:, i * 128:(i + 1) * 128], nat[:, dc * 128:(dc + 1) * 128],
                            idb[:]), waits=(w if i == 0 else []), inc=chPE)
                    trn = chPE.n
                    if g == 1:
                        nat_free[wn] = trn
                    P.op("vector", lambda e, pdst=pdst, wt=wt, g=g: e.tensor_copy(
                        wt[:, g * 512:(g + 1) * 512], pdst[:, 0:512]),
                        waits=[(chPE, trn)], inc=chDVE)
                    trS["ev"][tpar] = chDVE.n
                    trS["i"] += 1
                n_wt = chDVE.n

                ev = scbs(cfg["nat"])
                for pp in range(2):
                    bankT = psAB if pp == 0 else psCD
                    for half in range(2):
                        p = pp * 2 + half
                        for dc in range(ND):
                            P.op("tensor", lambda e, p=p, dc=dc, bankT=bankT, half=half,
                                 wt=wt: e.matmul(
                                     bankT[:, half * 512:(half + 1) * 512],
                                     wt[:, dc * 128:(dc + 1) * 128],
                                     arB[:, dc * T + p * 512:dc * T + (p + 1) * 512],
                                     start=(dc == 0), stop=(dc == ND - 1)),
                                 waits=([(chDVE, n_wt), (chACT, bank_ev[pp])]
                                        if (dc == 0 and half == 0) else []), inc=chPE)
                    mmn = chPE.n
                    P.op("scalar", lambda e, pp=pp, bankT=bankT, ev=ev: e.activation(
                        ev[:, pp * 1024:(pp + 1) * 1024], bankT[:], AF.Identity,
                        bias=C_ZERO, scale=1.0),
                        waits=[(chPE, mmn), (chSW, ev_free.get(wn, 0))], inc=chACT)
                    bank_ev[pp] = chACT.n
                wt_free[(wn, par)] = chPE.n
                evn = chACT.n

                P.op("vector", lambda e, ev=ev, c=c, wi=wi: e.tensor_reduce(
                    pmx[:, wi * 24 + c:wi * 24 + c + 1], ev, AX.XY, ALU.max,
                    apply_absolute_value=True),
                    waits=[(chACT, evn)], inc=chDVE)
                P.op("sync", lambda e, ev=ev, c=c, wn=wn: e.dma_start(
                    out=scr[wn][c, :, :], in_=ev),
                    waits=[(chACT, evn)], inc=chSW, inck=16)
                ev_free[wn] = chSW.n

        p1_gp_done = chGP.n
        p1_sw_done = chSW.n
        m_up = part_max_to_ccs(pmx[:, 0:24], 0)
        m_gt = part_max_to_ccs(pmx[:, 24:48], 1)
        c2 = collective(1, chDVE, m_gt)
        s_up = scale_math(0, I_SUP, I_RUP, c2)
        s_gt = scale_math(1, I_SGT, I_RGT, s_up)
        P.op("gpsimd", lambda e: e.memset(ccs[0:1, 0:1], 0.0),
             waits=[(chGW, chGW.n)], inc=chGP)

        attw_free = {}
        qlr_free = {}
        vhat_free = {}
        sc_used = {}
        yevac_done = [0]
        ynorm_done = [0]
        wload = {}

        def prefetch_attn(a):
            par = a % 2
            extra = []
            if a == 0:
                extra = [(chPE, nat_free["up"]), (chPE, wt_free[("up", 0)]),
                         (chPE, wt_free[("gt", 0)])]
            elif a == 1:
                extra = [(chPE, nat_free["gt"]), (chPE, wt_free[("up", 1)]),
                         (chPE, wt_free[("gt", 1)])]
            for wi, wh in enumerate((wq_h, wk_h, wv_h)):
                P.op("gpsimd", lambda e, wh=wh, wi=wi, par=par, a=a: e.dma_start(
                    out=wbslot(2 * wi + par).rearrange("p (c o) -> p c o", o=128),
                    in_=wh[a].rearrange("(c p) o -> p c o", p=128)),
                    waits=extra + [(chPE, attw_free.get((wi, par), 0)),
                                   (chGP, p1_gp_done)],
                    inc=chGW, inck=16)
            wload[a] = chGW.n

        prefetch_attn(0)
        prefetch_attn(1)

        p2_infree = {}
        p2_outfree = {0: p1_sw_done, 1: p1_sw_done}
        p2_sig_done = {}
        p3_infree = {}
        p3_outfree = {}
        s_g1 = [0]
        s_o = [0]
        p3_seed = [0]
        p2_gp = [0]
        p3_gp = [0]

        def emit_p2_chunk(c):
            par = c % 2
            inb = wfb[:, par * 2048:(par + 1) * 2048]
            gate0 = wf32[:, 2 * 1024:4 * 1024]
            sig = wf32[:, 4 * 1024:6 * 1024]
            outb = scbs(par)
            P.op("sync", lambda e, c=c, inb=inb: e.dma_start(out=inb, in_=scr["gt"][c, :, :]),
                 waits=[(chSW, p1_sw_done), (chDVE, p2_infree.get(par, 0)),
                        (chGP, p1_gp_done)],
                 inc=chSR, inck=16)
            ldn = chSR.n
            P.op("vector", lambda e, inb=inb, gate0=gate0: e.tensor_scalar(
                gate0, inb, SCL(I_RGT), MAGIC, ALU.mult, ALU.add),
                waits=[(chSR, ldn), (chDVE, s_gt), (chACT, p2_sig_done.get(c - 1, 0)),
                       (chGP, p2_gp[0])],
                inc=chDVE)
            P.op("vector", lambda e, gate0=gate0: e.tensor_scalar(
                gate0, gate0, MAGIC, SCL(I_SGT), ALU.subtract, ALU.mult),
                waits=[(chDVE, chDVE.n)], inc=chDVE)
            p2_infree[par] = chDVE.n
            n_g0 = chDVE.n
            P.op("scalar", lambda e, gate0=gate0, sig=sig: e.activation(
                sig, gate0, AF.Tanh, bias=C_ZERO, scale=0.5),
                waits=[(chDVE, n_g0)], inc=chACT)
            p2_sig_done[c] = chACT.n
            P.op("vector", lambda e, sig=sig: e.tensor_scalar(
                sig, sig, 32768.0, MAGIC + 32768.0, ALU.mult, ALU.add),
                waits=[(chACT, p2_sig_done[c])], inc=chDVE)
            P.op("vector", lambda e, sig=sig: e.tensor_scalar(
                sig, sig, MAGIC, 65535.0, ALU.subtract, ALU.min),
                waits=[(chDVE, chDVE.n)], inc=chDVE)
            P.op("vector", lambda e, sig=sig: e.tensor_scalar(
                sig, sig, 1.0 / 65536.0, None, ALU.mult),
                waits=[(chDVE, chDVE.n)], inc=chDVE)
            P.op("gpsimd", lambda e, gate0=gate0, sig=sig, outb=outb: e.tensor_tensor(
                outb, gate0, sig, ALU.mult),
                waits=[(chSW, p2_outfree.get(par, 0)), (chDVE, chDVE.n)], inc=chGP)
            P.op("vector", lambda e, outb=outb, c=c: e.tensor_reduce(
                pmx[:, 48 + c:48 + c + 1], outb, AX.XY, ALU.max,
                apply_absolute_value=True),
                waits=[(chGP, chGP.n)], inc=chDVE)
            n_out = chGP.n
            P.op("sync", lambda e, outb=outb, c=c: e.dma_start(
                out=scr["g1"][c, :, :], in_=outb),
                waits=[(chGP, n_out)], inc=chSW, inck=16)
            p2_outfree[par] = chSW.n
            p2_gp[0] = chGP.n

        def emit_p3_chunk(c, p2_dve_done):
            par = c % 2
            ing = wfb[:, par * 2048:(par + 1) * 2048]
            inu = wfb[:, 4096 + par * 2048:4096 + (par + 1) * 2048]
            g1 = wf32[:, 4 * 1024:6 * 1024]
            up = wf32[:, 6 * 1024:8 * 1024]
            outb = scbs(par)
            P.op("sync", lambda e, c=c, ing=ing: e.dma_start(out=ing, in_=scr["g1"][c, :, :]),
                 waits=[(chSW, chSW.n), (chDVE, p3_infree.get(par, p2_dve_done))],
                 inc=chSR, inck=16)
            P.op("sync", lambda e, c=c, inu=inu: e.dma_start(out=inu, in_=scr["up"][c, :, :]),
                 inc=chSR, inck=16)
            ldn = chSR.n
            P.op("vector", lambda e, ing=ing, g1=g1: e.tensor_scalar(
                g1, ing, SCL(I_RG1), MAGIC, ALU.mult, ALU.add),
                waits=[(chSR, ldn), (chDVE, s_g1[0]), (chGP, p3_gp[0])], inc=chDVE)
            P.op("vector", lambda e, g1=g1: e.tensor_scalar(
                g1, g1, MAGIC, SCL(I_SG1), ALU.subtract, ALU.mult),
                waits=[(chDVE, chDVE.n)], inc=chDVE)
            P.op("vector", lambda e, inu=inu, up=up: e.tensor_scalar(
                up, inu, SCL(I_RUP), MAGIC, ALU.mult, ALU.add), inc=chDVE)
            P.op("vector", lambda e, up=up: e.tensor_scalar(
                up, up, MAGIC, SCL(I_SUP), ALU.subtract, ALU.mult),
                waits=[(chDVE, chDVE.n)], inc=chDVE)
            p3_infree[par] = chDVE.n
            P.op("gpsimd", lambda e, g1=g1, up=up, outb=outb: e.tensor_tensor(
                outb, g1, up, ALU.mult),
                waits=[(chSW, p3_outfree.get(par, 0)), (chDVE, chDVE.n)], inc=chGP)
            P.op("vector", lambda e, outb=outb, c=c: e.tensor_reduce(
                pmx[:, 72 + c:72 + c + 1], outb, AX.XY, ALU.max,
                apply_absolute_value=True),
                waits=[(chGP, chGP.n)], inc=chDVE)
            n_out = chGP.n
            P.op("sync", lambda e, outb=outb, c=c: e.dma_start(
                out=scr["o"][c, :, :], in_=outb),
                waits=[(chGP, n_out)], inc=chSW, inck=16)
            p3_outfree[par] = chSW.n
            p3_gp[0] = chGP.n

        den_free = [0]
        bg_work = []
        bg_work.extend((lambda c=c: emit_p2_chunk(c)) for c in range(NCF))

        def pump_bg(k=1):
            for _ in range(k):
                if bg_work:
                    bg_work.pop(0)()
        for a in range(8):
            par = a % 2
            vh_ready = 0
            for wi, wn in enumerate(("q", "k", "v")):
                wt = wbslot(2 * wi + par)
                for pp in range(2):
                    bankT = psAB if pp == 0 else psCD
                    for half in range(2):
                        p = pp * 2 + half
                        for dc in range(ND):
                            P.op("tensor", lambda e, p=p, dc=dc, bankT=bankT, half=half,
                                 wt=wt: e.matmul(
                                     bankT[:, half * 512:(half + 1) * 512],
                                     wt[:, dc * 128:(dc + 1) * 128],
                                     arA[:, dc * T + p * 512:dc * T + (p + 1) * 512],
                                     start=(dc == 0), stop=(dc == ND - 1)),
                                 waits=([(chGW, wload[a]), (chACT, bank_ev[pp])]
                                        if (dc == 0 and half == 0) else []), inc=chPE)
                    mmn = chPE.n
                    P.op("scalar", lambda e, pp=pp, bankT=bankT, wi=wi, a=a: e.activation(
                        qkscr[:, pp * 1024:(pp + 1) * 1024], bankT[:], AF.Identity,
                        bias=bias4[:, wi * 8 + a:wi * 8 + a + 1], scale=1.0),
                        waits=[(chPE, mmn)], inc=chACT)
                    bank_ev[pp] = chACT.n
                attw_free[(wi, par)] = chPE.n
                evn = chACT.n

                if wn in ("q", "k"):
                    lsr_w = lsq if wn == "q" else lsk
                    dstb = qlrb if wn == "q" else klrb
                    for p in range(4):
                        P.op("tensor", lambda e, p=p, a=a, lsr_w=lsr_w: e.matmul(
                            psD[0:64, :], lsr_w[:, a * 64:(a + 1) * 64],
                            qkscr[:, p * 512:(p + 1) * 512], start=True, stop=True),
                            waits=[(chACT, evn), (chGW, lsr_ready),
                                   (chACT, lsr_ev.get("any", 0)),
                                   (chDVE, psD_dve.get(0, 0)),
                                   (chDVE, av_free.get(1, 0)),
                                   (chACT, av_evac.get(1, 0)),
                                   (chPE, qlr_free.get(0, 0))], inc=chPE)
                        mmn = chPE.n
                        P.op("scalar", lambda e, p=p, dstb=dstb: e.activation(
                            dstb[:, p * 512:(p + 1) * 512],
                            psD[0:64, :], AF.Identity, bias=C_ZERO[0:64, :], scale=1.0),
                            waits=[(chPE, mmn)], inc=chACT)
                        lsr_ev["any"] = chACT.n
                else:
                    for g in range(4):
                        tpar = trS["i"] % 2
                        pdst = psE if tpar == 0 else psF
                        w = [(chACT, evn), (chDVE, trS["ev"].get(tpar, 0)),
                             (chPE, vhat_free.get(0, 0))]
                        for i in range(4):
                            tt = g * 4 + i
                            P.op("tensor", lambda e, i=i, tt=tt, pdst=pdst: e.transpose(
                                pdst[:, i * 128:(i + 1) * 128],
                                qkscr[:, tt * 128:(tt + 1) * 128], idb[:]),
                                waits=(w if i == 0 else []), inc=chPE)
                        trn = chPE.n
                        for half in range(2):
                            P.op("vector", lambda e, pdst=pdst, g=g, half=half:
                                 e.tensor_copy(
                                     vhat[:].rearrange("p (b w) -> p b w", w=130)
                                     [:, g * 4:(g + 1) * 4, half * 65:half * 65 + 64],
                                     pdst[:, 0:512].rearrange("p (a w) -> p a w", w=128)
                                     [:, :, half * 64:(half + 1) * 64]),
                                 waits=[(chPE, trn), (chGP, lsr_memset)], inc=chDVE)
                        trS["ev"][tpar] = chDVE.n
                        trS["i"] += 1
                    vh_ready = chDVE.n

            if a + 2 < 8:
                prefetch_attn(a + 2)
            qln = chACT.n
            for hh in range(2):
                qsl = qlrb[hh * 32:(hh + 1) * 32, :]
                ksl = klrb[hh * 32:(hh + 1) * 32, :]
                for ic in range(4):
                    njt = 4 * ic + 4
                    npair = njt // 2
                    acc_i = (hh * 4 + ic) % 2
                    accT = psC if acc_i == 0 else psC2
                    acc = accT[0:65, :]
                    mwaits = {}
                    for step in range(npair + 1):
                        if step < npair:
                            pk = step
                            bankT = psAB if pk % 2 == 0 else psCD
                            for half in range(2):
                                jt = pk * 2 + half
                                P.op("tensor", lambda e, jt=jt, ic=ic, bankT=bankT,
                                     half=half, qsl=qsl, ksl=ksl: e.matmul(
                                         bankT[:, half * 512:(half + 1) * 512],
                                         ksl[:, jt * 128:(jt + 1) * 128],
                                         qsl[:, ic * 512:(ic + 1) * 512],
                                         start=True, stop=True),
                                     waits=([(chACT, qln), (chACT, bank_ev[pk % 2])]
                                            if half == 0 else []), inc=chPE)
                            mmn = chPE.n
                            ps2 = ptile[:, ((2 * pk) % 4) * 512:((2 * pk) % 4) * 512 + 1024]
                            P.op("scalar", lambda e, bankT=bankT, ps2=ps2: e.activation(
                                ps2, bankT[:], AF.Exp, bias=C_ZERO, scale=C_RSQ32),
                                waits=[(chPE, mmn), (chPE, sc_used.get(pk % 2, 0))],
                                inc=chACT)
                            bank_ev[pk % 2] = chACT.n
                            expn = chACT.n
                            mwaits[2 * pk] = [(chACT, expn)]
                            mwaits[2 * pk + 1] = [(chACT, expn)]
                            for half in range(2):
                                jt = pk * 2 + half
                                if jt >= 4 * ic:
                                    kk = jt - 4 * ic
                                    P.op("gpsimd", lambda e, jt=jt, kk=kk: e.tensor_tensor(
                                        pslot(jt), pslot(jt),
                                        mask[:, 512 - 128 * kk:1024 - 128 * kk], ALU.mult),
                                        waits=[(chACT, expn)], inc=chGP)
                                    mwaits[jt] = [(chGP, chGP.n)]
                        if step >= 1:
                            for half in range(2):
                                jt = (step - 1) * 2 + half
                                P.op("tensor", lambda e, jt=jt, hh=hh, njt=njt, acc=acc:
                                     e.matmul(
                                         acc,
                                         vhat[:, jt * 130 + hh * 65:jt * 130 + hh * 65 + 65],
                                         pslot(jt), start=(jt == 0), stop=(jt == njt - 1)),
                                     waits=mwaits[jt] + ([(chDVE, vh_ready),
                                                          (chDVE, av_free.get(acc_i, 0)),
                                                          (chACT, av_evac.get(acc_i, 0))]
                                                         + ([(chACT, lsr_ev.get("any", 0)),
                                                             (chDVE, psD_dve.get(0, 0))]
                                                            if acc_i == 1 else [])
                                                         if jt == 0 else []),
                                     inc=chPE)
                            sc_used[(step - 1) % 2] = chPE.n
                    avn = chPE.n
                    P.op("scalar", lambda e, accT=accT: e.activation(
                        den[0:1, :], accT[64:65, :], AF.Identity,
                        bias=C_ZERO[0:1, :], scale=1.0),
                        waits=[(chPE, avn), (chPE, den_free[0])], inc=chACT)
                    P.op("scalar", lambda e, accT=accT: e.activation(
                        ybf[:], accT[0:64, :], AF.Identity,
                        bias=C_ZERO[0:64, :], scale=1.0),
                        waits=[(chDVE, ynorm_done[0])], inc=chACT)
                    ev2 = chACT.n
                    av_evac[acc_i] = ev2
                    for c in range(4):
                        P.op("tensor", lambda e, c=c, accT=accT: e.transpose(
                            accT[:, c:c + 1], den[0:1, c * 128:(c + 1) * 128],
                            idf[0:1, 0:1]),
                            waits=([(chACT, ev2)] if c == 0 else []), inc=chPE)
                    den_free[0] = chPE.n
                    trin = chPE.n
                    P.op("vector", lambda e, accT=accT: e.reciprocal(r4[:], accT[:, 0:4]),
                         waits=[(chPE, trin)], inc=chDVE)
                    rn = chDVE.n
                    for c in range(4):
                        P.op("tensor", lambda e, c=c, accT=accT: e.transpose(
                            accT[0:1, c * 128:(c + 1) * 128], r4[:, c:c + 1], idf[:]),
                            waits=([(chDVE, rn)] if c == 0 else []), inc=chPE)
                    trout = chPE.n
                    P.op("vector", lambda e, accT=accT: e.tensor_copy(
                        rcp[0:1, :], accT[0:1, 0:512]),
                        waits=[(chPE, trout)], inc=chDVE)
                    rc = chDVE.n
                    P.op("tensor", lambda e, accT=accT: e.matmul(
                        accT[0:64, :], onef[0:1, 0:64], rcp[0:1, :],
                        start=True, stop=True),
                        waits=[(chDVE, rc)], inc=chPE)
                    bn = chPE.n
                    P.op("vector", lambda e, a=a, hh=hh, ic=ic, accT=accT: e.tensor_tensor(
                        arC[hh * 64:(hh + 1) * 64, a * T + ic * 512:a * T + (ic + 1) * 512],
                        ybf[:], accT[0:64, :], ALU.mult),
                        waits=[(chPE, bn), (chACT, ev2)], inc=chDVE)
                    ynorm_done[0] = chDVE.n
                    av_free[acc_i] = chDVE.n
                    pump_bg(1)
            qlr_free[0] = chPE.n
            vhat_free[0] = chPE.n

            if a == 4:
                m_g1 = part_max_to_ccs(pmx[:, 48:72], 0)
                c3 = collective(2, chDVE, m_g1)
                s_g1[0] = scale_math(0, I_SG1, I_RG1, c3)
                P.op("gpsimd", lambda e: e.memset(ccs[0:1, 0:1], 0.0),
                     waits=[(chGW, chGW.n)], inc=chGP)
                p3_seed[0] = chDVE.n
                bg_work.extend(
                    (lambda c=c: emit_p3_chunk(c, p3_seed[0])) for c in range(NCF))
            if a == 7:
                m_o = part_max_to_ccs(pmx[:, 72:96], 0)
                c4 = collective(3, chDVE, m_o)
                s_o[0] = scale_math(0, I_SO, I_RO, c4)

        pump_bg(len(bg_work))
        qkv_all_done = chPE.n

        if DEBUG:
            P.op("sync", lambda e: e.dma_start(out=d_yall[:, :], in_=arC[:]),
                 waits=[(chDVE, chDVE.n)], inc=chDBG, inck=16)
            P.op("sync", lambda e: e.dma_start(out=d_scl[:, :], in_=scl[:]),
                 waits=[(chDVE, chDVE.n)], inc=chDBG, inck=16)
        P.op("sync", lambda e: e.dma_start(
            out=arAf.rearrange("p (t w) -> p t w", w=1024),
            in_=x_h[0:1024, :].rearrange("(t p) d -> p t d", p=128)),
            waits=[(chPE, qkv_all_done), (chDBG, chDBG.n)], inc=chACC, inck=16)
        P.op("sync", lambda e: e.dma_start(
            out=arBf.rearrange("p (t w) -> p t w", w=1024),
            in_=x_h[1024:2048, :].rearrange("(t p) d -> p t d", p=128)),
            inc=chACC, inck=16)
        acc_ld = chACC.n

        op_wfree = {}
        trf_ev = {0: chDVE.n, 1: chDVE.n}
        trf_add = [0]

        def evac_pair_add(bankT, bias_ap, pair, oc, mm_snap):
            P.op("scalar", lambda e, bankT=bankT, bias_ap=bias_ap: e.activation(
                xstage[:, 3 * 1024:4 * 1024], bankT[:], AF.Identity,
                bias=bias_ap, scale=1.0),
                waits=[(chPE, mm_snap), (chDVE, trf_add[0])], inc=chACT)
            evn = chACT.n
            arf = arAf if pair == 0 else arBf
            for g in range(2):
                tpar = trS["i"] % 2
                pdst = (psE if tpar == 0 else psF)[:].bitcast(F32)
                for i in range(4):
                    P.op("tensor", lambda e, g=g, i=i, pdst=pdst: e.transpose(
                        pdst[:, i * 128:(i + 1) * 128],
                        xstage[:, 3 * 1024 + (g * 4 + i) * 128:
                               3 * 1024 + (g * 4 + i + 1) * 128], idf[:]),
                        waits=([(chACT, evn), (chDVE, trS["ev"].get(tpar, 0))]
                               if i == 0 else []), inc=chPE)
                trn = chPE.n
                accv = arf.rearrange("p (t w) -> p t w", w=1024)[
                    :, g * 4:(g + 1) * 4, oc * 128:(oc + 1) * 128]
                P.op("vector", lambda e, accv=accv, pdst=pdst: e.tensor_tensor(
                    accv, accv, pdst[:, 0:512].rearrange("p (a w) -> p a w", w=128),
                    ALU.add),
                    waits=[(chPE, trn), (chACC, acc_ld)], inc=chDVE)
                trS["ev"][tpar] = chDVE.n
                trf_add[0] = chDVE.n
                trS["i"] += 1

        for oc in range(ND):
            par = oc % 2
            P.op("gpsimd", lambda e, oc=oc, par=par: e.dma_start(
                out=wbslot(par).rearrange("p (c o) -> p c o", o=128),
                in_=wo_h[oc].rearrange("(c p) o -> p c o", p=128)),
                waits=[(chPE, attw_free.get((0, par), 0)),
                       (chPE, op_wfree.get(par, 0))], inc=chGW, inck=16)
            ldn = chGW.n
            for pp in range(2):
                bankT = psAB if pp == 0 else psCD
                for half in range(2):
                    p = pp * 2 + half
                    for dc in range(ND):
                        P.op("tensor", lambda e, p=p, dc=dc, bankT=bankT, half=half,
                             par=par: e.matmul(
                                 bankT[:, half * 512:(half + 1) * 512],
                                 wbslot(par)[:, dc * 128:(dc + 1) * 128],
                                 arC[:, dc * T + p * 512:dc * T + (p + 1) * 512],
                                 start=(dc == 0), stop=(dc == ND - 1)),
                             waits=([(chGW, ldn), (chACT, bank_ev[pp]),
                                     (chDVE, ynorm_done[0])]
                                    if (dc == 0 and half == 0) else []), inc=chPE)
                mmn = chPE.n
                evac_pair_add(bankT, bias4[:, 24 + oc:24 + oc + 1], pp, oc, mmn)
                bank_ev[pp] = chACT.n
            op_wfree[par] = chPE.n
        oproj_pe_done = chPE.n
        oproj_done = chDVE.n

        def oT_view(ic):
            if ic < 16:
                return arC[:, ic * 1024:(ic + 1) * 1024]
            if ic < 18:
                return vhat[:, (ic - 16) * 1024:(ic - 15) * 1024]
            if ic < 20:
                return qkscr[:, (ic - 18) * 1024:(ic - 17) * 1024]
            return scbh(2 + (ic - 20))

        p3_dve_done = chDVE.n
        dn_nat_free = {}
        dn_dqs_free = [0]
        dn_wt_used = {}
        oT_used = {}
        sr_free = {}
        p4_ld_free = [0]
        dn_cached = {}

        for H in range(2):
            for ic in range(NCF):
                par = ic % 2
                inb = scbh(par)
                P.op("sync", lambda e, ic=ic, H=H, inb=inb: e.dma_start(
                    out=inb, in_=scr["o"][ic, :, H * 1024:(H + 1) * 1024]),
                    waits=[(chSW, chSW.n), (chDVE, sr_free.get(par, 0)),
                           (chPE, oproj_pe_done)],
                    inc=chSR, inck=16)
                ldn = chSR.n
                ft = wfslot(7)
                P.op("vector", lambda e, inb=inb, ft=ft: e.tensor_scalar(
                    ft, inb, SCL(I_RO), MAGIC, ALU.mult, ALU.add),
                    waits=[(chSR, ldn), (chDVE, s_o[0]), (chPE, oproj_pe_done),
                           (chPE, oT_used.get(ic, 0)), (chDBG, chDBG.n)], inc=chDVE)
                P.op("vector", lambda e, ic=ic, ft=ft: e.tensor_scalar(
                    oT_view(ic), ft, MAGIC, SCL(I_SO), ALU.subtract, ALU.mult),
                    waits=[(chSW, chSW.n), (chDVE, chDVE.n)], inc=chDVE)
                sr_free[par] = chDVE.n
            oT_ready = chDVE.n

            for oc in range(ND):
                par = oc % 2
                wdnT = wbf[:, par * 3072:(par + 1) * 3072]
                if oc not in dn_cached:
                    P.op("sync", lambda e, oc=oc: e.dma_start(
                        out=wf32[:, 0:3 * 1024].rearrange("p (a w) -> p a w", w=1024),
                        in_=wdn_h[oc * 128:(oc + 1) * 128, :].rearrange(
                            "p (a w) -> p a w", w=1024)),
                        waits=[(chDVE, p3_dve_done), (chDVE, p4_ld_free[0]),
                               (chPE, dn_wt_used.get(par, 0))],
                        inc=chL, inck=16)
                    ldn = chL.n
                    sq = dqs[:, 0:32]
                    rq = dqs[:, 32:64]
                    for pc in range(3):
                        fw = wfslot(pc)
                        nat = xsb[:, (pc % 2) * 1024:((pc % 2) + 1) * 1024]
                        P.op("vector", lambda e, fw=fw, sq=sq: e.tensor_reduce(
                            sq, fw.rearrange("p (b w) -> p b w", w=32), AX.X, ALU.max,
                            apply_absolute_value=True),
                            waits=[(chL, ldn), (chDVE, dn_dqs_free[0]),
                                   (chACT, chACT.n if (H == 0 and oc == 0 and pc == 0) else 0)],
                            inc=chDVE)
                        P.op("vector", lambda e, sq=sq: e.tensor_scalar(
                            sq, sq, 1.0 / 7.0, 1e-12, ALU.mult, ALU.max),
                            waits=[(chDVE, chDVE.n)], inc=chDVE)
                        P.op("vector", lambda e, sq=sq, rq=rq: e.reciprocal(rq, sq),
                            waits=[(chDVE, chDVE.n)], inc=chDVE)
                        n_sr2 = chDVE.n
                        P.op("vector", lambda e, fw=fw, rq=rq: e.tensor_tensor(
                            fw.rearrange("p (b w) -> p b w", w=32),
                            fw.rearrange("p (b w) -> p b w", w=32),
                            rq.unsqueeze(2).broadcast_to([128, 32, 32]), ALU.mult),
                            waits=[(chDVE, n_sr2)], inc=chDVE)
                        P.op("vector", lambda e, fw=fw: e.tensor_scalar(
                            fw, fw, MAGIC, MAGIC, ALU.add, ALU.subtract),
                            waits=[(chDVE, chDVE.n)], inc=chDVE)
                        P.op("vector", lambda e, fw=fw, nat=nat, sq=sq: e.tensor_tensor(
                            nat.rearrange("p (b w) -> p b w", w=32),
                            fw.rearrange("p (b w) -> p b w", w=32),
                            sq.unsqueeze(2).broadcast_to([128, 32, 32]), ALU.mult),
                            waits=[(chDVE, chDVE.n), (chPE, dn_nat_free.get(pc % 2, 0))],
                            inc=chDVE)
                        dn_dqs_free[0] = chDVE.n
                        n_nat2 = chDVE.n
                        if pc == 2:
                            p4_ld_free[0] = chDVE.n
                        for g in range(2):
                            tpar = trS["i"] % 2
                            pdst = psE if tpar == 0 else psF
                            w = [(chDVE, n_nat2), (chDVE, trS["ev"].get(tpar, 0)),
                                 (chPE, dn_wt_used.get(par, 0))]
                            for i in range(4):
                                dc8 = g * 4 + i
                                P.op("tensor", lambda e, i=i, dc8=dc8, pdst=pdst, nat=nat:
                                     e.transpose(pdst[:, i * 128:(i + 1) * 128],
                                                 nat[:, dc8 * 128:(dc8 + 1) * 128], idb[:]),
                                     waits=(w if i == 0 else []), inc=chPE)
                            trn = chPE.n
                            dn_nat_free[pc % 2] = trn
                            P.op("vector", lambda e, pdst=pdst, wdnT=wdnT, pc=pc, g=g:
                                 e.tensor_copy(
                                     wdnT[:, pc * 1024 + g * 512:pc * 1024 + (g + 1) * 512],
                                     pdst[:, 0:512]),
                                 waits=[(chPE, trn)], inc=chDVE)
                            trS["ev"][tpar] = chDVE.n
                            trS["i"] += 1
                    wt_ready_dve = chDVE.n
                    P.op("sync", lambda e, oc=oc, wdnT=wdnT: e.dma_start(
                        out=dnT_h[oc, :, :], in_=wdnT),
                        waits=[(chDVE, wt_ready_dve)], inc=chL, inck=16)
                    dn_cached[oc] = (wt_ready_dve, chL.n)
                    wt_dep = [(chDVE, wt_ready_dve)]
                else:
                    P.op("sync", lambda e, oc=oc, wdnT=wdnT: e.dma_start(
                        out=wdnT, in_=dnT_h[oc, :, :]),
                        waits=[(chL, dn_cached[oc][1]), (chPE, dn_wt_used.get(par, 0))],
                        inc=chL, inck=16)
                    wt_dep = [(chL, chL.n)]

                pp = oc % 2
                bankT = psAB if pp == 0 else psCD
                for half in range(2):
                    for ic in range(NCF):
                        P.op("tensor", lambda e, half=half, ic=ic, bankT=bankT, wdnT=wdnT:
                             e.matmul(bankT[:, half * 512:(half + 1) * 512],
                                      wdnT[:, ic * 128:(ic + 1) * 128],
                                      oT_view(ic)[:, half * 512:(half + 1) * 512],
                                      start=(ic == 0), stop=(ic == NCF - 1)),
                             waits=(wt_dep + [(chDVE, oT_ready), (chACT, bank_ev[pp])]
                                    if (ic == 0 and half == 0) else []), inc=chPE)
                mmn = chPE.n
                dn_wt_used[par] = mmn
                for ic2 in range(NCF):
                    oT_used[ic2] = mmn
                evac_pair_add(bankT, C_ZERO, H, oc, mmn)
                bank_ev[pp] = chACT.n

        ffn_done = chDVE.n

        P.op("sync", lambda e: e.dma_start(
            out=out_h[0:1024, :].rearrange("(t p) d -> p t d", p=128),
            in_=arAf.rearrange("p (t w) -> p t w", w=1024)),
            waits=[(chDVE, ffn_done)], inc=chOUT, inck=16)
        P.op("sync", lambda e: e.dma_start(
            out=out_h[1024:2048, :].rearrange("(t p) d -> p t d", p=128),
            in_=arBf.rearrange("p (t w) -> p t w", w=1024)),
            inc=chOUT, inck=16)
        P.op("sync", lambda e: e.nop(), waits=[(chOUT, chOUT.n), (chDBG, chDBG.n)])

        for c in P.chans:
            c.sem = ctx.enter_context(nc.semaphore(c.name))
        block = ctx.enter_context(nc.Block())
        P.emit(block)

    return nc


def kernel(L0_attn_x, L0_attn_Wq, L0_attn_bq, L0_attn_Wk, L0_attn_bk,
           L0_attn_Wv, L0_attn_bv, L0_attn_Wo, L0_attn_bo,
           L0_attn_Wq_lsr, L0_attn_Wk_lsr,
           L0_ffn_w_gate, L0_ffn_w_up, L0_ffn_w_down):
    f32 = np.float32
    x = np.ascontiguousarray(np.asarray(L0_attn_x, dtype=f32))

    def wt_chunks(W):
        WT = np.ascontiguousarray(np.asarray(W, dtype=f32).T)
        return np.ascontiguousarray(WT.reshape(D, ND, 128).transpose(1, 0, 2))

    shared = {
        "wq_t": wt_chunks(L0_attn_Wq),
        "wk_t": wt_chunks(L0_attn_Wk),
        "wv_t": wt_chunks(L0_attn_Wv),
        "wo_t": wt_chunks(L0_attn_Wo),
        "lsr_q": np.ascontiguousarray(np.asarray(L0_attn_Wq_lsr, dtype=f32)),
        "lsr_k": np.ascontiguousarray(np.asarray(L0_attn_Wk_lsr, dtype=f32)),
        "b_qkvo": np.ascontiguousarray(
            np.stack([np.asarray(b, dtype=f32) for b in
                      (L0_attn_bq, L0_attn_bk, L0_attn_bv, L0_attn_bo)])),
        "w_up": np.ascontiguousarray(np.asarray(L0_ffn_w_up, dtype=f32)),
        "w_gate": np.ascontiguousarray(np.asarray(L0_ffn_w_gate, dtype=f32)),
        "w_down": np.ascontiguousarray(np.asarray(L0_ffn_w_down, dtype=f32)),
        "idn_f": np.eye(128, dtype=f32),
        "idn_b": np.eye(128, dtype=ml_dtypes.bfloat16),
        "maskbig": (np.arange(1024)[None, :] >= (np.arange(128)[:, None] + 512)
                    ).astype(ml_dtypes.bfloat16),
        "ones_f": np.ones((1, 128), dtype=f32),
        "consts": np.broadcast_to(
            np.array([EPS, 1.0 / np.sqrt(32.0), 0.0, 1.0], dtype=f32), (128, 4)).copy(),
    }

    nc = build_nc()
    in_maps = [dict(shared, x=np.ascontiguousarray(x[b])) for b in range(8)]
    res = run_bass_kernel_spmd(nc, in_maps, core_ids=list(range(8)))
    out = np.stack([np.asarray(res.results[b]["out"]) for b in range(8)])
    return out.astype(np.float32)
